# version 02
# baseline (speedup 1.0000x reference)
import numpy as np
import ml_dtypes
import concourse.bass as bass
import concourse.mybir as mybir
from concourse.bass_utils import run_bass_kernel_spmd
from concourse.alu_op_type import AluOpType
from contextlib import ExitStack

F32 = mybir.dt.float32
BF16 = mybir.dt.bfloat16
I32 = mybir.dt.int32
I8 = mybir.dt.int8
AF = mybir.ActivationFunctionType
AX = mybir.AxisListType
bfdt = ml_dtypes.bfloat16

T = 2048
D = 2048
NC = 16
NT4 = 4
NT16 = 16
HD = 128
NH = 16
KVH0 = 4
DFF1 = 8192
NFC = DFF1 // 128
NI = 1408
NIC = 11
NEXP = 8
NEG = -1e30


class Emit:

    def __init__(self):
        self.steps = []

    def _add(self, eng, kind, fn, inc):
        self.steps.append((eng, kind, fn, inc))

    def dma(self, dst, src):
        self._add("sync", "dma", lambda e: e.dma_start(dst, src), 16)

    def dmag(self, dst, src):
        self._add("gpsimd", "dma", lambda e: e.dma_start(dst, src), 16)

    def dmat(self, dst, src):
        self._add("sync", "dma", lambda e: e.dma_start(dst, src, transpose=True), 16)

    def pe(self, fn):
        self._add("tensor", "c", fn, 1)

    def dve(self, fn):
        self._add("vector", "c", fn, 1)

    def act(self, fn):
        self._add("scalar", "c", fn, 1)

    def gp(self, fn):
        self._add("gpsimd", "c", fn, 1)

    def finalize(self, nc, block, sem):
        steps = self.steps
        n = len(steps)
        cum = []
        c = 0
        for eng, kind, fn, inc in steps:
            cum.append(c)
            c += inc
        total = c
        waits = [0] * n
        for i in range(n):
            if i == 0:
                continue
            ej, kj, _, incj = steps[i - 1]
            if steps[i][0] == "tensor" and ej == "tensor" and kj == "c":
                waits[i] = waits[i - 1]
            else:
                waits[i] = cum[i]

        def emit(eng_obj, name):
            last_wait = 0
            for (eng, kind, fn, inc), w, cm in zip(steps, waits, cum):
                if eng != name:
                    continue
                if w > last_wait:
                    eng_obj.wait_ge(sem, w)
                    last_wait = w
                fn(eng_obj).then_inc(sem, inc)

        @block.sync
        def _(sync):
            emit(sync, "sync")
            sync.wait_ge(sem, total)

        @block.vector
        def _(vector):
            emit(vector, "vector")

        @block.scalar
        def _(scalar):
            emit(scalar, "scalar")

        @block.tensor
        def _(tensor):
            emit(tensor, "tensor")

        @block.gpsimd
        def _(gpsimd):
            emit(gpsimd, "gpsimd")



def _chunk_cols(vec):
    n = vec.shape[0] // 128
    return np.ascontiguousarray(vec.reshape(n, 128).T.astype(np.float32))

def _rope_tables():
    inv_freq = 1.0 / (10000.0 ** (np.arange(0, HD, 2, dtype=np.float64) / HD))
    t = np.arange(T, dtype=np.float64)
    fr = np.outer(t, inv_freq)
    cos = np.cos(fr).T
    sin = np.sin(fr).T
    cos2 = np.vstack([cos, cos]).astype(bfdt)
    sin2 = np.vstack([sin, -sin]).astype(bfdt)
    return np.ascontiguousarray(cos2), np.ascontiguousarray(sin2)

def _consts_host():
    mask = np.where(np.tril(np.ones((128, 128), dtype=bool)), 0.0, NEG).astype(np.float32)
    onesb = np.ones((128, 128), dtype=bfdt)
    onesf = np.ones((128, 1), dtype=np.float32)
    ident = np.eye(128, dtype=bfdt)
    onesrow_f = np.ones((1, 128), dtype=np.float32)
    return mask, onesb, onesf, ident, onesrow_f



def build_fused(dev=False):
    nc = bass.Bass("TRN2", target_bir_lowering=False)
    E = Emit()
    es = ExitStack()

    def din(name, shape, dt=F32):
        return nc.dram_tensor(name, shape, dt, kind="ExternalInput").ap()

    def dout(name, shape, dt=F32):
        return nc.dram_tensor(name, shape, dt, kind="ExternalOutput").ap()

    def dint(name, shape, dt=BF16):
        kind = "ExternalOutput" if dev else "Internal"
        return nc.dram_tensor(name, shape, dt, kind=kind).ap()

    xT = din("xT", [D, T])
    wqT = din("wqT", [D, D])
    wkT = din("wkT", [D, 512])
    wvT = din("wvT", [D, 512])
    woT = din("woT", [D, D])
    bcols = din("bcols", [128, 40])
    bvrow = din("bvrow", [1, 512])
    qg = din("qg", [1, 16])
    ropec = din("ropec", [128, T], BF16)
    ropes = din("ropes", [128, T], BF16)
    maskc = din("maskc", [128, 128])
    onesb_in = din("onesb", [128, 128], BF16)
    identb_in = din("identb", [128, 128], BF16)
    onesf_in = din("onesf", [128, 1])
    onesrowf_in = din("onesrowf", [1, 128])
    sel8_in = din("sel8c", [8, 8 * 128], BF16)
    routerT = din("routerT", [D, 8])
    sgT = din("sgT", [D, NI])
    suT = din("suT", [D, NI])
    sdT = din("sdT", [NI, D])
    wgTs = [din(f"wgT{e}", [D, NI]) for e in range(NEXP)]
    wuTs = [din(f"wuT{e}", [D, NI]) for e in range(NEXP)]
    wdTs = [din(f"wdT{e}", [NI, D]) for e in range(NEXP)]

    outT = dout("outT", [D, T])

    x1T = dint("x1T", [D, T], F32)
    xnT = dint("xnT", [D, T])
    qrawT = dint("qrawT", [D, T])
    QT = dint("QT", [D, T])
    KT = dint("KT", [512, T])
    Vtok = dint("Vtok", [T, 512])
    OT = dint("OT", [D, T])
    hshT = dint("hshT", [NI, T])
    heT = dint("heT", [NI, T])
    accT = dint("accTl0", [D, T], F32)

    sb = lambda name, shape, dt=BF16: es.enter_context(nc.sbuf_tensor(name, shape, dt))
    xns = sb("xns", [128, NC * T])
    wbufA = sb("wbufA", [128, 16 * 512])
    wbufB = sb("wbufB", [128, 16 * 512])
    wbufW = sb("wbufW", [128, 64 * 128])
    wbufC = wbufW
    sA = sb("sA", [128, 2048])
    sB = sb("sB", [128, 2048])
    fA = sb("fA", [128, 512], F32)
    fB = sb("fB", [128, 512], F32)
    ktile = sb("ktile", [128, 2048])
    vtile = sb("vtile", [128, 2048])
    ptile = sb("ptile", [128, 2048])
    sD = ptile
    pttile = sb("pttile", [128, 2048])
    cbs_full = pttile
    qtile = sb("qtile", [128, 128])
    rows = sb("rows", [1, 1536], F32)
    rowsb = sb("rowsb", [1, 2 * T])
    comb8 = sb("comb8", [8, T])
    sel8 = sb("sel8", [8, 8 * 128])
    combtok = sb("combtok", [128, 16 * 8], F32)
    mbs = sb("mbs", [128, 512])
    rbs = sb("rbs", [128, 512])
    cols = sb("cols", [128, 8], F32)
    bvb = sb("bvb", [128, 512])
    maskb = sb("maskb", [128, 128], F32)
    onesb = sb("onesbS", [128, 128])
    identb = sb("identbS", [128, 128])
    onesf = sb("onesfS", [128, 1], F32)
    onesrowf = sb("onesrowfS", [1, 128], F32)
    biasc = sb("biascS", [128, 40], F32)
    qgS = sb("qgS", [1, 16], F32)
    epsc = sb("epsc", [1, 2], F32)
    hbig = sb("hbig", [128, DFF1])
    h8 = sb("h8", [128, DFF1], I8)
    x8 = h8[:, 0:2048]
    fX = sb("fX", [128, 2048], F32)
    scol = sb("scol", [128, 256], F32)
    statc = sb("statc", [128, 16], F32)
    eps128 = sb("eps128S", [128, 1], F32)

    ps = lambda name, shape, dt=F32: es.enter_context(nc.psum_tensor(name, shape, dt))
    pS = ps("pS", [128, 2048])
    pA = ps("pA", [128, 512])
    pB = ps("pB", [128, 512])
    pT = ps("pT", [128, 128], BF16)
    pO = ps("pO", [128, 512])
    pR = pO[0:1, 0:512]

    sem = es.enter_context(nc.semaphore("gsem"))
    block = es.enter_context(nc.Block())

    E.dma(maskb[:], maskc)
    E.dma(onesb[:], onesb_in)
    E.dma(identb[:], identb_in)
    E.dma(onesf[:], onesf_in)
    E.dma(onesrowf[:], onesrowf_in)
    E.dma(biasc[:], bcols)
    E.dma(qgS[:], qg)
    E.gp(lambda e: e.memset(epsc[:, 0:1], 1e-5))
    E.gp(lambda e: e.memset(epsc[:, 1:2], 1.1920929e-07))
    E.gp(lambda e: e.memset(eps128[:], 1e-5))
    E.dma(sel8[:], sel8_in)
    ropec_s = sb("ropecS", [128, T])
    ropes_s = sb("ropesS", [128, T])
    E.dma(ropec_s[:], ropec)
    E.dma(ropes_s[:], ropes)
    E.dma(rows[:, 0:512], bvrow)
    E.dve(lambda e: e.tensor_copy(rowsb[:, 0:512], rows[:, 0:512]))
    E.pe(lambda e: e.matmul(pA[:], onesb[0:1, :], rowsb[0:1, 0:512], start=True, stop=True))
    E.dve(lambda e: e.tensor_copy(bvb[:], pA[:]))

    def wsl(wT_dram, oc, nch, width=128):
        return wT_dram.rearrange("(c p) o -> p c o", p=128)[:, :, oc * width:(oc + 1) * width]

    def bufv(buf, nch, width=128):
        return buf[:, 0:nch * width].rearrange("p (c o) -> p c o", o=width)

    ONE_COL = onesb[:, 0:1]
    ONE_ROW = onesb[0:1, :]

    def bcast_row_bf16(src_row_bf, psum, dst_bf, n):
        E.pe(lambda e: e.matmul(psum[:, 0:n], ONE_ROW, src_row_bf, start=True, stop=True))
        E.dve(lambda e: e.tensor_copy(dst_bf[:, 0:n], psum[:, 0:n]))

    for t4 in range(NT4):
        tsl = slice(t4 * 512, (t4 + 1) * 512)
        for c in range(NC):
            xv = xns[:, c * T + t4 * 512: c * T + (t4 + 1) * 512]
            E.dmag(xv, xT[c * 128:(c + 1) * 128, tsl])
        for c in range(NC):
            xv = xns[:, c * T + t4 * 512: c * T + (t4 + 1) * 512]
            E.pe(lambda e, xv=xv, c=c: e.matmul(pR, ONE_COL, xv, start=(c == 0), stop=(c == NC - 1)))
        E.dve(lambda e: e.tensor_scalar(rows[:, 0:512], pR, 1.0 / D, None, AluOpType.mult))
        for c in range(NC):
            xv = xns[:, c * T + t4 * 512: c * T + (t4 + 1) * 512]
            E.act(lambda e, xv=xv: e.activation(sA[:, 0:512], xv, AF.Square))
            E.pe(lambda e, c=c: e.matmul(pR, ONE_COL, sA[:, 0:512], start=(c == 0), stop=(c == NC - 1)))
        mrow = rows[:, 0:512]
        e2row = rows[:, 512:1024]
        E.dve(lambda e, e2row=e2row: e.tensor_scalar(e2row, pR, 1.0 / D, None, AluOpType.mult))
        msq = rows[:, 1024:1536]
        E.dve(lambda e, mrow=mrow, msq=msq: e.tensor_tensor(msq, mrow, mrow, AluOpType.mult))
        E.dve(lambda e, e2row=e2row, msq=msq: e.tensor_tensor(e2row, e2row, msq, AluOpType.subtract))
        E.act(lambda e, e2row=e2row: e.activation(e2row, e2row, AF.Ln, bias=epsc[:, 0:1]))
        E.act(lambda e, e2row=e2row: e.activation(e2row, e2row, AF.Exp, scale=-0.5))
        E.dve(lambda e, mrow=mrow, t4=t4: e.tensor_copy(rowsb[:, t4 * 512:(t4 + 1) * 512], mrow))
        E.dve(lambda e, e2row=e2row, t4=t4: e.tensor_copy(rowsb[:, T + t4 * 512: T + (t4 + 1) * 512], e2row))
    for t4 in range(NT4):
        bcast_row_bf16(rowsb[0:1, t4 * 512:(t4 + 1) * 512], pA, mbs, 512)
        bcast_row_bf16(rowsb[0:1, T + t4 * 512: T + (t4 + 1) * 512], pB, rbs, 512)
        for c in range(NC):
            xv = xns[:, c * T + t4 * 512: c * T + (t4 + 1) * 512]
            E.dve(lambda e, xv=xv: e.tensor_tensor(sA[:, 0:512], xv, mbs[:], AluOpType.subtract))
            E.dve(lambda e, xv=xv: e.tensor_tensor(xv, sA[:, 0:512], rbs[:], AluOpType.mult))
            if dev:
                E.dma(xnT[c * 128:(c + 1) * 128, t4 * 512:(t4 + 1) * 512], xv)

    def proj(dst_dram, wT_dram, out_chunks, bias_off=None, dst_scale=None):
        for oc in range(out_chunks):
            E.dmag(bufv(wbufA, NC), wsl(wT_dram, oc, NC))
            for t4 in range(NT4):
                for c in range(NC):
                    xv = xns[:, c * T + t4 * 512: c * T + (t4 + 1) * 512]
                    E.pe(lambda e, c=c, xv=xv: e.matmul(
                        pA[:], wbufA[:, c * 128:(c + 1) * 128], xv,
                        start=(c == 0), stop=(c == NC - 1)))
                if bias_off is not None:
                    bc = biasc[:, bias_off + oc: bias_off + oc + 1]
                    E.dve(lambda e, bc=bc: e.tensor_scalar(sA[:, 0:512], pA[:], bc, None, AluOpType.add))
                else:
                    E.dve(lambda e: e.tensor_copy(sA[:, 0:512], pA[:]))
                E.dma(dst_dram[oc * 128:(oc + 1) * 128, t4 * 512:(t4 + 1) * 512], sA[:, 0:512])

    proj(qrawT, wqT, 16, bias_off=0)

    kraw = dint("krawT", [512, T])
    proj(kraw, wkT, 4, bias_off=16)

    SCL = float(1.0 / np.sqrt(HD))

    def rms_rope(src_dram, dst_dram, n_heads, head_of, gain_fold):
        for h in range(n_heads):
            hc = head_of(h)
            E.dma(sA[:], src_dram[hc * 128:(hc + 1) * 128, :])
            for t4 in range(NT4):
                tsl = slice(t4 * 512, (t4 + 1) * 512)
                E.act(lambda e, tsl=tsl: e.activation(sB[:, 0:512], sA[:, tsl], AF.Square))
                E.pe(lambda e: e.matmul(pR, ONE_COL, sB[:, 0:512], start=True, stop=True))
                rr = rows[:, 0:512]
                E.act(lambda e, rr=rr: e.activation(rr, pR, AF.Ln, bias=epsc[:, 1:2], scale=1.0 / HD))
                E.act(lambda e, rr=rr: e.activation(rr, rr, AF.Exp, scale=-0.5))
                if gain_fold is not None:
                    gf = gain_fold(h)
                    E.dve(lambda e, rr=rr, gf=gf: e.tensor_scalar(rr, rr, gf, SCL, AluOpType.mult, AluOpType.mult))
                E.dve(lambda e, rr=rr, t4=t4: e.tensor_copy(rowsb[:, t4 * 512:(t4 + 1) * 512], rr))
            E.dma(sD[0:64, :], sA[64:128, :])
            E.dma(sD[64:128, :], sA[0:64, :])
            E.dve(lambda e: e.tensor_tensor(sB[:], sA[:], ropec_s[:], AluOpType.mult))
            E.dve(lambda e: e.tensor_tensor(sD[:], sD[:], ropes_s[:], AluOpType.mult))
            E.dve(lambda e: e.tensor_tensor(sB[:], sB[:], sD[:], AluOpType.add))
            for t4 in range(NT4):
                tsl = slice(t4 * 512, (t4 + 1) * 512)
                bcast_row_bf16(rowsb[0:1, t4 * 512:(t4 + 1) * 512], pA, rbs, 512)
                E.dve(lambda e, tsl=tsl: e.tensor_tensor(sB[:, tsl], sB[:, tsl], rbs[:], AluOpType.mult))
                E.dma(dst_dram[hc * 128:(hc + 1) * 128, tsl], sB[:, tsl])

    rms_rope(qrawT, QT, 16, lambda h: h,
             gain_fold=lambda h: qgS[0:1, h:h + 1])
    rms_rope(kraw, KT, 4, lambda h: h, gain_fold=None)

    E.dmag(bufv(wbufC, NC, 512), wsl(wvT, 0, NC, 512))
    for t16 in range(NT16):
        for c in range(NC):
            lh = xns[:, c * T + t16 * 128: c * T + (t16 + 1) * 128]
            rh = wbufC[:, c * 512:(c + 1) * 512]
            E.pe(lambda e, lh=lh, rh=rh, c=c: e.matmul(pA[:], lh, rh, start=(c == 0), stop=(c == NC - 1)))
        E.dve(lambda e: e.tensor_copy(sA[:, 0:512], pA[:]))
        E.dve(lambda e: e.tensor_tensor(sA[:, 0:512], sA[:, 0:512], bvb[:], AluOpType.add))
        E.dma(Vtok[t16 * 128:(t16 + 1) * 128, :], sA[:, 0:512])

    E.dmag(bufv(wbufA, NC, 8), wsl(routerT, 0, NC, 8))
    for t16 in range(NT16):
        for c in range(NC):
            lh = xns[:, c * T + t16 * 128: c * T + (t16 + 1) * 128]
            E.pe(lambda e, lh=lh, c=c: e.matmul(pA[:, 0:8], lh, wbufA[:, c * 8:(c + 1) * 8],
                                                start=(c == 0), stop=(c == NC - 1)))
        E.dve(lambda e: e.tensor_reduce(cols[:, 0:1], pA[:, 0:8], AX.X, AluOpType.max))
        E.dve(lambda e: e.tensor_scalar(cols[:, 1:2], cols[:, 0:1], -1.0, None, AluOpType.mult))
        def _exp_router(e):
            e.activation(fA[:, 0:8], pA[:, 0:8], AF.Exp, bias=cols[:, 1:2], accum_out=cols[:, 2:3])
            return e.copy(cols[:, 6:7], cols[:, 2:3])
        E.act(_exp_router)
        E.dve(lambda e: e.reciprocal(cols[:, 3:4], cols[:, 2:3]))
        E.dve(lambda e: e.tensor_scalar(fA[:, 0:8], fA[:, 0:8], cols[:, 3:4], None, AluOpType.mult))
        E.dve(lambda e: e.tensor_reduce(cols[:, 4:5], fA[:, 0:8], AX.X, AluOpType.max))
        E.dve(lambda e: e.tensor_scalar(fB[:, 0:8], fA[:, 0:8], cols[:, 4:5], None, AluOpType.is_ge))
        E.dve(lambda e: e.scalar_tensor_tensor(fB[:, 0:8], fB[:, 0:8], NEG, fA[:, 0:8], AluOpType.mult, AluOpType.add))
        E.dve(lambda e: e.tensor_reduce(cols[:, 5:6], fB[:, 0:8], AX.X, AluOpType.max))
        E.dve(lambda e: e.tensor_scalar(fB[:, 0:8], fA[:, 0:8], cols[:, 5:6], None, AluOpType.is_ge))
        E.dve(lambda e, t16=t16: e.tensor_tensor(combtok[:, t16 * 8:(t16 + 1) * 8], fA[:, 0:8], fB[:, 0:8], AluOpType.mult))
    for t16 in range(NT16):
        E.dve(lambda e, t16=t16: e.tensor_copy(sA[:, 0:8], combtok[:, t16 * 8:(t16 + 1) * 8]))
        E.pe(lambda e: e.transpose(pT[0:8, 0:128], sA[:, 0:8], identb[:]))
        E.dve(lambda e, t16=t16: e.tensor_copy(comb8[0:8, t16 * 128:(t16 + 1) * 128], pT[0:8, 0:128]))

    def gated_ffn(gT, uT, dst_dram, comb_e=None):
        if comb_e is not None:
            for t4 in range(NT4):
                E.pe(lambda e, t4=t4, comb_e=comb_e: e.matmul(
                    pO[:], sel8[:, comb_e * 128:(comb_e + 1) * 128],
                    comb8[0:8, t4 * 512:(t4 + 1) * 512], start=True, stop=True))
                E.dve(lambda e, t4=t4: e.tensor_copy(cbs_full[:, t4 * 512:(t4 + 1) * 512], pO[:]))
        for oc in range(NIC):
            E.dmag(bufv(wbufA, NC), wsl(gT, oc, NC))
            E.dmag(bufv(wbufB, NC), wsl(uT, oc, NC))
            for t4 in range(NT4):
                for c in range(NC):
                    xv = xns[:, c * T + t4 * 512: c * T + (t4 + 1) * 512]
                    E.pe(lambda e, c=c, xv=xv: e.matmul(pA[:], wbufA[:, c * 128:(c + 1) * 128], xv,
                                                        start=(c == 0), stop=(c == NC - 1)))
                for c in range(NC):
                    xv = xns[:, c * T + t4 * 512: c * T + (t4 + 1) * 512]
                    E.pe(lambda e, c=c, xv=xv: e.matmul(pB[:], wbufB[:, c * 128:(c + 1) * 128], xv,
                                                        start=(c == 0), stop=(c == NC - 1)))
                E.act(lambda e: e.activation(sA[:, 0:512], pA[:], AF.Sigmoid))
                E.dve(lambda e: e.tensor_tensor(sB[:, 0:512], sA[:, 0:512], pB[:], AluOpType.mult))
                if comb_e is not None:
                    E.dve(lambda e, t4=t4: e.tensor_tensor(sB[:, 0:512], sB[:, 0:512],
                                                           cbs_full[:, t4 * 512:(t4 + 1) * 512], AluOpType.mult))
                E.dma(dst_dram[oc * 128:(oc + 1) * 128, t4 * 512:(t4 + 1) * 512], sB[:, 0:512])

    def down_proj(dT, src_dram, first):
        for oc in range(NC):
            E.dmag(bufv(wbufA, NIC), wsl(dT, oc, NIC))
            for t4 in range(NT4):
                tsl = slice(t4 * 512, (t4 + 1) * 512)
                for ic in range(NIC):
                    E.dma(sA[:, 0:512], src_dram[ic * 128:(ic + 1) * 128, tsl])
                    E.pe(lambda e, ic=ic: e.matmul(pA[:], wbufA[:, ic * 128:(ic + 1) * 128], sA[:, 0:512],
                                                   start=(ic == 0), stop=(ic == NIC - 1)))
                if first:
                    E.dma(fA[:], xT[oc * 128:(oc + 1) * 128, tsl])
                    xnv = xns[:, oc * T + t4 * 512: oc * T + (t4 + 1) * 512]
                    E.dve(lambda e, xnv=xnv: e.tensor_tensor(fA[:], fA[:], xnv, AluOpType.add))
                else:
                    E.dma(fA[:], accT[oc * 128:(oc + 1) * 128, tsl])
                E.dve(lambda e: e.tensor_tensor(fA[:], fA[:], pA[:], AluOpType.add))
                E.dma(accT[oc * 128:(oc + 1) * 128, tsl], fA[:])

    gated_ffn(sgT, suT, hshT, comb_e=None)
    down_proj(sdT, hshT, first=True)
    for e_i in range(NEXP):
        gated_ffn(wgTs[e_i], wuTs[e_i], heT, comb_e=e_i)
        down_proj(wdTs[e_i], heT, first=False)

    for qt in range(NT16):
        nk = (qt + 1) * 128
        nk4 = (nk + 511) // 512
        for kv in range(KVH0):
            E.dma(ktile[:, 0:nk], KT[kv * 128:(kv + 1) * 128, 0:nk])
            E.dma(vtile[:, 0:nk].rearrange("p (b d) -> p b d", d=128),
                  Vtok[0:nk, kv * 128:(kv + 1) * 128].rearrange("(b p) d -> p b d", p=128))
            for hq in range(4):
                h = kv * 4 + hq
                E.dma(qtile[:], QT[h * 128:(h + 1) * 128, qt * 128:(qt + 1) * 128])
                for kb4 in range(nk4):
                    w = min(512, nk - kb4 * 512)
                    E.pe(lambda e, kb4=kb4, w=w: e.matmul(
                        pS[:, kb4 * 512: kb4 * 512 + w], qtile[:], ktile[:, kb4 * 512: kb4 * 512 + w],
                        start=True, stop=True))
                E.dve(lambda e, qt=qt: e.tensor_tensor(pS[:, qt * 128:(qt + 1) * 128],
                                                       pS[:, qt * 128:(qt + 1) * 128], maskb[:], AluOpType.add))
                E.dve(lambda e, nk=nk: e.tensor_reduce(cols[:, 0:1], pS[:, 0:nk], AX.X, AluOpType.max))
                E.dve(lambda e: e.tensor_scalar(cols[:, 1:2], cols[:, 0:1], -1.0, None, AluOpType.mult))
                def _exp_attn(e, nk=nk):
                    e.activation(ptile[:, 0:nk], pS[:, 0:nk], AF.Exp,
                                 bias=cols[:, 1:2], accum_out=cols[:, 2:3])
                    return e.copy(cols[:, 6:7], cols[:, 2:3])
                E.act(_exp_attn)
                E.dve(lambda e: e.reciprocal(cols[:, 3:4], cols[:, 2:3]))
                E.dve(lambda e, nk=nk: e.tensor_scalar(ptile[:, 0:nk], ptile[:, 0:nk], cols[:, 3:4], None, AluOpType.mult))
                for kb in range(qt + 1):
                    E.pe(lambda e, kb=kb: e.transpose(pT[:], ptile[:, kb * 128:(kb + 1) * 128], identb[:]))
                    E.dve(lambda e, kb=kb: e.tensor_copy(pttile[:, kb * 128:(kb + 1) * 128], pT[:]))
                for kb in range(qt + 1):
                    E.pe(lambda e, kb=kb, qt=qt: e.matmul(
                        pO[:, 0:128], vtile[:, kb * 128:(kb + 1) * 128], pttile[:, kb * 128:(kb + 1) * 128],
                        start=(kb == 0), stop=(kb == qt)))
                E.dve(lambda e: e.tensor_copy(sA[:, 0:128], pO[:, 0:128]))
                E.dma(OT[h * 128:(h + 1) * 128, qt * 128:(qt + 1) * 128], sA[:, 0:128])

    for oc in range(NC):
        E.dmag(bufv(wbufA, NC), wsl(woT, oc, NC))
        for t4 in range(NT4):
            tsl = slice(t4 * 512, (t4 + 1) * 512)
            for c in range(NC):
                E.dma(sA[:, 0:512], OT[c * 128:(c + 1) * 128, tsl])
                E.pe(lambda e, c=c: e.matmul(pA[:], wbufA[:, c * 128:(c + 1) * 128], sA[:, 0:512],
                                             start=(c == 0), stop=(c == NC - 1)))
            E.dma(fA[:], accT[oc * 128:(oc + 1) * 128, tsl])
            bc = biasc[:, 24 + oc: 24 + oc + 1]
            E.dve(lambda e, bc=bc: e.scalar_tensor_tensor(fB[:], pA[:], bc, fA[:], AluOpType.add, AluOpType.add))
            E.dma(x1T[oc * 128:(oc + 1) * 128, tsl], fB[:])


    m1wqT = din("m1wqT", [D, D])
    m1wkT = din("m1wkT", [D, D])
    m1wvT = din("m1wvT", [D, D])
    m1woT = din("m1woT", [D, D])
    m1w1T = din("m1w1T", [D, DFF1])
    m1w2T = din("m1w2T", [D, DFF1])
    m1w3T = din("m1w3T", [DFF1, D])

    xqAtok = dint("xqAtok", [T, D])
    xqFtok = dint("xqFtok", [T, D])
    xqAT = dint("xqAT", [D, T])
    xqFT = dint("xqFT", [D, T])
    wq_q = dint("wq_q", [D, D])
    wk_q = dint("wk_q", [D, D])
    wv_q = dint("wv_q", [D, D])
    wo_q = dint("wo_q", [D, D])
    w1_q = dint("w1_q", [D, DFF1])
    w2_q = dint("w2_q", [D, DFF1])
    w3_q = dint("w3_q", [DFF1, D])
    QT1 = dint("QT1", [D, T])
    KT1 = dint("KT1", [D, T])
    Vtok1 = dint("Vtok1", [T, D])
    OT1 = dint("OT1", [D, T])
    Oqtok = dint("Oqtok", [T, D])
    OqT = dint("OqT", [D, T])
    htok = dint("htok", [T, DFF1])
    hqT = dint("hqT", [DFF1, T])
    accT1 = dint("accTl1", [D, T], F32)
    p11 = pO[0:1, 0:1]

    for t4 in range(NT4):
        tsl = slice(t4 * 512, (t4 + 1) * 512)
        for c in range(NC):
            xv = xns[:, c * T + t4 * 512: c * T + (t4 + 1) * 512]
            E.dmag(xv, x1T[c * 128:(c + 1) * 128, tsl])
        for c in range(NC):
            xv = xns[:, c * T + t4 * 512: c * T + (t4 + 1) * 512]
            E.pe(lambda e, xv=xv, c=c: e.matmul(pR, ONE_COL, xv, start=(c == 0), stop=(c == NC - 1)))
        E.dve(lambda e: e.tensor_scalar(rows[:, 0:512], pR, 1.0 / D, None, AluOpType.mult))
        for c in range(NC):
            xv = xns[:, c * T + t4 * 512: c * T + (t4 + 1) * 512]
            E.act(lambda e, xv=xv: e.activation(sA[:, 0:512], xv, AF.Square))
            E.pe(lambda e, c=c: e.matmul(pR, ONE_COL, sA[:, 0:512], start=(c == 0), stop=(c == NC - 1)))
        mrow = rows[:, 0:512]
        e2row = rows[:, 512:1024]
        msq = rows[:, 1024:1536]
        E.dve(lambda e, e2row=e2row: e.tensor_scalar(e2row, pR, 1.0 / D, None, AluOpType.mult))
        E.dve(lambda e, mrow=mrow, msq=msq: e.tensor_tensor(msq, mrow, mrow, AluOpType.mult))
        E.dve(lambda e, e2row=e2row, msq=msq: e.tensor_tensor(e2row, e2row, msq, AluOpType.subtract))
        E.act(lambda e, e2row=e2row: e.activation(e2row, e2row, AF.Ln, bias=epsc[:, 0:1]))
        E.act(lambda e, e2row=e2row: e.activation(e2row, e2row, AF.Exp, scale=-0.5))
        E.dve(lambda e, mrow=mrow, t4=t4: e.tensor_copy(rowsb[:, t4 * 512:(t4 + 1) * 512], mrow))
        E.dve(lambda e, e2row=e2row, t4=t4: e.tensor_copy(rowsb[:, T + t4 * 512: T + (t4 + 1) * 512], e2row))
    for t4 in range(NT4):
        bcast_row_bf16(rowsb[0:1, t4 * 512:(t4 + 1) * 512], pA, mbs, 512)
        bcast_row_bf16(rowsb[0:1, T + t4 * 512: T + (t4 + 1) * 512], pB, rbs, 512)
        for c in range(NC):
            xv = xns[:, c * T + t4 * 512: c * T + (t4 + 1) * 512]
            E.dve(lambda e, xv=xv: e.tensor_tensor(sA[:, 0:512], xv, mbs[:], AluOpType.subtract))
            E.dve(lambda e, xv=xv: e.tensor_tensor(xv, sA[:, 0:512], rbs[:], AluOpType.mult))

    f3 = lambda t: t.rearrange("p (g i) -> p g i", i=64)
    for t16 in range(NT16):
        rsl = slice(t16 * 128, (t16 + 1) * 128)
        for c in range(NC):
            E.pe(lambda e, c=c, t16=t16: e.transpose(
                pT[:], xns[:, c * T + t16 * 128: c * T + (t16 + 1) * 128], identb[:]))
            E.dve(lambda e, c=c: e.tensor_copy(sA[:, c * 128:(c + 1) * 128], pT[:]))
        E.dve(lambda e: e.tensor_reduce(cols[:, 3:4], sA[:], AX.X, AluOpType.max, apply_absolute_value=True))
        E.dve(lambda e: e.tensor_scalar(cols[:, 3:4], cols[:, 3:4], 1e-5, None, AluOpType.max))
        E.dve(lambda e: e.reciprocal(cols[:, 4:5], cols[:, 3:4]))
        E.dve(lambda e: e.tensor_scalar(x8, sA[:], cols[:, 4:5], 127.0, AluOpType.mult, AluOpType.mult))
        E.dve(lambda e: e.tensor_scalar(sB[:], x8, cols[:, 3:4], 1.0 / 127.0, AluOpType.mult, AluOpType.mult))
        E.dma(xqAtok[rsl, :], sB[:])
        E.dve(lambda e: e.tensor_reduce(scol[:, 0:32], f3(sA[:]), AX.X, AluOpType.max, apply_absolute_value=True))
        E.dve(lambda e: e.tensor_scalar(scol[:, 0:32], scol[:, 0:32], 1e-5, None, AluOpType.max))
        E.dve(lambda e: e.reciprocal(scol[:, 32:64], scol[:, 0:32]))
        E.dve(lambda e: e.tensor_scalar(scol[:, 32:64], scol[:, 32:64], 127.0, None, AluOpType.mult))
        E.dve(lambda e: e.tensor_tensor(f3(x8), f3(sA[:]),
                                        scol[:, 32:64].unsqueeze(2).broadcast_to([128, 32, 64]), AluOpType.mult))
        E.dve(lambda e: e.tensor_scalar(scol[:, 64:96], scol[:, 0:32], 1.0 / 127.0, None, AluOpType.mult))
        E.dve(lambda e: e.tensor_tensor(f3(sB[:]), f3(x8),
                                        scol[:, 64:96].unsqueeze(2).broadcast_to([128, 32, 64]), AluOpType.mult))
        E.dma(xqFtok[rsl, :], sB[:])

    def transpose_tok_to_fm(src_tok, dst_T):
        for t16 in range(NT16):
            E.dma(sA[:], src_tok[t16 * 128:(t16 + 1) * 128, :])
            for c in range(NC):
                E.pe(lambda e, c=c: e.transpose(pT[:], sA[:, c * 128:(c + 1) * 128], identb[:]))
                E.dve(lambda e, c=c: e.tensor_copy(sB[:, c * 128:(c + 1) * 128], pT[:]))
            for c in range(NC):
                E.dma(dst_T[c * 128:(c + 1) * 128, t16 * 128:(t16 + 1) * 128], sB[:, c * 128:(c + 1) * 128])

    transpose_tok_to_fm(xqAtok, xqAT)
    transpose_tok_to_fm(xqFtok, xqFT)

    def weight_stats2(i, wT_dram, numel, clip_eps):
        rows_n = wT_dram.shape[0]
        wT3 = wT_dram.rearrange("(c p) o -> p c o", p=128)
        nco = wT_dram.shape[1] // 2048
        E.gp(lambda e: e.memset(scol[:, 96:97], 0.0))
        for c in range(rows_n // 128):
            for ob in range(nco):
                E.dma(fX[:], wT3[:, c, ob * 2048:(ob + 1) * 2048])
                def _abs(e):
                    e.activation(fX[:], fX[:], AF.Abs, accum_out=cols[:, 7:8])
                    return e.copy(cols[:, 6:7], cols[:, 7:8])
                E.act(_abs)
                E.dve(lambda e: e.tensor_tensor(scol[:, 96:97], scol[:, 96:97], cols[:, 7:8], AluOpType.add))
        E.pe(lambda e: e.matmul(p11, scol[:, 96:97], onesf[:], start=True, stop=True))
        E.dve(lambda e, i=i, numel=numel: e.tensor_scalar(statc[0:1, 8 + i:9 + i], p11, 1.0 / numel, None, AluOpType.mult))
        if clip_eps is not None:
            E.dve(lambda e, i=i, clip_eps=clip_eps: e.tensor_scalar(
                statc[0:1, 8 + i:9 + i], statc[0:1, 8 + i:9 + i], clip_eps, None, AluOpType.max))
        E.pe(lambda e, i=i: e.matmul(pO[:, 0:1], onesrowf[0:1, :], statc[0:1, 8 + i:9 + i], start=True, stop=True))
        E.dve(lambda e, i=i: e.tensor_copy(statc[:, i:i + 1], pO[:, 0:1]))
        E.dve(lambda e, i=i: e.reciprocal(statc[:, 8 + i:9 + i], statc[:, i:i + 1]))

    WSTATS = [(m1wqT, D * D), (m1wkT, D * D), (m1wvT, D * D), (m1woT, D * D),
              (m1w1T, D * DFF1), (m1w2T, D * DFF1), (m1w3T, DFF1 * D)]
    for i, (wt, ne) in enumerate(WSTATS):
        weight_stats2(i, wt, ne, 1e-5 if i < 4 else None)

    def quant_weight_ternary(i, wT_dram, dst):
        rows_n = wT_dram.shape[0]
        wT3 = wT_dram.rearrange("(c p) o -> p c o", p=128)
        dst3 = dst.rearrange("(c p) o -> p c o", p=128)
        nco = wT_dram.shape[1] // 2048
        for c in range(rows_n // 128):
            for ob in range(nco):
                E.dma(fX[:], wT3[:, c, ob * 2048:(ob + 1) * 2048])
                E.dve(lambda e, i=i: e.tensor_scalar(fX[:], fX[:], statc[:, 8 + i:9 + i], 1.0, AluOpType.mult, AluOpType.min))
                E.dve(lambda e: e.tensor_scalar(x8, fX[:], -1.0, None, AluOpType.max))
                E.dve(lambda e: e.tensor_copy(sA[:], x8))
                E.dma(dst3[:, c, ob * 2048:(ob + 1) * 2048], sA[:])

    def quant_weight_sign(wT_dram, dst):
        rows_n = wT_dram.shape[0]
        wT3 = wT_dram.rearrange("(c p) o -> p c o", p=128)
        dst3 = dst.rearrange("(c p) o -> p c o", p=128)
        nco = wT_dram.shape[1] // 2048
        for c in range(rows_n // 128):
            for ob in range(nco):
                E.dma(fX[:], wT3[:, c, ob * 2048:(ob + 1) * 2048])
                E.act(lambda e: e.activation(sA[:], fX[:], AF.Sign))
                E.dma(dst3[:, c, ob * 2048:(ob + 1) * 2048], sA[:])

    quant_weight_ternary(0, m1wqT, wq_q)
    quant_weight_ternary(1, m1wkT, wk_q)
    quant_weight_ternary(2, m1wvT, wv_q)
    quant_weight_ternary(3, m1woT, wo_q)
    quant_weight_sign(m1w1T, w1_q)
    quant_weight_sign(m1w2T, w2_q)
    quant_weight_sign(m1w3T, w3_q)

    for c in range(NC):
        E.dma(xns[:, c * T:(c + 1) * T], xqAT[c * 128:(c + 1) * 128, :])

    def proj_l1(dst_dram, wq_dram, dq_i, extra_scale):
        for oc in range(NC):
            E.dma(bufv(wbufA, NC), wsl(wq_dram, oc, NC))
            for t4 in range(NT4):
                for c in range(NC):
                    xv = xns[:, c * T + t4 * 512: c * T + (t4 + 1) * 512]
                    E.pe(lambda e, c=c, xv=xv: e.matmul(pA[:], wbufA[:, c * 128:(c + 1) * 128], xv,
                                                        start=(c == 0), stop=(c == NC - 1)))
                E.dve(lambda e, dq_i=dq_i, extra_scale=extra_scale: e.tensor_scalar(
                    sA[:, 0:512], pA[:], statc[:, dq_i:dq_i + 1], extra_scale, AluOpType.mult, AluOpType.mult))
                E.dma(dst_dram[oc * 128:(oc + 1) * 128, t4 * 512:(t4 + 1) * 512], sA[:, 0:512])

    proj_l1(QT1, wq_q, 0, SCL)
    proj_l1(KT1, wk_q, 1, 1.0)

    for vb in range(4):
        E.dma(bufv(wbufA, NC, 512), wsl(wv_q, vb, NC, 512))
        for t16 in range(NT16):
            for c in range(NC):
                lh = xns[:, c * T + t16 * 128: c * T + (t16 + 1) * 128]
                rh = wbufA[:, c * 512:(c + 1) * 512]
                E.pe(lambda e, lh=lh, rh=rh, c=c: e.matmul(pA[:], lh, rh, start=(c == 0), stop=(c == NC - 1)))
            E.dve(lambda e: e.tensor_scalar(sA[:, 0:512], pA[:], statc[:, 2:3], None, AluOpType.mult))
            E.dma(Vtok1[t16 * 128:(t16 + 1) * 128, vb * 512:(vb + 1) * 512], sA[:, 0:512])

    for c in range(NC):
        E.dma(xns[:, c * T:(c + 1) * T], xqFT[c * 128:(c + 1) * 128, :])

    for ffb in range(16):
        E.dma(bufv(wbufA, NC, 512), wsl(w1_q, ffb, NC, 512))
        E.dma(bufv(wbufB, NC, 512), wsl(w2_q, ffb, NC, 512))
        for t16 in range(NT16):
            for c in range(NC):
                lh = xns[:, c * T + t16 * 128: c * T + (t16 + 1) * 128]
                E.pe(lambda e, lh=lh, c=c: e.matmul(pA[:], lh, wbufA[:, c * 512:(c + 1) * 512],
                                                    start=(c == 0), stop=(c == NC - 1)))
            for c in range(NC):
                lh = xns[:, c * T + t16 * 128: c * T + (t16 + 1) * 128]
                E.pe(lambda e, lh=lh, c=c: e.matmul(pB[:], lh, wbufB[:, c * 512:(c + 1) * 512],
                                                    start=(c == 0), stop=(c == NC - 1)))
            E.act(lambda e: e.activation(sA[:, 0:512], pA[:], AF.Silu, scale=statc[:, 4:5]))
            E.dve(lambda e: e.tensor_scalar(sB[:, 0:512], pB[:], statc[:, 5:6], None, AluOpType.mult))
            E.dve(lambda e: e.tensor_tensor(sA[:, 0:512], sA[:, 0:512], sB[:, 0:512], AluOpType.mult))
            E.dma(htok[t16 * 128:(t16 + 1) * 128, ffb * 512:(ffb + 1) * 512], sA[:, 0:512])

    f3h = lambda t: t.rearrange("p (g i) -> p g i", i=64)
    for t16 in range(NT16):
        rsl = slice(t16 * 128, (t16 + 1) * 128)
        E.dma(hbig[:], htok[rsl, :])
        E.dve(lambda e: e.tensor_reduce(scol[:, 0:128], f3h(hbig[:]), AX.X, AluOpType.max, apply_absolute_value=True))
        E.dve(lambda e: e.tensor_scalar(scol[:, 0:128], scol[:, 0:128], 1e-5, None, AluOpType.max))
        E.dve(lambda e: e.reciprocal(scol[:, 128:256], scol[:, 0:128]))
        E.dve(lambda e: e.tensor_scalar(scol[:, 128:256], scol[:, 128:256], 127.0, None, AluOpType.mult))
        E.dve(lambda e: e.tensor_tensor(f3h(h8[:]), f3h(hbig[:]),
                                        scol[:, 128:256].unsqueeze(2).broadcast_to([128, 128, 64]), AluOpType.mult))
        E.dve(lambda e: e.tensor_scalar(scol[:, 128:256], scol[:, 0:128], 1.0 / 127.0, None, AluOpType.mult))
        E.dve(lambda e: e.tensor_tensor(f3h(hbig[:]), f3h(h8[:]),
                                        scol[:, 128:256].unsqueeze(2).broadcast_to([128, 128, 64]), AluOpType.mult))
        for fc in range(NFC):
            E.pe(lambda e, fc=fc: e.transpose(pT[:], hbig[:, fc * 128:(fc + 1) * 128], identb[:]))
            E.dve(lambda e: e.tensor_copy(sB[:, 0:128], pT[:]))
            E.dma(hqT[fc * 128:(fc + 1) * 128, rsl], sB[:, 0:128])

    for t4 in range(NT4):
        tsl = slice(t4 * 512, (t4 + 1) * 512)
        for fc in range(NFC):
            E.dma(xns[:, fc * 512:(fc + 1) * 512], hqT[fc * 128:(fc + 1) * 128, tsl])
        for oc in range(NC):
            E.dma(bufv(wbufW, NFC), wsl(m1w3T, oc, NFC) if False else wsl(w3_q, oc, NFC))
            for fc in range(NFC):
                E.pe(lambda e, fc=fc: e.matmul(pA[:], wbufW[:, fc * 128:(fc + 1) * 128],
                                               xns[:, fc * 512:(fc + 1) * 512],
                                               start=(fc == 0), stop=(fc == NFC - 1)))
            E.dma(fA[:], x1T[oc * 128:(oc + 1) * 128, tsl])
            E.dve(lambda e: e.tensor_scalar(fB[:], pA[:], statc[:, 6:7], None, AluOpType.mult))
            E.dve(lambda e: e.tensor_tensor(fB[:], fB[:], fA[:], AluOpType.add))
            E.dma(accT1[oc * 128:(oc + 1) * 128, tsl], fB[:])

    for qt in range(NT16):
        nk = (qt + 1) * 128
        nk4 = (nk + 511) // 512
        for h in range(NH):
            E.dma(ktile[:, 0:nk], KT1[h * 128:(h + 1) * 128, 0:nk])
            E.dma(vtile[:, 0:nk].rearrange("p (b d) -> p b d", d=128),
                  Vtok1[0:nk, h * 128:(h + 1) * 128].rearrange("(b p) d -> p b d", p=128))
            E.dma(qtile[:], QT1[h * 128:(h + 1) * 128, qt * 128:(qt + 1) * 128])
            for kb4 in range(nk4):
                w = min(512, nk - kb4 * 512)
                E.pe(lambda e, kb4=kb4, w=w: e.matmul(
                    pS[:, kb4 * 512: kb4 * 512 + w], qtile[:], ktile[:, kb4 * 512: kb4 * 512 + w],
                    start=True, stop=True))
            E.dve(lambda e, qt=qt: e.tensor_tensor(pS[:, qt * 128:(qt + 1) * 128],
                                                   pS[:, qt * 128:(qt + 1) * 128], maskb[:], AluOpType.add))
            E.dve(lambda e, nk=nk: e.tensor_reduce(cols[:, 0:1], pS[:, 0:nk], AX.X, AluOpType.max))
            E.dve(lambda e: e.tensor_scalar(cols[:, 1:2], cols[:, 0:1], -1.0, None, AluOpType.mult))
            def _exp_attn1(e, nk=nk):
                e.activation(ptile[:, 0:nk], pS[:, 0:nk], AF.Exp,
                             bias=cols[:, 1:2], accum_out=cols[:, 2:3])
                return e.copy(cols[:, 6:7], cols[:, 2:3])
            E.act(_exp_attn1)
            E.dve(lambda e: e.reciprocal(cols[:, 3:4], cols[:, 2:3]))
            E.dve(lambda e, nk=nk: e.tensor_scalar(ptile[:, 0:nk], ptile[:, 0:nk], cols[:, 3:4], None, AluOpType.mult))
            for kb in range(qt + 1):
                E.pe(lambda e, kb=kb: e.transpose(pT[:], ptile[:, kb * 128:(kb + 1) * 128], identb[:]))
                E.dve(lambda e, kb=kb: e.tensor_copy(pttile[:, kb * 128:(kb + 1) * 128], pT[:]))
            for kb in range(qt + 1):
                E.pe(lambda e, kb=kb, qt=qt: e.matmul(
                    pO[:, 0:128], vtile[:, kb * 128:(kb + 1) * 128], pttile[:, kb * 128:(kb + 1) * 128],
                    start=(kb == 0), stop=(kb == qt)))
            E.dve(lambda e: e.tensor_copy(sA[:, 0:128], pO[:, 0:128]))
            E.dma(OT1[h * 128:(h + 1) * 128, qt * 128:(qt + 1) * 128], sA[:, 0:128])

    for t16 in range(NT16):
        rsl = slice(t16 * 128, (t16 + 1) * 128)
        for c in range(NC):
            E.dma(sA[:, c * 128:(c + 1) * 128], OT1[c * 128:(c + 1) * 128, rsl])
        for c in range(NC):
            E.pe(lambda e, c=c: e.transpose(pT[:], sA[:, c * 128:(c + 1) * 128], identb[:]))
            E.dve(lambda e, c=c: e.tensor_copy(sB[:, c * 128:(c + 1) * 128], pT[:]))
        E.dve(lambda e: e.tensor_reduce(cols[:, 3:4], sB[:], AX.X, AluOpType.max, apply_absolute_value=True))
        E.dve(lambda e: e.tensor_scalar(cols[:, 3:4], cols[:, 3:4], 1e-5, None, AluOpType.max))
        E.dve(lambda e: e.reciprocal(cols[:, 4:5], cols[:, 3:4]))
        E.dve(lambda e: e.tensor_scalar(x8, sB[:], cols[:, 4:5], 127.0, AluOpType.mult, AluOpType.mult))
        E.dve(lambda e: e.tensor_scalar(sB[:], x8, cols[:, 3:4], 1.0 / 127.0, AluOpType.mult, AluOpType.mult))
        E.dma(Oqtok[rsl, :], sB[:])
    transpose_tok_to_fm(Oqtok, OqT)

    for oc in range(NC):
        E.dma(bufv(wbufA, NC), wsl(wo_q, oc, NC))
        for t4 in range(NT4):
            tsl = slice(t4 * 512, (t4 + 1) * 512)
            for c in range(NC):
                E.dma(sA[:, 0:512], OqT[c * 128:(c + 1) * 128, tsl])
                E.pe(lambda e, c=c: e.matmul(pA[:], wbufA[:, c * 128:(c + 1) * 128], sA[:, 0:512],
                                             start=(c == 0), stop=(c == NC - 1)))
            E.dma(fA[:], accT1[oc * 128:(oc + 1) * 128, tsl])
            E.dve(lambda e: e.scalar_tensor_tensor(fB[:], pA[:], statc[:, 3:4], fA[:],
                                                   AluOpType.mult, AluOpType.add))
            E.dma(outT[oc * 128:(oc + 1) * 128, tsl], fB[:])

    E.finalize(nc, block, sem)
    es.close()
    return nc



def stage_l0(inputs, b):
    f32 = np.float32
    x = np.asarray(inputs["L0_attn_x"][b], dtype=f32)
    mask, onesb, onesf, ident, onesrowf = _consts_host()
    cos2, sin2 = _rope_tables()
    m = {
        "xT": np.ascontiguousarray(x.T),
        "wqT": np.ascontiguousarray(np.asarray(inputs["L0_attn_Wq"], f32).T),
        "wkT": np.ascontiguousarray(np.asarray(inputs["L0_attn_Wk"], f32).T),
        "wvT": np.ascontiguousarray(np.asarray(inputs["L0_attn_Wv"], f32).T),
        "woT": np.ascontiguousarray(np.asarray(inputs["L0_attn_Wo"], f32).T),
        "bcols": np.concatenate([
            _chunk_cols(np.asarray(inputs["L0_attn_bq"], f32)),
            _chunk_cols(np.asarray(inputs["L0_attn_bk"], f32)),
            _chunk_cols(np.asarray(inputs["L0_attn_bv"], f32)),
            _chunk_cols(np.asarray(inputs["L0_attn_bo"], f32))], axis=1),
        "bvrow": np.ascontiguousarray(np.asarray(inputs["L0_attn_bv"], f32)[None, :]),
        "qg": np.ascontiguousarray(np.asarray(inputs["L0_attn_q_gain"], f32)[None, :]),
        "ropec": cos2, "ropes": sin2,
        "maskc": mask, "onesb": onesb, "identb": ident,
        "onesf": onesf, "onesrowf": onesrowf,
        "routerT": np.ascontiguousarray(np.asarray(inputs["L0_ffn_router_w"], f32).T),
        "sgT": np.ascontiguousarray(np.asarray(inputs["L0_ffn_shared_gate"], f32).T),
        "suT": np.ascontiguousarray(np.asarray(inputs["L0_ffn_shared_up"], f32).T),
        "sdT": np.ascontiguousarray(np.asarray(inputs["L0_ffn_shared_down"], f32).T),
    }
    for e in range(NEXP):
        m[f"wgT{e}"] = np.ascontiguousarray(np.asarray(inputs["L0_ffn_w_gate"][e], f32).T)
        m[f"wuT{e}"] = np.ascontiguousarray(np.asarray(inputs["L0_ffn_w_up"][e], f32).T)
        m[f"wdT{e}"] = np.ascontiguousarray(np.asarray(inputs["L0_ffn_w_down"][e], f32).T)
    return m




def _sel8_host():
    m = np.zeros((8, 8 * 128), dtype=bfdt)
    for e in range(8):
        m[e, e * 128:(e + 1) * 128] = 1.0
    return m


def stage_fused(inputs, b):
    m = stage_l0(inputs, b)
    f32 = np.float32
    m["sel8c"] = _sel8_host()
    m["m1wqT"] = np.ascontiguousarray(np.asarray(inputs["L1_attn_w_q"], f32).T)
    m["m1wkT"] = np.ascontiguousarray(np.asarray(inputs["L1_attn_w_k"], f32).T)
    m["m1wvT"] = np.ascontiguousarray(np.asarray(inputs["L1_attn_w_v"], f32).T)
    m["m1woT"] = np.ascontiguousarray(np.asarray(inputs["L1_attn_w_o"], f32).T)
    m["m1w1T"] = np.ascontiguousarray(np.asarray(inputs["L1_ffn_w1"], f32).T)
    m["m1w2T"] = np.ascontiguousarray(np.asarray(inputs["L1_ffn_w2"], f32).T)
    m["m1w3T"] = np.ascontiguousarray(np.asarray(inputs["L1_ffn_w3"], f32).T)
    return m


def run_fused(inputs, cores=8, dev=False):
    nc = build_fused(dev=dev)
    in_maps = [stage_fused(inputs, b) for b in range(cores)]
    res = run_bass_kernel_spmd(nc, in_maps, core_ids=list(range(cores)))
    return res.results


def kernel(**inputs):
    n = 8
    res = run_fused(inputs, cores=n, dev=False)
    out = np.stack([np.asarray(res[b]["outT"], dtype=np.float32).T for b in range(n)], axis=0)
    return out


# version 03
# speedup vs baseline: 1.0297x; 1.0297x over previous
import numpy as np
import ml_dtypes
import concourse.bass as bass
import concourse.mybir as mybir
from concourse.bass_utils import run_bass_kernel_spmd
from concourse.alu_op_type import AluOpType
from contextlib import ExitStack

F32 = mybir.dt.float32
BF16 = mybir.dt.bfloat16
I32 = mybir.dt.int32
I8 = mybir.dt.int8
AF = mybir.ActivationFunctionType
AX = mybir.AxisListType
bfdt = ml_dtypes.bfloat16

T = 2048
D = 2048
NC = 16
NT4 = 4
NT16 = 16
HD = 128
NH = 16
KVH0 = 4
DFF1 = 8192
NFC = DFF1 // 128
NI = 1408
NIC = 11
NEXP = 8
NEG = -1e30


class Emit:

    def __init__(self):
        self.steps = []

    def _add(self, eng, kind, fn, inc):
        self.steps.append((eng, kind, fn, inc))

    def dma(self, dst, src):
        self._add("sync", "dma", lambda e: e.dma_start(dst, src), 16)

    def dmag(self, dst, src):
        self._add("gpsimd", "dma", lambda e: e.dma_start(dst, src), 16)

    def dmat(self, dst, src):
        self._add("sync", "dma", lambda e: e.dma_start(dst, src, transpose=True), 16)

    def pe(self, fn):
        self._add("tensor", "c", fn, 1)

    def dve(self, fn):
        self._add("vector", "c", fn, 1)

    def act(self, fn):
        self._add("scalar", "c", fn, 1)

    def gp(self, fn):
        self._add("gpsimd", "c", fn, 1)

    def finalize(self, nc, block, sem):
        steps = self.steps
        n = len(steps)
        cum = []
        c = 0
        for eng, kind, fn, inc in steps:
            cum.append(c)
            c += inc
        total = c
        waits = [0] * n
        for i in range(n):
            if i == 0:
                continue
            ej, kj, _, incj = steps[i - 1]
            if steps[i][0] == "tensor" and ej == "tensor" and kj == "c":
                waits[i] = waits[i - 1]
            else:
                waits[i] = cum[i]

        def emit(eng_obj, name):
            last_wait = 0
            for (eng, kind, fn, inc), w, cm in zip(steps, waits, cum):
                if eng != name:
                    continue
                if w > last_wait:
                    eng_obj.wait_ge(sem, w)
                    last_wait = w
                fn(eng_obj).then_inc(sem, inc)

        @block.sync
        def _(sync):
            emit(sync, "sync")
            sync.wait_ge(sem, total)

        @block.vector
        def _(vector):
            emit(vector, "vector")

        @block.scalar
        def _(scalar):
            emit(scalar, "scalar")

        @block.tensor
        def _(tensor):
            emit(tensor, "tensor")

        @block.gpsimd
        def _(gpsimd):
            emit(gpsimd, "gpsimd")



def _chunk_cols(vec):
    n = vec.shape[0] // 128
    return np.ascontiguousarray(vec.reshape(n, 128).T.astype(np.float32))

def _rope_tables():
    inv_freq = 1.0 / (10000.0 ** (np.arange(0, HD, 2, dtype=np.float64) / HD))
    t = np.arange(T, dtype=np.float64)
    fr = np.outer(t, inv_freq)
    cos = np.cos(fr).T
    sin = np.sin(fr).T
    cos2 = np.vstack([cos, cos]).astype(bfdt)
    sin2 = np.vstack([sin, -sin]).astype(bfdt)
    return np.ascontiguousarray(cos2), np.ascontiguousarray(sin2)

def _consts_host():
    mask = np.where(np.tril(np.ones((128, 128), dtype=bool)), 0.0, NEG).astype(np.float32)
    onesb = np.ones((128, 128), dtype=bfdt)
    onesf = np.ones((128, 1), dtype=np.float32)
    ident = np.eye(128, dtype=bfdt)
    onesrow_f = np.ones((1, 128), dtype=np.float32)
    return mask, onesb, onesf, ident, onesrow_f



def build_fused(dev=False):
    nc = bass.Bass("TRN2", target_bir_lowering=False)
    E = Emit()
    es = ExitStack()

    def din(name, shape, dt=F32):
        return nc.dram_tensor(name, shape, dt, kind="ExternalInput").ap()

    def dout(name, shape, dt=F32):
        return nc.dram_tensor(name, shape, dt, kind="ExternalOutput").ap()

    def dint(name, shape, dt=BF16):
        kind = "ExternalOutput" if dev else "Internal"
        return nc.dram_tensor(name, shape, dt, kind=kind).ap()

    xT = din("xT", [D, T])
    wqT = din("wqT", [D, D])
    wkT = din("wkT", [D, 512])
    wvT = din("wvT", [D, 512])
    woT = din("woT", [D, D])
    bcols = din("bcols", [128, 40])
    bvrow = din("bvrow", [1, 512])
    qg = din("qg", [1, 16])
    ropec = din("ropec", [128, T], BF16)
    ropes = din("ropes", [128, T], BF16)
    maskc = din("maskc", [128, 128])
    onesb_in = din("onesb", [128, 128], BF16)
    identb_in = din("identb", [128, 128], BF16)
    onesf_in = din("onesf", [128, 1])
    onesrowf_in = din("onesrowf", [1, 128])
    sel8_in = din("sel8c", [8, 8 * 128], BF16)
    routerT = din("routerT", [D, 8])
    sgT = din("sgT", [D, NI])
    suT = din("suT", [D, NI])
    sdT = din("sdT", [NI, D])
    wgTs = [din(f"wgT{e}", [D, NI]) for e in range(NEXP)]
    wuTs = [din(f"wuT{e}", [D, NI]) for e in range(NEXP)]
    wdTs = [din(f"wdT{e}", [NI, D]) for e in range(NEXP)]

    outT = dout("outT", [D, T])

    x1T = dint("x1T", [D, T], F32)
    xnT = dint("xnT", [D, T])
    qrawT = dint("qrawT", [D, T])
    QT = dint("QT", [D, T])
    KT = dint("KT", [512, T])
    Vtok = dint("Vtok", [T, 512])
    OT = dint("OT", [D, T])
    hshT = dint("hshT", [NI, T])
    heT = dint("heT", [NI, T])
    accT = dint("accTl0", [D, T], F32)

    sb = lambda name, shape, dt=BF16: es.enter_context(nc.sbuf_tensor(name, shape, dt))
    xns = sb("xns", [128, NC * T])
    wbufA = sb("wbufA", [128, 16 * 512])
    wbufB = sb("wbufB", [128, 16 * 512])
    wbufW = sb("wbufW", [128, 64 * 128])
    wbufC = wbufW
    sA = sb("sA", [128, 2048])
    sB = sb("sB", [128, 2048])
    fA = sb("fA", [128, 512], F32)
    fB = sb("fB", [128, 512], F32)
    ktile = sb("ktile", [128, 2048])
    vtile = sb("vtile", [128, 2048])
    ptile = sb("ptile", [128, 2048])
    sD = ptile
    pttile = sb("pttile", [128, 2048])
    cbs_full = pttile
    qtile = sb("qtile", [128, 128])
    rows = sb("rows", [1, 1536], F32)
    rowsb = sb("rowsb", [1, 2 * T])
    comb8 = sb("comb8", [8, T])
    sel8 = sb("sel8", [8, 8 * 128])
    combtok = sb("combtok", [128, 16 * 8], F32)
    mbs = sb("mbs", [128, 512])
    rbs = sb("rbs", [128, 512])
    cols = sb("cols", [128, 8], F32)
    bvb = sb("bvb", [128, 512])
    maskb = sb("maskb", [128, 128], F32)
    onesb = sb("onesbS", [128, 128])
    identb = sb("identbS", [128, 128])
    onesf = sb("onesfS", [128, 1], F32)
    onesrowf = sb("onesrowfS", [1, 128], F32)
    biasc = sb("biascS", [128, 40], F32)
    qgS = sb("qgS", [1, 16], F32)
    epsc = sb("epsc", [1, 2], F32)
    hbig = sb("hbig", [128, DFF1])
    h8 = sb("h8", [128, DFF1], I8)
    x8 = h8[:, 0:2048]
    fX = sb("fX", [128, 2048], F32)
    scol = sb("scol", [128, 256], F32)
    statc = sb("statc", [128, 16], F32)
    eps128 = sb("eps128S", [128, 1], F32)

    ps = lambda name, shape, dt=F32: es.enter_context(nc.psum_tensor(name, shape, dt))
    pS = ps("pS", [128, 2048])
    pA = ps("pA", [128, 512])
    pB = ps("pB", [128, 512])
    pT = ps("pT", [128, 512], BF16)
    pO = ps("pO", [128, 512])
    pR = pO[0:1, 0:512]

    sem = es.enter_context(nc.semaphore("gsem"))
    block = es.enter_context(nc.Block())

    E.dma(maskb[:], maskc)
    E.dma(onesb[:], onesb_in)
    E.dma(identb[:], identb_in)
    E.dma(onesf[:], onesf_in)
    E.dma(onesrowf[:], onesrowf_in)
    E.dma(biasc[:], bcols)
    E.dma(qgS[:], qg)
    E.gp(lambda e: e.memset(epsc[:, 0:1], 1e-5))
    E.gp(lambda e: e.memset(epsc[:, 1:2], 1.1920929e-07))
    E.gp(lambda e: e.memset(eps128[:], 1e-5))
    E.dma(sel8[:], sel8_in)
    ropec_s = sb("ropecS", [128, T])
    ropes_s = sb("ropesS", [128, T])
    E.dma(ropec_s[:], ropec)
    E.dma(ropes_s[:], ropes)
    E.dma(rows[:, 0:512], bvrow)
    E.dve(lambda e: e.tensor_copy(rowsb[:, 0:512], rows[:, 0:512]))
    E.pe(lambda e: e.matmul(pA[:], onesb[0:1, :], rowsb[0:1, 0:512], start=True, stop=True))
    E.dve(lambda e: e.tensor_copy(bvb[:], pA[:]))

    def wsl(wT_dram, oc, nch, width=128):
        return wT_dram.rearrange("(c p) o -> p c o", p=128)[:, :, oc * width:(oc + 1) * width]

    def bufv(buf, nch, width=128):
        return buf[:, 0:nch * width].rearrange("p (c o) -> p c o", o=width)

    ONE_COL = onesb[:, 0:1]
    ONE_ROW = onesb[0:1, :]

    def bcast_row_bf16(src_row_bf, psum, dst_bf, n):
        E.pe(lambda e: e.matmul(psum[:, 0:n], ONE_ROW, src_row_bf, start=True, stop=True))
        E.dve(lambda e: e.tensor_copy(dst_bf[:, 0:n], psum[:, 0:n]))

    for t4 in range(NT4):
        tsl = slice(t4 * 512, (t4 + 1) * 512)
        for c in range(NC):
            xv = xns[:, c * T + t4 * 512: c * T + (t4 + 1) * 512]
            E.dmag(xv, xT[c * 128:(c + 1) * 128, tsl])
        for c in range(NC):
            xv = xns[:, c * T + t4 * 512: c * T + (t4 + 1) * 512]
            E.pe(lambda e, xv=xv, c=c: e.matmul(pR, ONE_COL, xv, start=(c == 0), stop=(c == NC - 1)))
        E.dve(lambda e: e.tensor_scalar(rows[:, 0:512], pR, 1.0 / D, None, AluOpType.mult))
        for c in range(NC):
            xv = xns[:, c * T + t4 * 512: c * T + (t4 + 1) * 512]
            E.act(lambda e, xv=xv: e.activation(sA[:, 0:512], xv, AF.Square))
            E.pe(lambda e, c=c: e.matmul(pR, ONE_COL, sA[:, 0:512], start=(c == 0), stop=(c == NC - 1)))
        mrow = rows[:, 0:512]
        e2row = rows[:, 512:1024]
        E.dve(lambda e, e2row=e2row: e.tensor_scalar(e2row, pR, 1.0 / D, None, AluOpType.mult))
        msq = rows[:, 1024:1536]
        E.dve(lambda e, mrow=mrow, msq=msq: e.tensor_tensor(msq, mrow, mrow, AluOpType.mult))
        E.dve(lambda e, e2row=e2row, msq=msq: e.tensor_tensor(e2row, e2row, msq, AluOpType.subtract))
        E.act(lambda e, e2row=e2row: e.activation(e2row, e2row, AF.Ln, bias=epsc[:, 0:1]))
        E.act(lambda e, e2row=e2row: e.activation(e2row, e2row, AF.Exp, scale=-0.5))
        E.dve(lambda e, mrow=mrow, t4=t4: e.tensor_copy(rowsb[:, t4 * 512:(t4 + 1) * 512], mrow))
        E.dve(lambda e, e2row=e2row, t4=t4: e.tensor_copy(rowsb[:, T + t4 * 512: T + (t4 + 1) * 512], e2row))
    for t4 in range(NT4):
        bcast_row_bf16(rowsb[0:1, t4 * 512:(t4 + 1) * 512], pA, mbs, 512)
        bcast_row_bf16(rowsb[0:1, T + t4 * 512: T + (t4 + 1) * 512], pB, rbs, 512)
        for c in range(NC):
            xv = xns[:, c * T + t4 * 512: c * T + (t4 + 1) * 512]
            E.dve(lambda e, xv=xv: e.tensor_tensor(sA[:, 0:512], xv, mbs[:], AluOpType.subtract))
            E.dve(lambda e, xv=xv: e.tensor_tensor(xv, sA[:, 0:512], rbs[:], AluOpType.mult))
            if dev:
                E.dma(xnT[c * 128:(c + 1) * 128, t4 * 512:(t4 + 1) * 512], xv)

    def proj(dst_dram, wT_dram, out_chunks, bias_off=None, dst_scale=None):
        for oc in range(out_chunks):
            E.dmag(bufv(wbufA, NC), wsl(wT_dram, oc, NC))
            for t4 in range(NT4):
                for c in range(NC):
                    xv = xns[:, c * T + t4 * 512: c * T + (t4 + 1) * 512]
                    E.pe(lambda e, c=c, xv=xv: e.matmul(
                        pA[:], wbufA[:, c * 128:(c + 1) * 128], xv,
                        start=(c == 0), stop=(c == NC - 1)))
                if bias_off is not None:
                    bc = biasc[:, bias_off + oc: bias_off + oc + 1]
                    E.dve(lambda e, bc=bc: e.tensor_scalar(sA[:, 0:512], pA[:], bc, None, AluOpType.add))
                else:
                    E.dve(lambda e: e.tensor_copy(sA[:, 0:512], pA[:]))
                E.dma(dst_dram[oc * 128:(oc + 1) * 128, t4 * 512:(t4 + 1) * 512], sA[:, 0:512])

    proj(qrawT, wqT, 16, bias_off=0)

    kraw = dint("krawT", [512, T])
    proj(kraw, wkT, 4, bias_off=16)

    SCL = float(1.0 / np.sqrt(HD))

    def rms_rope(src_dram, dst_dram, n_heads, head_of, gain_fold):
        for h in range(n_heads):
            hc = head_of(h)
            E.dma(sA[:], src_dram[hc * 128:(hc + 1) * 128, :])
            for t4 in range(NT4):
                tsl = slice(t4 * 512, (t4 + 1) * 512)
                E.act(lambda e, tsl=tsl: e.activation(sB[:, 0:512], sA[:, tsl], AF.Square))
                E.pe(lambda e: e.matmul(pR, ONE_COL, sB[:, 0:512], start=True, stop=True))
                rr = rows[:, 0:512]
                E.act(lambda e, rr=rr: e.activation(rr, pR, AF.Ln, bias=epsc[:, 1:2], scale=1.0 / HD))
                E.act(lambda e, rr=rr: e.activation(rr, rr, AF.Exp, scale=-0.5))
                if gain_fold is not None:
                    gf = gain_fold(h)
                    E.dve(lambda e, rr=rr, gf=gf: e.tensor_scalar(rr, rr, gf, SCL, AluOpType.mult, AluOpType.mult))
                E.dve(lambda e, rr=rr, t4=t4: e.tensor_copy(rowsb[:, t4 * 512:(t4 + 1) * 512], rr))
            E.dma(sD[0:64, :], sA[64:128, :])
            E.dma(sD[64:128, :], sA[0:64, :])
            E.dve(lambda e: e.tensor_tensor(sB[:], sA[:], ropec_s[:], AluOpType.mult))
            E.dve(lambda e: e.tensor_tensor(sD[:], sD[:], ropes_s[:], AluOpType.mult))
            E.dve(lambda e: e.tensor_tensor(sB[:], sB[:], sD[:], AluOpType.add))
            for t4 in range(NT4):
                tsl = slice(t4 * 512, (t4 + 1) * 512)
                bcast_row_bf16(rowsb[0:1, t4 * 512:(t4 + 1) * 512], pA, rbs, 512)
                E.dve(lambda e, tsl=tsl: e.tensor_tensor(sB[:, tsl], sB[:, tsl], rbs[:], AluOpType.mult))
                E.dma(dst_dram[hc * 128:(hc + 1) * 128, tsl], sB[:, tsl])

    rms_rope(qrawT, QT, 16, lambda h: h,
             gain_fold=lambda h: qgS[0:1, h:h + 1])
    rms_rope(kraw, KT, 4, lambda h: h, gain_fold=None)

    E.dmag(bufv(wbufC, NC, 512), wsl(wvT, 0, NC, 512))
    for t16 in range(NT16):
        for c in range(NC):
            lh = xns[:, c * T + t16 * 128: c * T + (t16 + 1) * 128]
            rh = wbufC[:, c * 512:(c + 1) * 512]
            E.pe(lambda e, lh=lh, rh=rh, c=c: e.matmul(pA[:], lh, rh, start=(c == 0), stop=(c == NC - 1)))
        E.dve(lambda e: e.tensor_copy(sA[:, 0:512], pA[:]))
        E.dve(lambda e: e.tensor_tensor(sA[:, 0:512], sA[:, 0:512], bvb[:], AluOpType.add))
        E.dma(Vtok[t16 * 128:(t16 + 1) * 128, :], sA[:, 0:512])

    E.dmag(bufv(wbufA, NC, 8), wsl(routerT, 0, NC, 8))
    for t16 in range(NT16):
        for c in range(NC):
            lh = xns[:, c * T + t16 * 128: c * T + (t16 + 1) * 128]
            E.pe(lambda e, lh=lh, c=c: e.matmul(pA[:, 0:8], lh, wbufA[:, c * 8:(c + 1) * 8],
                                                start=(c == 0), stop=(c == NC - 1)))
        E.dve(lambda e: e.tensor_reduce(cols[:, 0:1], pA[:, 0:8], AX.X, AluOpType.max))
        E.dve(lambda e: e.tensor_scalar(cols[:, 1:2], cols[:, 0:1], -1.0, None, AluOpType.mult))
        def _exp_router(e):
            e.activation(fA[:, 0:8], pA[:, 0:8], AF.Exp, bias=cols[:, 1:2], accum_out=cols[:, 2:3])
            return e.copy(cols[:, 6:7], cols[:, 2:3])
        E.act(_exp_router)
        E.dve(lambda e: e.reciprocal(cols[:, 3:4], cols[:, 2:3]))
        E.dve(lambda e: e.tensor_scalar(fA[:, 0:8], fA[:, 0:8], cols[:, 3:4], None, AluOpType.mult))
        E.dve(lambda e: e.tensor_reduce(cols[:, 4:5], fA[:, 0:8], AX.X, AluOpType.max))
        E.dve(lambda e: e.tensor_scalar(fB[:, 0:8], fA[:, 0:8], cols[:, 4:5], None, AluOpType.is_ge))
        E.dve(lambda e: e.scalar_tensor_tensor(fB[:, 0:8], fB[:, 0:8], NEG, fA[:, 0:8], AluOpType.mult, AluOpType.add))
        E.dve(lambda e: e.tensor_reduce(cols[:, 5:6], fB[:, 0:8], AX.X, AluOpType.max))
        E.dve(lambda e: e.tensor_scalar(fB[:, 0:8], fA[:, 0:8], cols[:, 5:6], None, AluOpType.is_ge))
        E.dve(lambda e, t16=t16: e.tensor_tensor(combtok[:, t16 * 8:(t16 + 1) * 8], fA[:, 0:8], fB[:, 0:8], AluOpType.mult))
    for t16 in range(NT16):
        E.dve(lambda e, t16=t16: e.tensor_copy(sA[:, 0:8], combtok[:, t16 * 8:(t16 + 1) * 8]))
        E.pe(lambda e: e.transpose(pT[0:8, 0:128], sA[:, 0:8], identb[:]))
        E.dve(lambda e, t16=t16: e.tensor_copy(comb8[0:8, t16 * 128:(t16 + 1) * 128], pT[0:8, 0:128]))

    def gated_ffn(gT, uT, dst_dram, comb_e=None):
        if comb_e is not None:
            for t4 in range(NT4):
                E.pe(lambda e, t4=t4, comb_e=comb_e: e.matmul(
                    pO[:], sel8[:, comb_e * 128:(comb_e + 1) * 128],
                    comb8[0:8, t4 * 512:(t4 + 1) * 512], start=True, stop=True))
                E.dve(lambda e, t4=t4: e.tensor_copy(cbs_full[:, t4 * 512:(t4 + 1) * 512], pO[:]))
        for oc in range(NIC):
            E.dmag(bufv(wbufA, NC), wsl(gT, oc, NC))
            E.dmag(bufv(wbufB, NC), wsl(uT, oc, NC))
            for t4 in range(NT4):
                for c in range(NC):
                    xv = xns[:, c * T + t4 * 512: c * T + (t4 + 1) * 512]
                    E.pe(lambda e, c=c, xv=xv: e.matmul(pA[:], wbufA[:, c * 128:(c + 1) * 128], xv,
                                                        start=(c == 0), stop=(c == NC - 1)))
                for c in range(NC):
                    xv = xns[:, c * T + t4 * 512: c * T + (t4 + 1) * 512]
                    E.pe(lambda e, c=c, xv=xv: e.matmul(pB[:], wbufB[:, c * 128:(c + 1) * 128], xv,
                                                        start=(c == 0), stop=(c == NC - 1)))
                E.act(lambda e: e.activation(sA[:, 0:512], pA[:], AF.Sigmoid))
                E.dve(lambda e: e.tensor_tensor(sB[:, 0:512], sA[:, 0:512], pB[:], AluOpType.mult))
                if comb_e is not None:
                    E.dve(lambda e, t4=t4: e.tensor_tensor(sB[:, 0:512], sB[:, 0:512],
                                                           cbs_full[:, t4 * 512:(t4 + 1) * 512], AluOpType.mult))
                E.dma(dst_dram[oc * 128:(oc + 1) * 128, t4 * 512:(t4 + 1) * 512], sB[:, 0:512])

    def down_proj(dT, src_dram, first):
        for oc in range(NC):
            E.dmag(bufv(wbufA, NIC), wsl(dT, oc, NIC))
            for t4 in range(NT4):
                tsl = slice(t4 * 512, (t4 + 1) * 512)
                for ic in range(NIC):
                    E.dma(sA[:, 0:512], src_dram[ic * 128:(ic + 1) * 128, tsl])
                    E.pe(lambda e, ic=ic: e.matmul(pA[:], wbufA[:, ic * 128:(ic + 1) * 128], sA[:, 0:512],
                                                   start=(ic == 0), stop=(ic == NIC - 1)))
                if first:
                    E.dma(fA[:], xT[oc * 128:(oc + 1) * 128, tsl])
                    xnv = xns[:, oc * T + t4 * 512: oc * T + (t4 + 1) * 512]
                    E.dve(lambda e, xnv=xnv: e.tensor_tensor(fA[:], fA[:], xnv, AluOpType.add))
                else:
                    E.dma(fA[:], accT[oc * 128:(oc + 1) * 128, tsl])
                E.dve(lambda e: e.tensor_tensor(fA[:], fA[:], pA[:], AluOpType.add))
                E.dma(accT[oc * 128:(oc + 1) * 128, tsl], fA[:])

    gated_ffn(sgT, suT, hshT, comb_e=None)
    down_proj(sdT, hshT, first=True)
    for e_i in range(NEXP):
        gated_ffn(wgTs[e_i], wuTs[e_i], heT, comb_e=e_i)
        down_proj(wdTs[e_i], heT, first=False)

    for qt in range(NT16):
        nk = (qt + 1) * 128
        nk4 = (nk + 511) // 512
        for kv in range(KVH0):
            E.dma(ktile[:, 0:nk], KT[kv * 128:(kv + 1) * 128, 0:nk])
            E.dma(vtile[:, 0:nk].rearrange("p (b d) -> p b d", d=128),
                  Vtok[0:nk, kv * 128:(kv + 1) * 128].rearrange("(b p) d -> p b d", p=128))
            for hq in range(4):
                h = kv * 4 + hq
                E.dma(qtile[:], QT[h * 128:(h + 1) * 128, qt * 128:(qt + 1) * 128])
                for kb4 in range(nk4):
                    w = min(512, nk - kb4 * 512)
                    E.pe(lambda e, kb4=kb4, w=w: e.matmul(
                        pS[:, kb4 * 512: kb4 * 512 + w], qtile[:], ktile[:, kb4 * 512: kb4 * 512 + w],
                        start=True, stop=True))
                E.dve(lambda e, qt=qt: e.tensor_tensor(pS[:, qt * 128:(qt + 1) * 128],
                                                       pS[:, qt * 128:(qt + 1) * 128], maskb[:], AluOpType.add))
                E.dve(lambda e, nk=nk: e.tensor_reduce(cols[:, 0:1], pS[:, 0:nk], AX.X, AluOpType.max))
                E.dve(lambda e: e.tensor_scalar(cols[:, 1:2], cols[:, 0:1], -1.0, None, AluOpType.mult))
                def _exp_attn(e, nk=nk):
                    e.activation(ptile[:, 0:nk], pS[:, 0:nk], AF.Exp,
                                 bias=cols[:, 1:2], accum_out=cols[:, 2:3])
                    return e.copy(cols[:, 6:7], cols[:, 2:3])
                E.act(_exp_attn)
                E.dve(lambda e: e.reciprocal(cols[:, 3:4], cols[:, 2:3]))
                E.dve(lambda e, nk=nk: e.tensor_scalar(ptile[:, 0:nk], ptile[:, 0:nk], cols[:, 3:4], None, AluOpType.mult))
                for kb in range(qt + 1):
                    sl4 = (kb % 4) * 128
                    E.pe(lambda e, kb=kb, sl4=sl4: e.transpose(
                        pT[:, sl4:sl4 + 128], ptile[:, kb * 128:(kb + 1) * 128], identb[:]))
                    if kb % 4 == 3 or kb == qt:
                        base = (kb // 4) * 512
                        wdt = (kb % 4 + 1) * 128
                        E.dve(lambda e, base=base, wdt=wdt: e.tensor_copy(
                            pttile[:, base:base + wdt], pT[:, 0:wdt]))
                for kb in range(qt + 1):
                    E.pe(lambda e, kb=kb, qt=qt: e.matmul(
                        pO[:, 0:128], vtile[:, kb * 128:(kb + 1) * 128], pttile[:, kb * 128:(kb + 1) * 128],
                        start=(kb == 0), stop=(kb == qt)))
                E.dve(lambda e: e.tensor_copy(sA[:, 0:128], pO[:, 0:128]))
                E.dma(OT[h * 128:(h + 1) * 128, qt * 128:(qt + 1) * 128], sA[:, 0:128])

    for oc in range(NC):
        E.dmag(bufv(wbufA, NC), wsl(woT, oc, NC))
        for t4 in range(NT4):
            tsl = slice(t4 * 512, (t4 + 1) * 512)
            for c in range(NC):
                E.dma(sA[:, 0:512], OT[c * 128:(c + 1) * 128, tsl])
                E.pe(lambda e, c=c: e.matmul(pA[:], wbufA[:, c * 128:(c + 1) * 128], sA[:, 0:512],
                                             start=(c == 0), stop=(c == NC - 1)))
            E.dma(fA[:], accT[oc * 128:(oc + 1) * 128, tsl])
            bc = biasc[:, 24 + oc: 24 + oc + 1]
            E.dve(lambda e, bc=bc: e.scalar_tensor_tensor(fB[:], pA[:], bc, fA[:], AluOpType.add, AluOpType.add))
            E.dma(x1T[oc * 128:(oc + 1) * 128, tsl], fB[:])


    m1wqT = din("m1wqT", [D, D])
    m1wkT = din("m1wkT", [D, D])
    m1wvT = din("m1wvT", [D, D])
    m1woT = din("m1woT", [D, D])
    m1w1T = din("m1w1T", [D, DFF1])
    m1w2T = din("m1w2T", [D, DFF1])
    m1w3T = din("m1w3T", [DFF1, D])

    xqAtok = dint("xqAtok", [T, D])
    xqFtok = dint("xqFtok", [T, D])
    xqAT = dint("xqAT", [D, T])
    xqFT = dint("xqFT", [D, T])
    wq_q = dint("wq_q", [D, D])
    wk_q = dint("wk_q", [D, D])
    wv_q = dint("wv_q", [D, D])
    wo_q = dint("wo_q", [D, D])
    w1_q = dint("w1_q", [D, DFF1])
    w2_q = dint("w2_q", [D, DFF1])
    w3_q = dint("w3_q", [DFF1, D])
    QT1 = dint("QT1", [D, T])
    KT1 = dint("KT1", [D, T])
    Vtok1 = dint("Vtok1", [T, D])
    OT1 = dint("OT1", [D, T])
    Oqtok = dint("Oqtok", [T, D])
    OqT = dint("OqT", [D, T])
    htok = dint("htok", [T, DFF1])
    hqT = dint("hqT", [DFF1, T])
    accT1 = dint("accTl1", [D, T], F32)
    p11 = pO[0:1, 0:1]

    for t4 in range(NT4):
        tsl = slice(t4 * 512, (t4 + 1) * 512)
        for c in range(NC):
            xv = xns[:, c * T + t4 * 512: c * T + (t4 + 1) * 512]
            E.dmag(xv, x1T[c * 128:(c + 1) * 128, tsl])
        for c in range(NC):
            xv = xns[:, c * T + t4 * 512: c * T + (t4 + 1) * 512]
            E.pe(lambda e, xv=xv, c=c: e.matmul(pR, ONE_COL, xv, start=(c == 0), stop=(c == NC - 1)))
        E.dve(lambda e: e.tensor_scalar(rows[:, 0:512], pR, 1.0 / D, None, AluOpType.mult))
        for c in range(NC):
            xv = xns[:, c * T + t4 * 512: c * T + (t4 + 1) * 512]
            E.act(lambda e, xv=xv: e.activation(sA[:, 0:512], xv, AF.Square))
            E.pe(lambda e, c=c: e.matmul(pR, ONE_COL, sA[:, 0:512], start=(c == 0), stop=(c == NC - 1)))
        mrow = rows[:, 0:512]
        e2row = rows[:, 512:1024]
        msq = rows[:, 1024:1536]
        E.dve(lambda e, e2row=e2row: e.tensor_scalar(e2row, pR, 1.0 / D, None, AluOpType.mult))
        E.dve(lambda e, mrow=mrow, msq=msq: e.tensor_tensor(msq, mrow, mrow, AluOpType.mult))
        E.dve(lambda e, e2row=e2row, msq=msq: e.tensor_tensor(e2row, e2row, msq, AluOpType.subtract))
        E.act(lambda e, e2row=e2row: e.activation(e2row, e2row, AF.Ln, bias=epsc[:, 0:1]))
        E.act(lambda e, e2row=e2row: e.activation(e2row, e2row, AF.Exp, scale=-0.5))
        E.dve(lambda e, mrow=mrow, t4=t4: e.tensor_copy(rowsb[:, t4 * 512:(t4 + 1) * 512], mrow))
        E.dve(lambda e, e2row=e2row, t4=t4: e.tensor_copy(rowsb[:, T + t4 * 512: T + (t4 + 1) * 512], e2row))
    for t4 in range(NT4):
        bcast_row_bf16(rowsb[0:1, t4 * 512:(t4 + 1) * 512], pA, mbs, 512)
        bcast_row_bf16(rowsb[0:1, T + t4 * 512: T + (t4 + 1) * 512], pB, rbs, 512)
        for c in range(NC):
            xv = xns[:, c * T + t4 * 512: c * T + (t4 + 1) * 512]
            E.dve(lambda e, xv=xv: e.tensor_tensor(sA[:, 0:512], xv, mbs[:], AluOpType.subtract))
            E.dve(lambda e, xv=xv: e.tensor_tensor(xv, sA[:, 0:512], rbs[:], AluOpType.mult))

    f3 = lambda t: t.rearrange("p (g i) -> p g i", i=64)
    for t16 in range(NT16):
        rsl = slice(t16 * 128, (t16 + 1) * 128)
        for c in range(NC):
            sl4 = (c % 4) * 128
            E.pe(lambda e, c=c, t16=t16, sl4=sl4: e.transpose(
                pT[:, sl4:sl4 + 128], xns[:, c * T + t16 * 128: c * T + (t16 + 1) * 128], identb[:]))
            if c % 4 == 3:
                base = (c // 4) * 512
                E.dve(lambda e, base=base: e.tensor_copy(sA[:, base:base + 512], pT[:, 0:512]))
        E.dve(lambda e: e.tensor_reduce(cols[:, 3:4], sA[:], AX.X, AluOpType.max, apply_absolute_value=True))
        E.dve(lambda e: e.tensor_scalar(cols[:, 3:4], cols[:, 3:4], 1e-5, None, AluOpType.max))
        E.dve(lambda e: e.reciprocal(cols[:, 4:5], cols[:, 3:4]))
        E.dve(lambda e: e.tensor_scalar(x8, sA[:], cols[:, 4:5], 127.0, AluOpType.mult, AluOpType.mult))
        E.dve(lambda e: e.tensor_scalar(sB[:], x8, cols[:, 3:4], 1.0 / 127.0, AluOpType.mult, AluOpType.mult))
        E.dma(xqAtok[rsl, :], sB[:])
        E.dve(lambda e: e.tensor_reduce(scol[:, 0:32], f3(sA[:]), AX.X, AluOpType.max, apply_absolute_value=True))
        E.dve(lambda e: e.tensor_scalar(scol[:, 0:32], scol[:, 0:32], 1e-5, None, AluOpType.max))
        E.dve(lambda e: e.reciprocal(scol[:, 32:64], scol[:, 0:32]))
        E.dve(lambda e: e.tensor_scalar(scol[:, 32:64], scol[:, 32:64], 127.0, None, AluOpType.mult))
        E.dve(lambda e: e.tensor_tensor(f3(x8), f3(sA[:]),
                                        scol[:, 32:64].unsqueeze(2).broadcast_to([128, 32, 64]), AluOpType.mult))
        E.dve(lambda e: e.tensor_scalar(scol[:, 64:96], scol[:, 0:32], 1.0 / 127.0, None, AluOpType.mult))
        E.dve(lambda e: e.tensor_tensor(f3(sB[:]), f3(x8),
                                        scol[:, 64:96].unsqueeze(2).broadcast_to([128, 32, 64]), AluOpType.mult))
        E.dma(xqFtok[rsl, :], sB[:])

    def transpose_tok_to_fm(src_tok, dst_T):
        for t16 in range(NT16):
            E.dma(sA[:], src_tok[t16 * 128:(t16 + 1) * 128, :])
            for c in range(NC):
                sl4 = (c % 4) * 128
                E.pe(lambda e, c=c, sl4=sl4: e.transpose(pT[:, sl4:sl4 + 128], sA[:, c * 128:(c + 1) * 128], identb[:]))
                if c % 4 == 3:
                    base = (c // 4) * 512
                    E.dve(lambda e, base=base: e.tensor_copy(sB[:, base:base + 512], pT[:, 0:512]))
            for c in range(NC):
                E.dma(dst_T[c * 128:(c + 1) * 128, t16 * 128:(t16 + 1) * 128], sB[:, c * 128:(c + 1) * 128])

    transpose_tok_to_fm(xqAtok, xqAT)
    transpose_tok_to_fm(xqFtok, xqFT)

    def weight_stats2(i, wT_dram, numel, clip_eps):
        rows_n = wT_dram.shape[0]
        wT3 = wT_dram.rearrange("(c p) o -> p c o", p=128)
        nco = wT_dram.shape[1] // 2048
        E.gp(lambda e: e.memset(scol[:, 96:97], 0.0))
        for c in range(rows_n // 128):
            for ob in range(nco):
                E.dma(fX[:], wT3[:, c, ob * 2048:(ob + 1) * 2048])
                def _abs(e):
                    e.activation(fX[:], fX[:], AF.Abs, accum_out=cols[:, 7:8])
                    return e.copy(cols[:, 6:7], cols[:, 7:8])
                E.act(_abs)
                E.dve(lambda e: e.tensor_tensor(scol[:, 96:97], scol[:, 96:97], cols[:, 7:8], AluOpType.add))
        E.pe(lambda e: e.matmul(p11, scol[:, 96:97], onesf[:], start=True, stop=True))
        E.dve(lambda e, i=i, numel=numel: e.tensor_scalar(statc[0:1, 8 + i:9 + i], p11, 1.0 / numel, None, AluOpType.mult))
        if clip_eps is not None:
            E.dve(lambda e, i=i, clip_eps=clip_eps: e.tensor_scalar(
                statc[0:1, 8 + i:9 + i], statc[0:1, 8 + i:9 + i], clip_eps, None, AluOpType.max))
        E.pe(lambda e, i=i: e.matmul(pO[:, 0:1], onesrowf[0:1, :], statc[0:1, 8 + i:9 + i], start=True, stop=True))
        E.dve(lambda e, i=i: e.tensor_copy(statc[:, i:i + 1], pO[:, 0:1]))
        E.dve(lambda e, i=i: e.reciprocal(statc[:, 8 + i:9 + i], statc[:, i:i + 1]))

    WSTATS = [(m1wqT, D * D), (m1wkT, D * D), (m1wvT, D * D), (m1woT, D * D),
              (m1w1T, D * DFF1), (m1w2T, D * DFF1), (m1w3T, DFF1 * D)]
    for i, (wt, ne) in enumerate(WSTATS):
        weight_stats2(i, wt, ne, 1e-5 if i < 4 else None)

    def quant_weight_ternary(i, wT_dram, dst):
        rows_n = wT_dram.shape[0]
        wT3 = wT_dram.rearrange("(c p) o -> p c o", p=128)
        dst3 = dst.rearrange("(c p) o -> p c o", p=128)
        nco = wT_dram.shape[1] // 2048
        for c in range(rows_n // 128):
            for ob in range(nco):
                E.dma(fX[:], wT3[:, c, ob * 2048:(ob + 1) * 2048])
                E.dve(lambda e, i=i: e.tensor_scalar(fX[:], fX[:], statc[:, 8 + i:9 + i], 1.0, AluOpType.mult, AluOpType.min))
                E.dve(lambda e: e.tensor_scalar(x8, fX[:], -1.0, None, AluOpType.max))
                E.dve(lambda e: e.tensor_copy(sA[:], x8))
                E.dma(dst3[:, c, ob * 2048:(ob + 1) * 2048], sA[:])

    def quant_weight_sign(wT_dram, dst):
        rows_n = wT_dram.shape[0]
        wT3 = wT_dram.rearrange("(c p) o -> p c o", p=128)
        dst3 = dst.rearrange("(c p) o -> p c o", p=128)
        nco = wT_dram.shape[1] // 2048
        for c in range(rows_n // 128):
            for ob in range(nco):
                E.dma(fX[:], wT3[:, c, ob * 2048:(ob + 1) * 2048])
                E.act(lambda e: e.activation(sA[:], fX[:], AF.Sign))
                E.dma(dst3[:, c, ob * 2048:(ob + 1) * 2048], sA[:])

    quant_weight_ternary(0, m1wqT, wq_q)
    quant_weight_ternary(1, m1wkT, wk_q)
    quant_weight_ternary(2, m1wvT, wv_q)
    quant_weight_ternary(3, m1woT, wo_q)
    quant_weight_sign(m1w1T, w1_q)
    quant_weight_sign(m1w2T, w2_q)
    quant_weight_sign(m1w3T, w3_q)

    for c in range(NC):
        E.dma(xns[:, c * T:(c + 1) * T], xqAT[c * 128:(c + 1) * 128, :])

    def proj_l1(dst_dram, wq_dram, dq_i, extra_scale):
        for oc in range(NC):
            E.dma(bufv(wbufA, NC), wsl(wq_dram, oc, NC))
            for t4 in range(NT4):
                for c in range(NC):
                    xv = xns[:, c * T + t4 * 512: c * T + (t4 + 1) * 512]
                    E.pe(lambda e, c=c, xv=xv: e.matmul(pA[:], wbufA[:, c * 128:(c + 1) * 128], xv,
                                                        start=(c == 0), stop=(c == NC - 1)))
                E.dve(lambda e, dq_i=dq_i, extra_scale=extra_scale: e.tensor_scalar(
                    sA[:, 0:512], pA[:], statc[:, dq_i:dq_i + 1], extra_scale, AluOpType.mult, AluOpType.mult))
                E.dma(dst_dram[oc * 128:(oc + 1) * 128, t4 * 512:(t4 + 1) * 512], sA[:, 0:512])

    proj_l1(QT1, wq_q, 0, SCL)
    proj_l1(KT1, wk_q, 1, 1.0)

    for vb in range(4):
        E.dma(bufv(wbufA, NC, 512), wsl(wv_q, vb, NC, 512))
        for t16 in range(NT16):
            for c in range(NC):
                lh = xns[:, c * T + t16 * 128: c * T + (t16 + 1) * 128]
                rh = wbufA[:, c * 512:(c + 1) * 512]
                E.pe(lambda e, lh=lh, rh=rh, c=c: e.matmul(pA[:], lh, rh, start=(c == 0), stop=(c == NC - 1)))
            E.dve(lambda e: e.tensor_scalar(sA[:, 0:512], pA[:], statc[:, 2:3], None, AluOpType.mult))
            E.dma(Vtok1[t16 * 128:(t16 + 1) * 128, vb * 512:(vb + 1) * 512], sA[:, 0:512])

    for c in range(NC):
        E.dma(xns[:, c * T:(c + 1) * T], xqFT[c * 128:(c + 1) * 128, :])

    for ffb in range(16):
        E.dma(bufv(wbufA, NC, 512), wsl(w1_q, ffb, NC, 512))
        E.dma(bufv(wbufB, NC, 512), wsl(w2_q, ffb, NC, 512))
        for t16 in range(NT16):
            for c in range(NC):
                lh = xns[:, c * T + t16 * 128: c * T + (t16 + 1) * 128]
                E.pe(lambda e, lh=lh, c=c: e.matmul(pA[:], lh, wbufA[:, c * 512:(c + 1) * 512],
                                                    start=(c == 0), stop=(c == NC - 1)))
            for c in range(NC):
                lh = xns[:, c * T + t16 * 128: c * T + (t16 + 1) * 128]
                E.pe(lambda e, lh=lh, c=c: e.matmul(pB[:], lh, wbufB[:, c * 512:(c + 1) * 512],
                                                    start=(c == 0), stop=(c == NC - 1)))
            E.act(lambda e: e.activation(sA[:, 0:512], pA[:], AF.Silu, scale=statc[:, 4:5]))
            E.dve(lambda e: e.tensor_scalar(sB[:, 0:512], pB[:], statc[:, 5:6], None, AluOpType.mult))
            E.dve(lambda e: e.tensor_tensor(sA[:, 0:512], sA[:, 0:512], sB[:, 0:512], AluOpType.mult))
            E.dma(htok[t16 * 128:(t16 + 1) * 128, ffb * 512:(ffb + 1) * 512], sA[:, 0:512])

    f3h = lambda t: t.rearrange("p (g i) -> p g i", i=64)
    for t16 in range(NT16):
        rsl = slice(t16 * 128, (t16 + 1) * 128)
        E.dma(hbig[:], htok[rsl, :])
        E.dve(lambda e: e.tensor_reduce(scol[:, 0:128], f3h(hbig[:]), AX.X, AluOpType.max, apply_absolute_value=True))
        E.dve(lambda e: e.tensor_scalar(scol[:, 0:128], scol[:, 0:128], 1e-5, None, AluOpType.max))
        E.dve(lambda e: e.reciprocal(scol[:, 128:256], scol[:, 0:128]))
        E.dve(lambda e: e.tensor_scalar(scol[:, 128:256], scol[:, 128:256], 127.0, None, AluOpType.mult))
        E.dve(lambda e: e.tensor_tensor(f3h(h8[:]), f3h(hbig[:]),
                                        scol[:, 128:256].unsqueeze(2).broadcast_to([128, 128, 64]), AluOpType.mult))
        E.dve(lambda e: e.tensor_scalar(scol[:, 128:256], scol[:, 0:128], 1.0 / 127.0, None, AluOpType.mult))
        E.dve(lambda e: e.tensor_tensor(f3h(hbig[:]), f3h(h8[:]),
                                        scol[:, 128:256].unsqueeze(2).broadcast_to([128, 128, 64]), AluOpType.mult))
        for fc in range(NFC):
            sl4 = (fc % 4) * 128
            E.pe(lambda e, fc=fc, sl4=sl4: e.transpose(pT[:, sl4:sl4 + 128], hbig[:, fc * 128:(fc + 1) * 128], identb[:]))
            if fc % 4 == 3:
                base4 = fc - 3
                E.dve(lambda e: e.tensor_copy(sB[:, 0:512], pT[:, 0:512]))
                for j in range(4):
                    E.dma(hqT[(base4 + j) * 128:(base4 + j + 1) * 128, rsl], sB[:, j * 128:(j + 1) * 128])

    for t4 in range(NT4):
        tsl = slice(t4 * 512, (t4 + 1) * 512)
        for fc in range(NFC):
            E.dma(xns[:, fc * 512:(fc + 1) * 512], hqT[fc * 128:(fc + 1) * 128, tsl])
        for oc in range(NC):
            E.dma(bufv(wbufW, NFC), wsl(m1w3T, oc, NFC) if False else wsl(w3_q, oc, NFC))
            for fc in range(NFC):
                E.pe(lambda e, fc=fc: e.matmul(pA[:], wbufW[:, fc * 128:(fc + 1) * 128],
                                               xns[:, fc * 512:(fc + 1) * 512],
                                               start=(fc == 0), stop=(fc == NFC - 1)))
            E.dma(fA[:], x1T[oc * 128:(oc + 1) * 128, tsl])
            E.dve(lambda e: e.tensor_scalar(fB[:], pA[:], statc[:, 6:7], None, AluOpType.mult))
            E.dve(lambda e: e.tensor_tensor(fB[:], fB[:], fA[:], AluOpType.add))
            E.dma(accT1[oc * 128:(oc + 1) * 128, tsl], fB[:])

    for qt in range(NT16):
        nk = (qt + 1) * 128
        nk4 = (nk + 511) // 512
        for h in range(NH):
            E.dma(ktile[:, 0:nk], KT1[h * 128:(h + 1) * 128, 0:nk])
            E.dma(vtile[:, 0:nk].rearrange("p (b d) -> p b d", d=128),
                  Vtok1[0:nk, h * 128:(h + 1) * 128].rearrange("(b p) d -> p b d", p=128))
            E.dma(qtile[:], QT1[h * 128:(h + 1) * 128, qt * 128:(qt + 1) * 128])
            for kb4 in range(nk4):
                w = min(512, nk - kb4 * 512)
                E.pe(lambda e, kb4=kb4, w=w: e.matmul(
                    pS[:, kb4 * 512: kb4 * 512 + w], qtile[:], ktile[:, kb4 * 512: kb4 * 512 + w],
                    start=True, stop=True))
            E.dve(lambda e, qt=qt: e.tensor_tensor(pS[:, qt * 128:(qt + 1) * 128],
                                                   pS[:, qt * 128:(qt + 1) * 128], maskb[:], AluOpType.add))
            E.dve(lambda e, nk=nk: e.tensor_reduce(cols[:, 0:1], pS[:, 0:nk], AX.X, AluOpType.max))
            E.dve(lambda e: e.tensor_scalar(cols[:, 1:2], cols[:, 0:1], -1.0, None, AluOpType.mult))
            def _exp_attn1(e, nk=nk):
                e.activation(ptile[:, 0:nk], pS[:, 0:nk], AF.Exp,
                             bias=cols[:, 1:2], accum_out=cols[:, 2:3])
                return e.copy(cols[:, 6:7], cols[:, 2:3])
            E.act(_exp_attn1)
            E.dve(lambda e: e.reciprocal(cols[:, 3:4], cols[:, 2:3]))
            E.dve(lambda e, nk=nk: e.tensor_scalar(ptile[:, 0:nk], ptile[:, 0:nk], cols[:, 3:4], None, AluOpType.mult))
            for kb in range(qt + 1):
                sl4 = (kb % 4) * 128
                E.pe(lambda e, kb=kb, sl4=sl4: e.transpose(
                    pT[:, sl4:sl4 + 128], ptile[:, kb * 128:(kb + 1) * 128], identb[:]))
                if kb % 4 == 3 or kb == qt:
                    base = (kb // 4) * 512
                    wdt = (kb % 4 + 1) * 128
                    E.dve(lambda e, base=base, wdt=wdt: e.tensor_copy(
                        pttile[:, base:base + wdt], pT[:, 0:wdt]))
            for kb in range(qt + 1):
                E.pe(lambda e, kb=kb, qt=qt: e.matmul(
                    pO[:, 0:128], vtile[:, kb * 128:(kb + 1) * 128], pttile[:, kb * 128:(kb + 1) * 128],
                    start=(kb == 0), stop=(kb == qt)))
            E.dve(lambda e: e.tensor_copy(sA[:, 0:128], pO[:, 0:128]))
            E.dma(OT1[h * 128:(h + 1) * 128, qt * 128:(qt + 1) * 128], sA[:, 0:128])

    for t16 in range(NT16):
        rsl = slice(t16 * 128, (t16 + 1) * 128)
        for c in range(NC):
            E.dma(sA[:, c * 128:(c + 1) * 128], OT1[c * 128:(c + 1) * 128, rsl])
        for c in range(NC):
            sl4 = (c % 4) * 128
            E.pe(lambda e, c=c, sl4=sl4: e.transpose(pT[:, sl4:sl4 + 128], sA[:, c * 128:(c + 1) * 128], identb[:]))
            if c % 4 == 3:
                base = (c // 4) * 512
                E.dve(lambda e, base=base: e.tensor_copy(sB[:, base:base + 512], pT[:, 0:512]))
        E.dve(lambda e: e.tensor_reduce(cols[:, 3:4], sB[:], AX.X, AluOpType.max, apply_absolute_value=True))
        E.dve(lambda e: e.tensor_scalar(cols[:, 3:4], cols[:, 3:4], 1e-5, None, AluOpType.max))
        E.dve(lambda e: e.reciprocal(cols[:, 4:5], cols[:, 3:4]))
        E.dve(lambda e: e.tensor_scalar(x8, sB[:], cols[:, 4:5], 127.0, AluOpType.mult, AluOpType.mult))
        E.dve(lambda e: e.tensor_scalar(sB[:], x8, cols[:, 3:4], 1.0 / 127.0, AluOpType.mult, AluOpType.mult))
        E.dma(Oqtok[rsl, :], sB[:])
    transpose_tok_to_fm(Oqtok, OqT)

    for oc in range(NC):
        E.dma(bufv(wbufA, NC), wsl(wo_q, oc, NC))
        for t4 in range(NT4):
            tsl = slice(t4 * 512, (t4 + 1) * 512)
            for c in range(NC):
                E.dma(sA[:, 0:512], OqT[c * 128:(c + 1) * 128, tsl])
                E.pe(lambda e, c=c: e.matmul(pA[:], wbufA[:, c * 128:(c + 1) * 128], sA[:, 0:512],
                                             start=(c == 0), stop=(c == NC - 1)))
            E.dma(fA[:], accT1[oc * 128:(oc + 1) * 128, tsl])
            E.dve(lambda e: e.scalar_tensor_tensor(fB[:], pA[:], statc[:, 3:4], fA[:],
                                                   AluOpType.mult, AluOpType.add))
            E.dma(outT[oc * 128:(oc + 1) * 128, tsl], fB[:])

    E.finalize(nc, block, sem)
    es.close()
    return nc



def stage_l0(inputs, b):
    f32 = np.float32
    x = np.asarray(inputs["L0_attn_x"][b], dtype=f32)
    mask, onesb, onesf, ident, onesrowf = _consts_host()
    cos2, sin2 = _rope_tables()
    m = {
        "xT": np.ascontiguousarray(x.T),
        "wqT": np.ascontiguousarray(np.asarray(inputs["L0_attn_Wq"], f32).T),
        "wkT": np.ascontiguousarray(np.asarray(inputs["L0_attn_Wk"], f32).T),
        "wvT": np.ascontiguousarray(np.asarray(inputs["L0_attn_Wv"], f32).T),
        "woT": np.ascontiguousarray(np.asarray(inputs["L0_attn_Wo"], f32).T),
        "bcols": np.concatenate([
            _chunk_cols(np.asarray(inputs["L0_attn_bq"], f32)),
            _chunk_cols(np.asarray(inputs["L0_attn_bk"], f32)),
            _chunk_cols(np.asarray(inputs["L0_attn_bv"], f32)),
            _chunk_cols(np.asarray(inputs["L0_attn_bo"], f32))], axis=1),
        "bvrow": np.ascontiguousarray(np.asarray(inputs["L0_attn_bv"], f32)[None, :]),
        "qg": np.ascontiguousarray(np.asarray(inputs["L0_attn_q_gain"], f32)[None, :]),
        "ropec": cos2, "ropes": sin2,
        "maskc": mask, "onesb": onesb, "identb": ident,
        "onesf": onesf, "onesrowf": onesrowf,
        "routerT": np.ascontiguousarray(np.asarray(inputs["L0_ffn_router_w"], f32).T),
        "sgT": np.ascontiguousarray(np.asarray(inputs["L0_ffn_shared_gate"], f32).T),
        "suT": np.ascontiguousarray(np.asarray(inputs["L0_ffn_shared_up"], f32).T),
        "sdT": np.ascontiguousarray(np.asarray(inputs["L0_ffn_shared_down"], f32).T),
    }
    for e in range(NEXP):
        m[f"wgT{e}"] = np.ascontiguousarray(np.asarray(inputs["L0_ffn_w_gate"][e], f32).T)
        m[f"wuT{e}"] = np.ascontiguousarray(np.asarray(inputs["L0_ffn_w_up"][e], f32).T)
        m[f"wdT{e}"] = np.ascontiguousarray(np.asarray(inputs["L0_ffn_w_down"][e], f32).T)
    return m




def _sel8_host():
    m = np.zeros((8, 8 * 128), dtype=bfdt)
    for e in range(8):
        m[e, e * 128:(e + 1) * 128] = 1.0
    return m


def stage_fused(inputs, b):
    m = stage_l0(inputs, b)
    f32 = np.float32
    m["sel8c"] = _sel8_host()
    m["m1wqT"] = np.ascontiguousarray(np.asarray(inputs["L1_attn_w_q"], f32).T)
    m["m1wkT"] = np.ascontiguousarray(np.asarray(inputs["L1_attn_w_k"], f32).T)
    m["m1wvT"] = np.ascontiguousarray(np.asarray(inputs["L1_attn_w_v"], f32).T)
    m["m1woT"] = np.ascontiguousarray(np.asarray(inputs["L1_attn_w_o"], f32).T)
    m["m1w1T"] = np.ascontiguousarray(np.asarray(inputs["L1_ffn_w1"], f32).T)
    m["m1w2T"] = np.ascontiguousarray(np.asarray(inputs["L1_ffn_w2"], f32).T)
    m["m1w3T"] = np.ascontiguousarray(np.asarray(inputs["L1_ffn_w3"], f32).T)
    return m


def run_fused(inputs, cores=8, dev=False):
    nc = build_fused(dev=dev)
    in_maps = [stage_fused(inputs, b) for b in range(cores)]
    res = run_bass_kernel_spmd(nc, in_maps, core_ids=list(range(cores)))
    return res.results


def kernel(**inputs):
    n = 8
    res = run_fused(inputs, cores=n, dev=False)
    out = np.stack([np.asarray(res[b]["outT"], dtype=np.float32).T for b in range(n)], axis=0)
    return out


# version 04
# speedup vs baseline: 1.2902x; 1.2530x over previous
import numpy as np
import ml_dtypes
import concourse.bass as bass
import concourse.mybir as mybir
from concourse.bass_utils import run_bass_kernel_spmd
from concourse.alu_op_type import AluOpType
from contextlib import ExitStack

F32 = mybir.dt.float32
BF16 = mybir.dt.bfloat16
I32 = mybir.dt.int32
I8 = mybir.dt.int8
AF = mybir.ActivationFunctionType
AX = mybir.AxisListType
bfdt = ml_dtypes.bfloat16

T = 2048
D = 2048
NC = 16
NT4 = 4
NT16 = 16
HD = 128
NH = 16
KVH0 = 4
DFF1 = 8192
NFC = DFF1 // 128
NI = 1408
NIC = 11
NEXP = 8
NEG = -1e30


class Emit:

    def __init__(self):
        self.steps = []

    def _add(self, eng, kind, fn, inc, after, sem):
        self.steps.append((eng, kind, fn, inc, after, sem))
        return len(self.steps) - 1

    def dma(self, dst, src, after=None, sem="Q"):
        return self._add("sync", "dma", lambda e: e.dma_start(dst, src), 16, after, sem)

    def dmag(self, dst, src, after=None, sem="GD"):
        return self._add("gpsimd", "dma", lambda e: e.dma_start(dst, src), 16, after, sem)

    def pe(self, fn, after=None):
        return self._add("tensor", "c", fn, 1, after, "T")

    def dve(self, fn, after=None):
        return self._add("vector", "c", fn, 1, after, "V")

    def act(self, fn, after=None):
        return self._add("scalar", "c", fn, 1, after, "S")

    def gp(self, fn, after=None):
        return self._add("gpsimd", "c", fn, 1, after, "G")

    def finalize(self, nc, block, sempool):
        steps = self.steps
        n = len(steps)
        cumsem = {}
        completion = []
        for eng, kind, fn, inc, after, sem in steps:
            cumsem[sem] = cumsem.get(sem, 0) + inc
            completion.append((sem, cumsem[sem]))
        totals = dict(cumsem)
        running = {}
        targets = [None] * n
        for i, (eng, kind, fn, inc, after, sem) in enumerate(steps):
            if after is None:
                tg = dict(running)
            else:
                toks = after if isinstance(after, (list, tuple)) else [after]
                tg = {}
                for t in toks:
                    if t is None:
                        continue
                    smt, val = completion[t]
                    if tg.get(smt, 0) < val:
                        tg[smt] = val
            targets[i] = tg
            running[sem] = running.get(sem, 0) + inc

        sems = {name: sempool(name) for name in totals}

        def emit(eng_obj, name):
            lastw = {}
            for (eng, kind, fn, inc, after, sem), tg in zip(steps, targets):
                if eng != name:
                    continue
                for smt in tg:
                    val = tg[smt]
                    if val > lastw.get(smt, 0):
                        eng_obj.wait_ge(sems[smt], val)
                        lastw[smt] = val
                fn(eng_obj).then_inc(sems[sem], inc)
            return lastw

        @block.sync
        def _(sync):
            emit(sync, "sync")
            for smt, tot in sorted(totals.items()):
                sync.wait_ge(sems[smt], tot)

        @block.vector
        def _(vector):
            emit(vector, "vector")

        @block.scalar
        def _(scalar):
            emit(scalar, "scalar")

        @block.tensor
        def _(tensor):
            emit(tensor, "tensor")

        @block.gpsimd
        def _(gpsimd):
            emit(gpsimd, "gpsimd")



def _chunk_cols(vec):
    n = vec.shape[0] // 128
    return np.ascontiguousarray(vec.reshape(n, 128).T.astype(np.float32))

def _rope_tables():
    inv_freq = 1.0 / (10000.0 ** (np.arange(0, HD, 2, dtype=np.float64) / HD))
    t = np.arange(T, dtype=np.float64)
    fr = np.outer(t, inv_freq)
    cos = np.cos(fr).T
    sin = np.sin(fr).T
    cos2 = np.vstack([cos, cos]).astype(bfdt)
    sin2 = np.vstack([sin, -sin]).astype(bfdt)
    return np.ascontiguousarray(cos2), np.ascontiguousarray(sin2)

def _consts_host():
    mask = np.where(np.tril(np.ones((128, 128), dtype=bool)), 0.0, NEG).astype(np.float32)
    onesb = np.ones((128, 128), dtype=bfdt)
    onesf = np.ones((128, 1), dtype=np.float32)
    ident = np.eye(128, dtype=bfdt)
    onesrow_f = np.ones((1, 128), dtype=np.float32)
    return mask, onesb, onesf, ident, onesrow_f



def build_fused(dev=False):
    nc = bass.Bass("TRN2", target_bir_lowering=False)
    E = Emit()
    es = ExitStack()

    def din(name, shape, dt=F32):
        return nc.dram_tensor(name, shape, dt, kind="ExternalInput").ap()

    def dout(name, shape, dt=F32):
        return nc.dram_tensor(name, shape, dt, kind="ExternalOutput").ap()

    def dint(name, shape, dt=BF16):
        kind = "ExternalOutput" if dev else "Internal"
        return nc.dram_tensor(name, shape, dt, kind=kind).ap()

    xT = din("xT", [D, T])
    wqT = din("wqT", [D, D])
    wkT = din("wkT", [D, 512])
    wvT = din("wvT", [D, 512])
    woT = din("woT", [D, D])
    bcols = din("bcols", [128, 40])
    bvrow = din("bvrow", [1, 512])
    qg = din("qg", [1, 16])
    ropec = din("ropec", [128, T], BF16)
    ropes = din("ropes", [128, T], BF16)
    maskc = din("maskc", [128, 128])
    onesb_in = din("onesb", [128, 128], BF16)
    identb_in = din("identb", [128, 128], BF16)
    onesf_in = din("onesf", [128, 1])
    onesrowf_in = din("onesrowf", [1, 128])
    sel8_in = din("sel8c", [8, 8 * 128], BF16)
    routerT = din("routerT", [D, 8])
    sgT = din("sgT", [D, NI])
    suT = din("suT", [D, NI])
    sdT = din("sdT", [NI, D])
    wgTs = [din(f"wgT{e}", [D, NI]) for e in range(NEXP)]
    wuTs = [din(f"wuT{e}", [D, NI]) for e in range(NEXP)]
    wdTs = [din(f"wdT{e}", [NI, D]) for e in range(NEXP)]

    outT = dout("outT", [D, T])

    x1T = dint("x1T", [D, T], F32)
    xnT = dint("xnT", [D, T])
    qrawT = dint("qrawT", [D, T])
    QT = dint("QT", [D, T])
    KT = dint("KT", [512, T])
    Vtok = dint("Vtok", [T, 512])
    OT = dint("OT", [D, T])
    hshT = dint("hshT", [NI, T])
    heT = dint("heT", [NI, T])
    accT = dint("accTl0", [D, T], F32)

    sb = lambda name, shape, dt=BF16: es.enter_context(nc.sbuf_tensor(name, shape, dt))
    xns = sb("xns", [128, NC * T])
    wbufA = sb("wbufA", [128, 16 * 512])
    wbufB = sb("wbufB", [128, 16 * 512])
    wbufW = sb("wbufW", [128, 64 * 128])
    wbufC = wbufW
    sA = sb("sA", [128, 2048])
    sB = sb("sB", [128, 2048])
    fA = sb("fA", [128, 512], F32)
    fB = sb("fB", [128, 512], F32)
    ktile = sb("ktile", [128, 2048])
    vtile = sb("vtile", [128, 2048])
    ptile = sb("ptile", [128, 2048])
    sD = ptile
    pttile = sb("pttile", [128, 2048])
    cbs_full = pttile
    qtile = sb("qtile", [128, 128])
    rows = sb("rows", [1, 1536], F32)
    rowsb = sb("rowsb", [1, 2 * T])
    comb8 = sb("comb8", [8, T])
    sel8 = sb("sel8", [8, 8 * 128])
    combtok = sb("combtok", [128, 16 * 8], F32)
    mbs = sb("mbs", [128, 512])
    rbs = sb("rbs", [128, 512])
    cols = sb("cols", [128, 8], F32)
    bvb = sb("bvb", [128, 512])
    maskb = sb("maskb", [128, 128], F32)
    onesb = sb("onesbS", [128, 128])
    identb = sb("identbS", [128, 128])
    onesf = sb("onesfS", [128, 1], F32)
    onesrowf = sb("onesrowfS", [1, 128], F32)
    biasc = sb("biascS", [128, 40], F32)
    qgS = sb("qgS", [1, 16], F32)
    epsc = sb("epsc", [1, 2], F32)
    hbig = sb("hbig", [128, DFF1])
    h8 = sb("h8", [128, DFF1], I8)
    x8 = h8[:, 0:2048]
    fX = sb("fX", [128, 2048], F32)
    scol = sb("scol", [128, 256], F32)
    statc = sb("statc", [128, 16], F32)
    eps128 = sb("eps128S", [128, 1], F32)

    ps = lambda name, shape, dt=F32: es.enter_context(nc.psum_tensor(name, shape, dt))
    pS = ps("pS", [128, 2048])
    pA = ps("pA", [128, 512])
    pB = ps("pB", [128, 512])
    pT = ps("pT", [128, 512], BF16)
    pO = ps("pO", [128, 512])
    pR = pO[0:1, 0:512]

    def sempool(name):
        return es.enter_context(nc.semaphore(f"sp_{name}"))
    block = es.enter_context(nc.Block())

    E.dma(maskb[:], maskc)
    E.dma(onesb[:], onesb_in)
    E.dma(identb[:], identb_in)
    E.dma(onesf[:], onesf_in)
    E.dma(onesrowf[:], onesrowf_in)
    E.dma(biasc[:], bcols)
    E.dma(qgS[:], qg)
    E.gp(lambda e: e.memset(epsc[:, 0:1], 1e-5))
    E.gp(lambda e: e.memset(epsc[:, 1:2], 1.1920929e-07))
    E.gp(lambda e: e.memset(eps128[:], 1e-5))
    E.dma(sel8[:], sel8_in)
    ropec_s = sb("ropecS", [128, T])
    ropes_s = sb("ropesS", [128, T])
    E.dma(ropec_s[:], ropec)
    E.dma(ropes_s[:], ropes)
    E.dma(rows[:, 0:512], bvrow)
    E.dve(lambda e: e.tensor_copy(rowsb[:, 0:512], rows[:, 0:512]))
    E.pe(lambda e: e.matmul(pA[:], onesb[0:1, :], rowsb[0:1, 0:512], start=True, stop=True))
    E.dve(lambda e: e.tensor_copy(bvb[:], pA[:]))

    def wsl(wT_dram, oc, nch, width=128):
        return wT_dram.rearrange("(c p) o -> p c o", p=128)[:, :, oc * width:(oc + 1) * width]

    def bufv(buf, nch, width=128):
        return buf[:, 0:nch * width].rearrange("p (c o) -> p c o", o=width)

    ONE_COL = onesb[:, 0:1]
    ONE_ROW = onesb[0:1, :]

    def bcast_row_bf16(src_row_bf, psum, dst_bf, n):
        E.pe(lambda e: e.matmul(psum[:, 0:n], ONE_ROW, src_row_bf, start=True, stop=True))
        E.dve(lambda e: e.tensor_copy(dst_bf[:, 0:n], psum[:, 0:n]))

    for t4 in range(NT4):
        tsl = slice(t4 * 512, (t4 + 1) * 512)
        for c in range(NC):
            xv = xns[:, c * T + t4 * 512: c * T + (t4 + 1) * 512]
            E.dmag(xv, xT[c * 128:(c + 1) * 128, tsl])
        for c in range(NC):
            xv = xns[:, c * T + t4 * 512: c * T + (t4 + 1) * 512]
            E.pe(lambda e, xv=xv, c=c: e.matmul(pR, ONE_COL, xv, start=(c == 0), stop=(c == NC - 1)))
        E.dve(lambda e: e.tensor_scalar(rows[:, 0:512], pR, 1.0 / D, None, AluOpType.mult))
        for c in range(NC):
            xv = xns[:, c * T + t4 * 512: c * T + (t4 + 1) * 512]
            E.act(lambda e, xv=xv: e.activation(sA[:, 0:512], xv, AF.Square))
            E.pe(lambda e, c=c: e.matmul(pR, ONE_COL, sA[:, 0:512], start=(c == 0), stop=(c == NC - 1)))
        mrow = rows[:, 0:512]
        e2row = rows[:, 512:1024]
        E.dve(lambda e, e2row=e2row: e.tensor_scalar(e2row, pR, 1.0 / D, None, AluOpType.mult))
        msq = rows[:, 1024:1536]
        E.dve(lambda e, mrow=mrow, msq=msq: e.tensor_tensor(msq, mrow, mrow, AluOpType.mult))
        E.dve(lambda e, e2row=e2row, msq=msq: e.tensor_tensor(e2row, e2row, msq, AluOpType.subtract))
        E.act(lambda e, e2row=e2row: e.activation(e2row, e2row, AF.Ln, bias=epsc[:, 0:1]))
        E.act(lambda e, e2row=e2row: e.activation(e2row, e2row, AF.Exp, scale=-0.5))
        E.dve(lambda e, mrow=mrow, t4=t4: e.tensor_copy(rowsb[:, t4 * 512:(t4 + 1) * 512], mrow))
        E.dve(lambda e, e2row=e2row, t4=t4: e.tensor_copy(rowsb[:, T + t4 * 512: T + (t4 + 1) * 512], e2row))
    for t4 in range(NT4):
        bcast_row_bf16(rowsb[0:1, t4 * 512:(t4 + 1) * 512], pA, mbs, 512)
        bcast_row_bf16(rowsb[0:1, T + t4 * 512: T + (t4 + 1) * 512], pB, rbs, 512)
        for c in range(NC):
            xv = xns[:, c * T + t4 * 512: c * T + (t4 + 1) * 512]
            E.dve(lambda e, xv=xv: e.tensor_tensor(sA[:, 0:512], xv, mbs[:], AluOpType.subtract))
            E.dve(lambda e, xv=xv: e.tensor_tensor(xv, sA[:, 0:512], rbs[:], AluOpType.mult))
            if dev:
                E.dma(xnT[c * 128:(c + 1) * 128, t4 * 512:(t4 + 1) * 512], xv)

    def proj(dst_dram, wT_dram, out_chunks, bias_off=None, dst_scale=None):
        for oc in range(out_chunks):
            E.dmag(bufv(wbufA, NC), wsl(wT_dram, oc, NC))
            for t4 in range(NT4):
                for c in range(NC):
                    xv = xns[:, c * T + t4 * 512: c * T + (t4 + 1) * 512]
                    E.pe(lambda e, c=c, xv=xv: e.matmul(
                        pA[:], wbufA[:, c * 128:(c + 1) * 128], xv,
                        start=(c == 0), stop=(c == NC - 1)))
                if bias_off is not None:
                    bc = biasc[:, bias_off + oc: bias_off + oc + 1]
                    E.dve(lambda e, bc=bc: e.tensor_scalar(sA[:, 0:512], pA[:], bc, None, AluOpType.add))
                else:
                    E.dve(lambda e: e.tensor_copy(sA[:, 0:512], pA[:]))
                E.dma(dst_dram[oc * 128:(oc + 1) * 128, t4 * 512:(t4 + 1) * 512], sA[:, 0:512])

    proj(qrawT, wqT, 16, bias_off=0)

    kraw = dint("krawT", [512, T])
    proj(kraw, wkT, 4, bias_off=16)

    SCL = float(1.0 / np.sqrt(HD))

    def rms_rope(src_dram, dst_dram, n_heads, head_of, gain_fold):
        for h in range(n_heads):
            hc = head_of(h)
            E.dma(sA[:], src_dram[hc * 128:(hc + 1) * 128, :])
            for t4 in range(NT4):
                tsl = slice(t4 * 512, (t4 + 1) * 512)
                E.act(lambda e, tsl=tsl: e.activation(sB[:, 0:512], sA[:, tsl], AF.Square))
                E.pe(lambda e: e.matmul(pR, ONE_COL, sB[:, 0:512], start=True, stop=True))
                rr = rows[:, 0:512]
                E.act(lambda e, rr=rr: e.activation(rr, pR, AF.Ln, bias=epsc[:, 1:2], scale=1.0 / HD))
                E.act(lambda e, rr=rr: e.activation(rr, rr, AF.Exp, scale=-0.5))
                if gain_fold is not None:
                    gf = gain_fold(h)
                    E.dve(lambda e, rr=rr, gf=gf: e.tensor_scalar(rr, rr, gf, SCL, AluOpType.mult, AluOpType.mult))
                E.dve(lambda e, rr=rr, t4=t4: e.tensor_copy(rowsb[:, t4 * 512:(t4 + 1) * 512], rr))
            E.dma(sD[0:64, :], sA[64:128, :])
            E.dma(sD[64:128, :], sA[0:64, :])
            E.dve(lambda e: e.tensor_tensor(sB[:], sA[:], ropec_s[:], AluOpType.mult))
            E.dve(lambda e: e.tensor_tensor(sD[:], sD[:], ropes_s[:], AluOpType.mult))
            E.dve(lambda e: e.tensor_tensor(sB[:], sB[:], sD[:], AluOpType.add))
            for t4 in range(NT4):
                tsl = slice(t4 * 512, (t4 + 1) * 512)
                bcast_row_bf16(rowsb[0:1, t4 * 512:(t4 + 1) * 512], pA, rbs, 512)
                E.dve(lambda e, tsl=tsl: e.tensor_tensor(sB[:, tsl], sB[:, tsl], rbs[:], AluOpType.mult))
                E.dma(dst_dram[hc * 128:(hc + 1) * 128, tsl], sB[:, tsl])

    rms_rope(qrawT, QT, 16, lambda h: h,
             gain_fold=lambda h: qgS[0:1, h:h + 1])
    rms_rope(kraw, KT, 4, lambda h: h, gain_fold=None)

    E.dmag(bufv(wbufC, NC, 512), wsl(wvT, 0, NC, 512))
    for t16 in range(NT16):
        for c in range(NC):
            lh = xns[:, c * T + t16 * 128: c * T + (t16 + 1) * 128]
            rh = wbufC[:, c * 512:(c + 1) * 512]
            E.pe(lambda e, lh=lh, rh=rh, c=c: e.matmul(pA[:], lh, rh, start=(c == 0), stop=(c == NC - 1)))
        E.dve(lambda e: e.tensor_copy(sA[:, 0:512], pA[:]))
        E.dve(lambda e: e.tensor_tensor(sA[:, 0:512], sA[:, 0:512], bvb[:], AluOpType.add))
        E.dma(Vtok[t16 * 128:(t16 + 1) * 128, :], sA[:, 0:512])

    E.dmag(bufv(wbufA, NC, 8), wsl(routerT, 0, NC, 8))
    for t16 in range(NT16):
        for c in range(NC):
            lh = xns[:, c * T + t16 * 128: c * T + (t16 + 1) * 128]
            E.pe(lambda e, lh=lh, c=c: e.matmul(pA[:, 0:8], lh, wbufA[:, c * 8:(c + 1) * 8],
                                                start=(c == 0), stop=(c == NC - 1)))
        E.dve(lambda e: e.tensor_reduce(cols[:, 0:1], pA[:, 0:8], AX.X, AluOpType.max))
        E.dve(lambda e: e.tensor_scalar(cols[:, 1:2], cols[:, 0:1], -1.0, None, AluOpType.mult))
        def _exp_router(e):
            e.activation(fA[:, 0:8], pA[:, 0:8], AF.Exp, bias=cols[:, 1:2], accum_out=cols[:, 2:3])
            return e.copy(cols[:, 6:7], cols[:, 2:3])
        E.act(_exp_router)
        E.dve(lambda e: e.reciprocal(cols[:, 3:4], cols[:, 2:3]))
        E.dve(lambda e: e.tensor_scalar(fA[:, 0:8], fA[:, 0:8], cols[:, 3:4], None, AluOpType.mult))
        E.dve(lambda e: e.tensor_reduce(cols[:, 4:5], fA[:, 0:8], AX.X, AluOpType.max))
        E.dve(lambda e: e.tensor_scalar(fB[:, 0:8], fA[:, 0:8], cols[:, 4:5], None, AluOpType.is_ge))
        E.dve(lambda e: e.scalar_tensor_tensor(fB[:, 0:8], fB[:, 0:8], NEG, fA[:, 0:8], AluOpType.mult, AluOpType.add))
        E.dve(lambda e: e.tensor_reduce(cols[:, 5:6], fB[:, 0:8], AX.X, AluOpType.max))
        E.dve(lambda e: e.tensor_scalar(fB[:, 0:8], fA[:, 0:8], cols[:, 5:6], None, AluOpType.is_ge))
        E.dve(lambda e, t16=t16: e.tensor_tensor(combtok[:, t16 * 8:(t16 + 1) * 8], fA[:, 0:8], fB[:, 0:8], AluOpType.mult))
    for t16 in range(NT16):
        E.dve(lambda e, t16=t16: e.tensor_copy(sA[:, 0:8], combtok[:, t16 * 8:(t16 + 1) * 8]))
        E.pe(lambda e: e.transpose(pT[0:8, 0:128], sA[:, 0:8], identb[:]))
        E.dve(lambda e, t16=t16: e.tensor_copy(comb8[0:8, t16 * 128:(t16 + 1) * 128], pT[0:8, 0:128]))

    pSa = pS[:, 0:512]
    pSb = pS[:, 512:1024]
    acc_store_tok = {}

    def gated_ffn(gT, uT, dst_dram, comb_e=None, tag=""):
        cb_tok = None
        if comb_e is not None:
            for t4 in range(NT4):
                E.pe(lambda e, t4=t4, comb_e=comb_e: e.matmul(
                    pO[:], sel8[:, comb_e * 128:(comb_e + 1) * 128],
                    comb8[0:8, t4 * 512:(t4 + 1) * 512], start=True, stop=True))
                cb_tok = E.dve(lambda e, t4=t4: e.tensor_copy(cbs_full[:, t4 * 512:(t4 + 1) * 512], pO[:]))
        pArr = [pA, pSa]
        pBrr = [pB, pSb]
        wlast = [None, None]
        pAev = [None, None]
        pBev = [None, None]
        sBst = [None, None]
        first = True
        it = 0
        for oc in range(NIC):
            pw = oc % 2
            wa_t = E.dmag(bufv(wbufA[:, pw * 2048:(pw + 1) * 2048], NC), wsl(gT, oc, NC),
                          after=None if first else [wlast[pw]], sem=f"wa{pw}")
            wb_t = E.dmag(bufv(wbufB[:, pw * 2048:(pw + 1) * 2048], NC), wsl(uT, oc, NC),
                          after=None if first else [wlast[pw]], sem=f"wb{pw}")
            first = False
            for t4 in range(NT4):
                pp = it % 2
                mmA = None
                for c in range(NC):
                    xv = xns[:, c * T + t4 * 512: c * T + (t4 + 1) * 512]
                    mmA = E.pe(lambda e, c=c, xv=xv, pp=pp, pw=pw: e.matmul(
                        pArr[pp][:], wbufA[:, pw * 2048 + c * 128: pw * 2048 + (c + 1) * 128], xv,
                        start=(c == 0), stop=(c == NC - 1)),
                        after=([wa_t, pAev[pp], cb_tok] if c == 0 else []))
                mmB = None
                for c in range(NC):
                    xv = xns[:, c * T + t4 * 512: c * T + (t4 + 1) * 512]
                    mmB = E.pe(lambda e, c=c, xv=xv, pp=pp, pw=pw: e.matmul(
                        pBrr[pp][:], wbufB[:, pw * 2048 + c * 128: pw * 2048 + (c + 1) * 128], xv,
                        start=(c == 0), stop=(c == NC - 1)),
                        after=([wb_t, pBev[pp]] if c == 0 else []))
                sg = E.act(lambda e, pp=pp: e.activation(sA[:, pp * 512:(pp + 1) * 512], pArr[pp][:], AF.Sigmoid),
                           after=[mmA])
                tt = E.dve(lambda e, pp=pp: e.tensor_tensor(sB[:, pp * 512:(pp + 1) * 512],
                                                            sA[:, pp * 512:(pp + 1) * 512], pBrr[pp][:],
                                                            AluOpType.mult), after=[sg, mmB, sBst[pp]])
                ttl = tt
                if comb_e is not None:
                    ttl = E.dve(lambda e, pp=pp, t4=t4: e.tensor_tensor(
                        sB[:, pp * 512:(pp + 1) * 512], sB[:, pp * 512:(pp + 1) * 512],
                        cbs_full[:, t4 * 512:(t4 + 1) * 512], AluOpType.mult), after=[ttl])
                st = E.dma(dst_dram[oc * 128:(oc + 1) * 128, t4 * 512:(t4 + 1) * 512],
                           sB[:, pp * 512:(pp + 1) * 512], after=[ttl], sem=f"hs{pp}")
                wlast[pw] = mmB
                pAev[pp] = sg
                pBev[pp] = tt
                sBst[pp] = st
                it += 1

    def down_proj(dT, src_dram, first_flag):
        pArr = [pA, pSa]
        wlast = [None, None]
        pAev = [None, None]
        fev = [None, None]
        he_last_mm = None
        first = True
        it = 0
        for t4 in range(NT4):
            tsl = slice(t4 * 512, (t4 + 1) * 512)
            he_toks = []
            for ic in range(NIC):
                he_toks.append(E.dma(wbufW[:, ic * 512:(ic + 1) * 512],
                                     src_dram[ic * 128:(ic + 1) * 128, tsl],
                                     after=None if first else [he_last_mm], sem="he"))
                first = False
            for oc in range(NC):
                pw = oc % 2
                pp = it % 2
                wd_t = E.dmag(bufv(wbufA[:, pw * 2048:(pw + 1) * 2048], NIC), wsl(dT, oc, NIC),
                              after=[wlast[pw]] if wlast[pw] is not None else [he_toks[0]],
                              sem=f"wd{pw}")
                mm = None
                for ic in range(NIC):
                    mm = E.pe(lambda e, ic=ic, pp=pp, pw=pw: e.matmul(
                        pArr[pp][:], wbufA[:, pw * 2048 + ic * 128: pw * 2048 + (ic + 1) * 128],
                        wbufW[:, ic * 512:(ic + 1) * 512],
                        start=(ic == 0), stop=(ic == NIC - 1)),
                        after=(he_toks + [wd_t, pAev[pp]] if ic == 0 else []))
                fbuf = fA if pp == 0 else fB
                prev_store = acc_store_tok.get((oc, t4))
                war = [x for x in [fev[pp], prev_store] if x is not None]
                if first_flag:
                    ld = E.dma(fbuf[:], xT[oc * 128:(oc + 1) * 128, tsl], after=war)
                    xnv = xns[:, oc * T + t4 * 512: oc * T + (t4 + 1) * 512]
                    a1 = E.dve(lambda e, fbuf=fbuf, xnv=xnv: e.tensor_tensor(fbuf[:], fbuf[:], xnv, AluOpType.add),
                               after=[ld])
                else:
                    ld = E.dma(fbuf[:], accT[oc * 128:(oc + 1) * 128, tsl], after=war)
                    a1 = ld
                a2 = E.dve(lambda e, fbuf=fbuf, pp=pp: e.tensor_tensor(fbuf[:], fbuf[:], pArr[pp][:], AluOpType.add),
                           after=[a1, mm])
                st = E.dma(accT[oc * 128:(oc + 1) * 128, tsl], fbuf[:], after=[a2], sem=f"as{pp}")
                acc_store_tok[(oc, t4)] = st
                wlast[pw] = mm
                pAev[pp] = a2
                fev[pp] = st
                he_last_mm = mm
                it += 1

    gated_ffn(sgT, suT, hshT, comb_e=None)
    down_proj(sdT, hshT, first_flag=True)
    for e_i in range(NEXP):
        gated_ffn(wgTs[e_i], wuTs[e_i], heT, comb_e=e_i)
        down_proj(wdTs[e_i], heT, first_flag=False)

    for qt in range(NT16):
        nk = (qt + 1) * 128
        nk4 = (nk + 511) // 512
        for kv in range(KVH0):
            E.dma(ktile[:, 0:nk], KT[kv * 128:(kv + 1) * 128, 0:nk])
            E.dma(vtile[:, 0:nk].rearrange("p (b d) -> p b d", d=128),
                  Vtok[0:nk, kv * 128:(kv + 1) * 128].rearrange("(b p) d -> p b d", p=128))
            for hq in range(4):
                h = kv * 4 + hq
                E.dma(qtile[:], QT[h * 128:(h + 1) * 128, qt * 128:(qt + 1) * 128])
                for kb4 in range(nk4):
                    w = min(512, nk - kb4 * 512)
                    E.pe(lambda e, kb4=kb4, w=w: e.matmul(
                        pS[:, kb4 * 512: kb4 * 512 + w], qtile[:], ktile[:, kb4 * 512: kb4 * 512 + w],
                        start=True, stop=True))
                E.dve(lambda e, qt=qt: e.tensor_tensor(pS[:, qt * 128:(qt + 1) * 128],
                                                       pS[:, qt * 128:(qt + 1) * 128], maskb[:], AluOpType.add))
                E.dve(lambda e, nk=nk: e.tensor_reduce(cols[:, 0:1], pS[:, 0:nk], AX.X, AluOpType.max))
                E.dve(lambda e: e.tensor_scalar(cols[:, 1:2], cols[:, 0:1], -1.0, None, AluOpType.mult))
                def _exp_attn(e, nk=nk):
                    e.activation(ptile[:, 0:nk], pS[:, 0:nk], AF.Exp,
                                 bias=cols[:, 1:2], accum_out=cols[:, 2:3])
                    return e.copy(cols[:, 6:7], cols[:, 2:3])
                E.act(_exp_attn)
                E.dve(lambda e: e.reciprocal(cols[:, 3:4], cols[:, 2:3]))
                E.dve(lambda e, nk=nk: e.tensor_scalar(ptile[:, 0:nk], ptile[:, 0:nk], cols[:, 3:4], None, AluOpType.mult))
                for kb in range(qt + 1):
                    sl4 = (kb % 4) * 128
                    E.pe(lambda e, kb=kb, sl4=sl4: e.transpose(
                        pT[:, sl4:sl4 + 128], ptile[:, kb * 128:(kb + 1) * 128], identb[:]))
                    if kb % 4 == 3 or kb == qt:
                        base = (kb // 4) * 512
                        wdt = (kb % 4 + 1) * 128
                        E.dve(lambda e, base=base, wdt=wdt: e.tensor_copy(
                            pttile[:, base:base + wdt], pT[:, 0:wdt]))
                for kb in range(qt + 1):
                    E.pe(lambda e, kb=kb, qt=qt: e.matmul(
                        pO[:, 0:128], vtile[:, kb * 128:(kb + 1) * 128], pttile[:, kb * 128:(kb + 1) * 128],
                        start=(kb == 0), stop=(kb == qt)))
                E.dve(lambda e: e.tensor_copy(sA[:, 0:128], pO[:, 0:128]))
                E.dma(OT[h * 128:(h + 1) * 128, qt * 128:(qt + 1) * 128], sA[:, 0:128])

    for oc in range(NC):
        E.dmag(bufv(wbufA, NC), wsl(woT, oc, NC))
        for t4 in range(NT4):
            tsl = slice(t4 * 512, (t4 + 1) * 512)
            for c in range(NC):
                E.dma(sA[:, 0:512], OT[c * 128:(c + 1) * 128, tsl])
                E.pe(lambda e, c=c: e.matmul(pA[:], wbufA[:, c * 128:(c + 1) * 128], sA[:, 0:512],
                                             start=(c == 0), stop=(c == NC - 1)))
            E.dma(fA[:], accT[oc * 128:(oc + 1) * 128, tsl])
            bc = biasc[:, 24 + oc: 24 + oc + 1]
            E.dve(lambda e, bc=bc: e.scalar_tensor_tensor(fB[:], pA[:], bc, fA[:], AluOpType.add, AluOpType.add))
            E.dma(x1T[oc * 128:(oc + 1) * 128, tsl], fB[:])


    m1wqT = din("m1wqT", [D, D])
    m1wkT = din("m1wkT", [D, D])
    m1wvT = din("m1wvT", [D, D])
    m1woT = din("m1woT", [D, D])
    m1w1T = din("m1w1T", [D, DFF1])
    m1w2T = din("m1w2T", [D, DFF1])
    m1w3T = din("m1w3T", [DFF1, D])

    xqAtok = dint("xqAtok", [T, D])
    xqFtok = dint("xqFtok", [T, D])
    xqAT = dint("xqAT", [D, T])
    xqFT = dint("xqFT", [D, T])
    wq_q = dint("wq_q", [D, D])
    wk_q = dint("wk_q", [D, D])
    wv_q = dint("wv_q", [D, D])
    wo_q = dint("wo_q", [D, D])
    w1_q = dint("w1_q", [D, DFF1])
    w2_q = dint("w2_q", [D, DFF1])
    w3_q = dint("w3_q", [DFF1, D])
    QT1 = dint("QT1", [D, T])
    KT1 = dint("KT1", [D, T])
    Vtok1 = dint("Vtok1", [T, D])
    OT1 = dint("OT1", [D, T])
    Oqtok = dint("Oqtok", [T, D])
    OqT = dint("OqT", [D, T])
    htok = dint("htok", [T, DFF1])
    hqT = dint("hqT", [DFF1, T])
    accT1 = dint("accTl1", [D, T], F32)
    p11 = pO[0:1, 0:1]

    for t4 in range(NT4):
        tsl = slice(t4 * 512, (t4 + 1) * 512)
        for c in range(NC):
            xv = xns[:, c * T + t4 * 512: c * T + (t4 + 1) * 512]
            E.dmag(xv, x1T[c * 128:(c + 1) * 128, tsl])
        for c in range(NC):
            xv = xns[:, c * T + t4 * 512: c * T + (t4 + 1) * 512]
            E.pe(lambda e, xv=xv, c=c: e.matmul(pR, ONE_COL, xv, start=(c == 0), stop=(c == NC - 1)))
        E.dve(lambda e: e.tensor_scalar(rows[:, 0:512], pR, 1.0 / D, None, AluOpType.mult))
        for c in range(NC):
            xv = xns[:, c * T + t4 * 512: c * T + (t4 + 1) * 512]
            E.act(lambda e, xv=xv: e.activation(sA[:, 0:512], xv, AF.Square))
            E.pe(lambda e, c=c: e.matmul(pR, ONE_COL, sA[:, 0:512], start=(c == 0), stop=(c == NC - 1)))
        mrow = rows[:, 0:512]
        e2row = rows[:, 512:1024]
        msq = rows[:, 1024:1536]
        E.dve(lambda e, e2row=e2row: e.tensor_scalar(e2row, pR, 1.0 / D, None, AluOpType.mult))
        E.dve(lambda e, mrow=mrow, msq=msq: e.tensor_tensor(msq, mrow, mrow, AluOpType.mult))
        E.dve(lambda e, e2row=e2row, msq=msq: e.tensor_tensor(e2row, e2row, msq, AluOpType.subtract))
        E.act(lambda e, e2row=e2row: e.activation(e2row, e2row, AF.Ln, bias=epsc[:, 0:1]))
        E.act(lambda e, e2row=e2row: e.activation(e2row, e2row, AF.Exp, scale=-0.5))
        E.dve(lambda e, mrow=mrow, t4=t4: e.tensor_copy(rowsb[:, t4 * 512:(t4 + 1) * 512], mrow))
        E.dve(lambda e, e2row=e2row, t4=t4: e.tensor_copy(rowsb[:, T + t4 * 512: T + (t4 + 1) * 512], e2row))
    for t4 in range(NT4):
        bcast_row_bf16(rowsb[0:1, t4 * 512:(t4 + 1) * 512], pA, mbs, 512)
        bcast_row_bf16(rowsb[0:1, T + t4 * 512: T + (t4 + 1) * 512], pB, rbs, 512)
        for c in range(NC):
            xv = xns[:, c * T + t4 * 512: c * T + (t4 + 1) * 512]
            E.dve(lambda e, xv=xv: e.tensor_tensor(sA[:, 0:512], xv, mbs[:], AluOpType.subtract))
            E.dve(lambda e, xv=xv: e.tensor_tensor(xv, sA[:, 0:512], rbs[:], AluOpType.mult))

    f3 = lambda t: t.rearrange("p (g i) -> p g i", i=64)
    for t16 in range(NT16):
        rsl = slice(t16 * 128, (t16 + 1) * 128)
        for c in range(NC):
            sl4 = (c % 4) * 128
            E.pe(lambda e, c=c, t16=t16, sl4=sl4: e.transpose(
                pT[:, sl4:sl4 + 128], xns[:, c * T + t16 * 128: c * T + (t16 + 1) * 128], identb[:]))
            if c % 4 == 3:
                base = (c // 4) * 512
                E.dve(lambda e, base=base: e.tensor_copy(sA[:, base:base + 512], pT[:, 0:512]))
        E.dve(lambda e: e.tensor_reduce(cols[:, 3:4], sA[:], AX.X, AluOpType.max, apply_absolute_value=True))
        E.dve(lambda e: e.tensor_scalar(cols[:, 3:4], cols[:, 3:4], 1e-5, None, AluOpType.max))
        E.dve(lambda e: e.reciprocal(cols[:, 4:5], cols[:, 3:4]))
        E.dve(lambda e: e.tensor_scalar(x8, sA[:], cols[:, 4:5], 127.0, AluOpType.mult, AluOpType.mult))
        E.dve(lambda e: e.tensor_scalar(sB[:], x8, cols[:, 3:4], 1.0 / 127.0, AluOpType.mult, AluOpType.mult))
        E.dma(xqAtok[rsl, :], sB[:])
        E.dve(lambda e: e.tensor_reduce(scol[:, 0:32], f3(sA[:]), AX.X, AluOpType.max, apply_absolute_value=True))
        E.dve(lambda e: e.tensor_scalar(scol[:, 0:32], scol[:, 0:32], 1e-5, None, AluOpType.max))
        E.dve(lambda e: e.reciprocal(scol[:, 32:64], scol[:, 0:32]))
        E.dve(lambda e: e.tensor_scalar(scol[:, 32:64], scol[:, 32:64], 127.0, None, AluOpType.mult))
        E.dve(lambda e: e.tensor_tensor(f3(x8), f3(sA[:]),
                                        scol[:, 32:64].unsqueeze(2).broadcast_to([128, 32, 64]), AluOpType.mult))
        E.dve(lambda e: e.tensor_scalar(scol[:, 64:96], scol[:, 0:32], 1.0 / 127.0, None, AluOpType.mult))
        E.dve(lambda e: e.tensor_tensor(f3(sB[:]), f3(x8),
                                        scol[:, 64:96].unsqueeze(2).broadcast_to([128, 32, 64]), AluOpType.mult))
        E.dma(xqFtok[rsl, :], sB[:])

    def transpose_tok_to_fm(src_tok, dst_T):
        for t16 in range(NT16):
            E.dma(sA[:], src_tok[t16 * 128:(t16 + 1) * 128, :])
            for c in range(NC):
                sl4 = (c % 4) * 128
                E.pe(lambda e, c=c, sl4=sl4: e.transpose(pT[:, sl4:sl4 + 128], sA[:, c * 128:(c + 1) * 128], identb[:]))
                if c % 4 == 3:
                    base = (c // 4) * 512
                    E.dve(lambda e, base=base: e.tensor_copy(sB[:, base:base + 512], pT[:, 0:512]))
            for c in range(NC):
                E.dma(dst_T[c * 128:(c + 1) * 128, t16 * 128:(t16 + 1) * 128], sB[:, c * 128:(c + 1) * 128])

    transpose_tok_to_fm(xqAtok, xqAT)
    transpose_tok_to_fm(xqFtok, xqFT)

    def weight_stats2(i, wT_dram, numel, clip_eps):
        rows_n = wT_dram.shape[0]
        wT3 = wT_dram.rearrange("(c p) o -> p c o", p=128)
        nco = wT_dram.shape[1] // 2048
        E.gp(lambda e: e.memset(scol[:, 96:97], 0.0))
        for c in range(rows_n // 128):
            for ob in range(nco):
                E.dma(fX[:], wT3[:, c, ob * 2048:(ob + 1) * 2048])
                def _abs(e):
                    e.activation(fX[:], fX[:], AF.Abs, accum_out=cols[:, 7:8])
                    return e.copy(cols[:, 6:7], cols[:, 7:8])
                E.act(_abs)
                E.dve(lambda e: e.tensor_tensor(scol[:, 96:97], scol[:, 96:97], cols[:, 7:8], AluOpType.add))
        E.pe(lambda e: e.matmul(p11, scol[:, 96:97], onesf[:], start=True, stop=True))
        E.dve(lambda e, i=i, numel=numel: e.tensor_scalar(statc[0:1, 8 + i:9 + i], p11, 1.0 / numel, None, AluOpType.mult))
        if clip_eps is not None:
            E.dve(lambda e, i=i, clip_eps=clip_eps: e.tensor_scalar(
                statc[0:1, 8 + i:9 + i], statc[0:1, 8 + i:9 + i], clip_eps, None, AluOpType.max))
        E.pe(lambda e, i=i: e.matmul(pO[:, 0:1], onesrowf[0:1, :], statc[0:1, 8 + i:9 + i], start=True, stop=True))
        E.dve(lambda e, i=i: e.tensor_copy(statc[:, i:i + 1], pO[:, 0:1]))
        E.dve(lambda e, i=i: e.reciprocal(statc[:, 8 + i:9 + i], statc[:, i:i + 1]))

    WSTATS = [(m1wqT, D * D), (m1wkT, D * D), (m1wvT, D * D), (m1woT, D * D),
              (m1w1T, D * DFF1), (m1w2T, D * DFF1), (m1w3T, DFF1 * D)]
    for i, (wt, ne) in enumerate(WSTATS):
        weight_stats2(i, wt, ne, 1e-5 if i < 4 else None)

    def quant_weight_ternary(i, wT_dram, dst):
        rows_n = wT_dram.shape[0]
        wT3 = wT_dram.rearrange("(c p) o -> p c o", p=128)
        dst3 = dst.rearrange("(c p) o -> p c o", p=128)
        nco = wT_dram.shape[1] // 2048
        for c in range(rows_n // 128):
            for ob in range(nco):
                E.dma(fX[:], wT3[:, c, ob * 2048:(ob + 1) * 2048])
                E.dve(lambda e, i=i: e.tensor_scalar(fX[:], fX[:], statc[:, 8 + i:9 + i], 1.0, AluOpType.mult, AluOpType.min))
                E.dve(lambda e: e.tensor_scalar(x8, fX[:], -1.0, None, AluOpType.max))
                E.dve(lambda e: e.tensor_copy(sA[:], x8))
                E.dma(dst3[:, c, ob * 2048:(ob + 1) * 2048], sA[:])

    def quant_weight_sign(wT_dram, dst):
        rows_n = wT_dram.shape[0]
        wT3 = wT_dram.rearrange("(c p) o -> p c o", p=128)
        dst3 = dst.rearrange("(c p) o -> p c o", p=128)
        nco = wT_dram.shape[1] // 2048
        for c in range(rows_n // 128):
            for ob in range(nco):
                E.dma(fX[:], wT3[:, c, ob * 2048:(ob + 1) * 2048])
                E.act(lambda e: e.activation(sA[:], fX[:], AF.Sign))
                E.dma(dst3[:, c, ob * 2048:(ob + 1) * 2048], sA[:])

    quant_weight_ternary(0, m1wqT, wq_q)
    quant_weight_ternary(1, m1wkT, wk_q)
    quant_weight_ternary(2, m1wvT, wv_q)
    quant_weight_ternary(3, m1woT, wo_q)
    quant_weight_sign(m1w1T, w1_q)
    quant_weight_sign(m1w2T, w2_q)
    quant_weight_sign(m1w3T, w3_q)

    for c in range(NC):
        E.dma(xns[:, c * T:(c + 1) * T], xqAT[c * 128:(c + 1) * 128, :])

    def proj_l1(dst_dram, wq_dram, dq_i, extra_scale):
        for oc in range(NC):
            E.dma(bufv(wbufA, NC), wsl(wq_dram, oc, NC))
            for t4 in range(NT4):
                for c in range(NC):
                    xv = xns[:, c * T + t4 * 512: c * T + (t4 + 1) * 512]
                    E.pe(lambda e, c=c, xv=xv: e.matmul(pA[:], wbufA[:, c * 128:(c + 1) * 128], xv,
                                                        start=(c == 0), stop=(c == NC - 1)))
                E.dve(lambda e, dq_i=dq_i, extra_scale=extra_scale: e.tensor_scalar(
                    sA[:, 0:512], pA[:], statc[:, dq_i:dq_i + 1], extra_scale, AluOpType.mult, AluOpType.mult))
                E.dma(dst_dram[oc * 128:(oc + 1) * 128, t4 * 512:(t4 + 1) * 512], sA[:, 0:512])

    proj_l1(QT1, wq_q, 0, SCL)
    proj_l1(KT1, wk_q, 1, 1.0)

    for vb in range(4):
        E.dma(bufv(wbufA, NC, 512), wsl(wv_q, vb, NC, 512))
        for t16 in range(NT16):
            for c in range(NC):
                lh = xns[:, c * T + t16 * 128: c * T + (t16 + 1) * 128]
                rh = wbufA[:, c * 512:(c + 1) * 512]
                E.pe(lambda e, lh=lh, rh=rh, c=c: e.matmul(pA[:], lh, rh, start=(c == 0), stop=(c == NC - 1)))
            E.dve(lambda e: e.tensor_scalar(sA[:, 0:512], pA[:], statc[:, 2:3], None, AluOpType.mult))
            E.dma(Vtok1[t16 * 128:(t16 + 1) * 128, vb * 512:(vb + 1) * 512], sA[:, 0:512])

    for c in range(NC):
        E.dma(xns[:, c * T:(c + 1) * T], xqFT[c * 128:(c + 1) * 128, :])

    for ffb in range(16):
        E.dma(bufv(wbufA, NC, 512), wsl(w1_q, ffb, NC, 512))
        E.dma(bufv(wbufB, NC, 512), wsl(w2_q, ffb, NC, 512))
        for t16 in range(NT16):
            for c in range(NC):
                lh = xns[:, c * T + t16 * 128: c * T + (t16 + 1) * 128]
                E.pe(lambda e, lh=lh, c=c: e.matmul(pA[:], lh, wbufA[:, c * 512:(c + 1) * 512],
                                                    start=(c == 0), stop=(c == NC - 1)))
            for c in range(NC):
                lh = xns[:, c * T + t16 * 128: c * T + (t16 + 1) * 128]
                E.pe(lambda e, lh=lh, c=c: e.matmul(pB[:], lh, wbufB[:, c * 512:(c + 1) * 512],
                                                    start=(c == 0), stop=(c == NC - 1)))
            E.act(lambda e: e.activation(sA[:, 0:512], pA[:], AF.Silu, scale=statc[:, 4:5]))
            E.dve(lambda e: e.tensor_scalar(sB[:, 0:512], pB[:], statc[:, 5:6], None, AluOpType.mult))
            E.dve(lambda e: e.tensor_tensor(sA[:, 0:512], sA[:, 0:512], sB[:, 0:512], AluOpType.mult))
            E.dma(htok[t16 * 128:(t16 + 1) * 128, ffb * 512:(ffb + 1) * 512], sA[:, 0:512])

    f3h = lambda t: t.rearrange("p (g i) -> p g i", i=64)
    for t16 in range(NT16):
        rsl = slice(t16 * 128, (t16 + 1) * 128)
        E.dma(hbig[:], htok[rsl, :])
        E.dve(lambda e: e.tensor_reduce(scol[:, 0:128], f3h(hbig[:]), AX.X, AluOpType.max, apply_absolute_value=True))
        E.dve(lambda e: e.tensor_scalar(scol[:, 0:128], scol[:, 0:128], 1e-5, None, AluOpType.max))
        E.dve(lambda e: e.reciprocal(scol[:, 128:256], scol[:, 0:128]))
        E.dve(lambda e: e.tensor_scalar(scol[:, 128:256], scol[:, 128:256], 127.0, None, AluOpType.mult))
        E.dve(lambda e: e.tensor_tensor(f3h(h8[:]), f3h(hbig[:]),
                                        scol[:, 128:256].unsqueeze(2).broadcast_to([128, 128, 64]), AluOpType.mult))
        E.dve(lambda e: e.tensor_scalar(scol[:, 128:256], scol[:, 0:128], 1.0 / 127.0, None, AluOpType.mult))
        E.dve(lambda e: e.tensor_tensor(f3h(hbig[:]), f3h(h8[:]),
                                        scol[:, 128:256].unsqueeze(2).broadcast_to([128, 128, 64]), AluOpType.mult))
        for fc in range(NFC):
            sl4 = (fc % 4) * 128
            E.pe(lambda e, fc=fc, sl4=sl4: e.transpose(pT[:, sl4:sl4 + 128], hbig[:, fc * 128:(fc + 1) * 128], identb[:]))
            if fc % 4 == 3:
                base4 = fc - 3
                E.dve(lambda e: e.tensor_copy(sB[:, 0:512], pT[:, 0:512]))
                for j in range(4):
                    E.dma(hqT[(base4 + j) * 128:(base4 + j + 1) * 128, rsl], sB[:, j * 128:(j + 1) * 128])

    for t4 in range(NT4):
        tsl = slice(t4 * 512, (t4 + 1) * 512)
        for fc in range(NFC):
            E.dma(xns[:, fc * 512:(fc + 1) * 512], hqT[fc * 128:(fc + 1) * 128, tsl])
        for oc in range(NC):
            E.dma(bufv(wbufW, NFC), wsl(m1w3T, oc, NFC) if False else wsl(w3_q, oc, NFC))
            for fc in range(NFC):
                E.pe(lambda e, fc=fc: e.matmul(pA[:], wbufW[:, fc * 128:(fc + 1) * 128],
                                               xns[:, fc * 512:(fc + 1) * 512],
                                               start=(fc == 0), stop=(fc == NFC - 1)))
            E.dma(fA[:], x1T[oc * 128:(oc + 1) * 128, tsl])
            E.dve(lambda e: e.tensor_scalar(fB[:], pA[:], statc[:, 6:7], None, AluOpType.mult))
            E.dve(lambda e: e.tensor_tensor(fB[:], fB[:], fA[:], AluOpType.add))
            E.dma(accT1[oc * 128:(oc + 1) * 128, tsl], fB[:])

    for qt in range(NT16):
        nk = (qt + 1) * 128
        nk4 = (nk + 511) // 512
        for h in range(NH):
            E.dma(ktile[:, 0:nk], KT1[h * 128:(h + 1) * 128, 0:nk])
            E.dma(vtile[:, 0:nk].rearrange("p (b d) -> p b d", d=128),
                  Vtok1[0:nk, h * 128:(h + 1) * 128].rearrange("(b p) d -> p b d", p=128))
            E.dma(qtile[:], QT1[h * 128:(h + 1) * 128, qt * 128:(qt + 1) * 128])
            for kb4 in range(nk4):
                w = min(512, nk - kb4 * 512)
                E.pe(lambda e, kb4=kb4, w=w: e.matmul(
                    pS[:, kb4 * 512: kb4 * 512 + w], qtile[:], ktile[:, kb4 * 512: kb4 * 512 + w],
                    start=True, stop=True))
            E.dve(lambda e, qt=qt: e.tensor_tensor(pS[:, qt * 128:(qt + 1) * 128],
                                                   pS[:, qt * 128:(qt + 1) * 128], maskb[:], AluOpType.add))
            E.dve(lambda e, nk=nk: e.tensor_reduce(cols[:, 0:1], pS[:, 0:nk], AX.X, AluOpType.max))
            E.dve(lambda e: e.tensor_scalar(cols[:, 1:2], cols[:, 0:1], -1.0, None, AluOpType.mult))
            def _exp_attn1(e, nk=nk):
                e.activation(ptile[:, 0:nk], pS[:, 0:nk], AF.Exp,
                             bias=cols[:, 1:2], accum_out=cols[:, 2:3])
                return e.copy(cols[:, 6:7], cols[:, 2:3])
            E.act(_exp_attn1)
            E.dve(lambda e: e.reciprocal(cols[:, 3:4], cols[:, 2:3]))
            E.dve(lambda e, nk=nk: e.tensor_scalar(ptile[:, 0:nk], ptile[:, 0:nk], cols[:, 3:4], None, AluOpType.mult))
            for kb in range(qt + 1):
                sl4 = (kb % 4) * 128
                E.pe(lambda e, kb=kb, sl4=sl4: e.transpose(
                    pT[:, sl4:sl4 + 128], ptile[:, kb * 128:(kb + 1) * 128], identb[:]))
                if kb % 4 == 3 or kb == qt:
                    base = (kb // 4) * 512
                    wdt = (kb % 4 + 1) * 128
                    E.dve(lambda e, base=base, wdt=wdt: e.tensor_copy(
                        pttile[:, base:base + wdt], pT[:, 0:wdt]))
            for kb in range(qt + 1):
                E.pe(lambda e, kb=kb, qt=qt: e.matmul(
                    pO[:, 0:128], vtile[:, kb * 128:(kb + 1) * 128], pttile[:, kb * 128:(kb + 1) * 128],
                    start=(kb == 0), stop=(kb == qt)))
            E.dve(lambda e: e.tensor_copy(sA[:, 0:128], pO[:, 0:128]))
            E.dma(OT1[h * 128:(h + 1) * 128, qt * 128:(qt + 1) * 128], sA[:, 0:128])

    for t16 in range(NT16):
        rsl = slice(t16 * 128, (t16 + 1) * 128)
        for c in range(NC):
            E.dma(sA[:, c * 128:(c + 1) * 128], OT1[c * 128:(c + 1) * 128, rsl])
        for c in range(NC):
            sl4 = (c % 4) * 128
            E.pe(lambda e, c=c, sl4=sl4: e.transpose(pT[:, sl4:sl4 + 128], sA[:, c * 128:(c + 1) * 128], identb[:]))
            if c % 4 == 3:
                base = (c // 4) * 512
                E.dve(lambda e, base=base: e.tensor_copy(sB[:, base:base + 512], pT[:, 0:512]))
        E.dve(lambda e: e.tensor_reduce(cols[:, 3:4], sB[:], AX.X, AluOpType.max, apply_absolute_value=True))
        E.dve(lambda e: e.tensor_scalar(cols[:, 3:4], cols[:, 3:4], 1e-5, None, AluOpType.max))
        E.dve(lambda e: e.reciprocal(cols[:, 4:5], cols[:, 3:4]))
        E.dve(lambda e: e.tensor_scalar(x8, sB[:], cols[:, 4:5], 127.0, AluOpType.mult, AluOpType.mult))
        E.dve(lambda e: e.tensor_scalar(sB[:], x8, cols[:, 3:4], 1.0 / 127.0, AluOpType.mult, AluOpType.mult))
        E.dma(Oqtok[rsl, :], sB[:])
    transpose_tok_to_fm(Oqtok, OqT)

    for oc in range(NC):
        E.dma(bufv(wbufA, NC), wsl(wo_q, oc, NC))
        for t4 in range(NT4):
            tsl = slice(t4 * 512, (t4 + 1) * 512)
            for c in range(NC):
                E.dma(sA[:, 0:512], OqT[c * 128:(c + 1) * 128, tsl])
                E.pe(lambda e, c=c: e.matmul(pA[:], wbufA[:, c * 128:(c + 1) * 128], sA[:, 0:512],
                                             start=(c == 0), stop=(c == NC - 1)))
            E.dma(fA[:], accT1[oc * 128:(oc + 1) * 128, tsl])
            E.dve(lambda e: e.scalar_tensor_tensor(fB[:], pA[:], statc[:, 3:4], fA[:],
                                                   AluOpType.mult, AluOpType.add))
            E.dma(outT[oc * 128:(oc + 1) * 128, tsl], fB[:])

    E.finalize(nc, block, sempool)
    es.close()
    return nc



def stage_l0(inputs, b):
    f32 = np.float32
    x = np.asarray(inputs["L0_attn_x"][b], dtype=f32)
    mask, onesb, onesf, ident, onesrowf = _consts_host()
    cos2, sin2 = _rope_tables()
    m = {
        "xT": np.ascontiguousarray(x.T),
        "wqT": np.ascontiguousarray(np.asarray(inputs["L0_attn_Wq"], f32).T),
        "wkT": np.ascontiguousarray(np.asarray(inputs["L0_attn_Wk"], f32).T),
        "wvT": np.ascontiguousarray(np.asarray(inputs["L0_attn_Wv"], f32).T),
        "woT": np.ascontiguousarray(np.asarray(inputs["L0_attn_Wo"], f32).T),
        "bcols": np.concatenate([
            _chunk_cols(np.asarray(inputs["L0_attn_bq"], f32)),
            _chunk_cols(np.asarray(inputs["L0_attn_bk"], f32)),
            _chunk_cols(np.asarray(inputs["L0_attn_bv"], f32)),
            _chunk_cols(np.asarray(inputs["L0_attn_bo"], f32))], axis=1),
        "bvrow": np.ascontiguousarray(np.asarray(inputs["L0_attn_bv"], f32)[None, :]),
        "qg": np.ascontiguousarray(np.asarray(inputs["L0_attn_q_gain"], f32)[None, :]),
        "ropec": cos2, "ropes": sin2,
        "maskc": mask, "onesb": onesb, "identb": ident,
        "onesf": onesf, "onesrowf": onesrowf,
        "routerT": np.ascontiguousarray(np.asarray(inputs["L0_ffn_router_w"], f32).T),
        "sgT": np.ascontiguousarray(np.asarray(inputs["L0_ffn_shared_gate"], f32).T),
        "suT": np.ascontiguousarray(np.asarray(inputs["L0_ffn_shared_up"], f32).T),
        "sdT": np.ascontiguousarray(np.asarray(inputs["L0_ffn_shared_down"], f32).T),
    }
    for e in range(NEXP):
        m[f"wgT{e}"] = np.ascontiguousarray(np.asarray(inputs["L0_ffn_w_gate"][e], f32).T)
        m[f"wuT{e}"] = np.ascontiguousarray(np.asarray(inputs["L0_ffn_w_up"][e], f32).T)
        m[f"wdT{e}"] = np.ascontiguousarray(np.asarray(inputs["L0_ffn_w_down"][e], f32).T)
    return m




def _sel8_host():
    m = np.zeros((8, 8 * 128), dtype=bfdt)
    for e in range(8):
        m[e, e * 128:(e + 1) * 128] = 1.0
    return m


def stage_fused(inputs, b):
    m = stage_l0(inputs, b)
    f32 = np.float32
    m["sel8c"] = _sel8_host()
    m["m1wqT"] = np.ascontiguousarray(np.asarray(inputs["L1_attn_w_q"], f32).T)
    m["m1wkT"] = np.ascontiguousarray(np.asarray(inputs["L1_attn_w_k"], f32).T)
    m["m1wvT"] = np.ascontiguousarray(np.asarray(inputs["L1_attn_w_v"], f32).T)
    m["m1woT"] = np.ascontiguousarray(np.asarray(inputs["L1_attn_w_o"], f32).T)
    m["m1w1T"] = np.ascontiguousarray(np.asarray(inputs["L1_ffn_w1"], f32).T)
    m["m1w2T"] = np.ascontiguousarray(np.asarray(inputs["L1_ffn_w2"], f32).T)
    m["m1w3T"] = np.ascontiguousarray(np.asarray(inputs["L1_ffn_w3"], f32).T)
    return m


def run_fused(inputs, cores=8, dev=False):
    nc = build_fused(dev=dev)
    in_maps = [stage_fused(inputs, b) for b in range(cores)]
    res = run_bass_kernel_spmd(nc, in_maps, core_ids=list(range(cores)))
    return res.results


def kernel(**inputs):
    n = 8
    res = run_fused(inputs, cores=n, dev=False)
    out = np.stack([np.asarray(res[b]["outT"], dtype=np.float32).T for b in range(n)], axis=0)
    return out


# version 05
# speedup vs baseline: 1.4382x; 1.1147x over previous
import numpy as np
import ml_dtypes
import concourse.bass as bass
import concourse.mybir as mybir
from concourse.bass_utils import run_bass_kernel_spmd
from concourse.alu_op_type import AluOpType
from contextlib import ExitStack

F32 = mybir.dt.float32
BF16 = mybir.dt.bfloat16
I32 = mybir.dt.int32
I8 = mybir.dt.int8
AF = mybir.ActivationFunctionType
AX = mybir.AxisListType
bfdt = ml_dtypes.bfloat16

T = 2048
D = 2048
NC = 16
NT4 = 4
NT16 = 16
HD = 128
NH = 16
KVH0 = 4
DFF1 = 8192
NFC = DFF1 // 128
NI = 1408
NIC = 11
NEXP = 8
NEG = -1e30


class Emit:

    def __init__(self):
        self.steps = []

    def _add(self, eng, kind, fn, inc, after, sem):
        self.steps.append((eng, kind, fn, inc, after, sem))
        return len(self.steps) - 1

    def dma(self, dst, src, after=None, sem="Q"):
        return self._add("sync", "dma", lambda e: e.dma_start(dst, src), 16, after, sem)

    def dmag(self, dst, src, after=None, sem="GD"):
        return self._add("gpsimd", "dma", lambda e: e.dma_start(dst, src), 16, after, sem)

    def pe(self, fn, after=None):
        return self._add("tensor", "c", fn, 1, after, "T")

    def dve(self, fn, after=None):
        return self._add("vector", "c", fn, 1, after, "V")

    def act(self, fn, after=None):
        return self._add("scalar", "c", fn, 1, after, "S")

    def gp(self, fn, after=None):
        return self._add("gpsimd", "c", fn, 1, after, "G")

    def finalize(self, nc, block, sempool):
        steps = self.steps
        n = len(steps)
        cumsem = {}
        completion = []
        for eng, kind, fn, inc, after, sem in steps:
            cumsem[sem] = cumsem.get(sem, 0) + inc
            completion.append((sem, cumsem[sem]))
        totals = dict(cumsem)
        running = {}
        targets = [None] * n
        for i, (eng, kind, fn, inc, after, sem) in enumerate(steps):
            if after is None:
                tg = dict(running)
            else:
                toks = after if isinstance(after, (list, tuple)) else [after]
                tg = {}
                for t in toks:
                    if t is None:
                        continue
                    smt, val = completion[t]
                    if tg.get(smt, 0) < val:
                        tg[smt] = val
            targets[i] = tg
            running[sem] = running.get(sem, 0) + inc

        sems = {name: sempool(name) for name in totals}

        def emit(eng_obj, name):
            lastw = {}
            for (eng, kind, fn, inc, after, sem), tg in zip(steps, targets):
                if eng != name:
                    continue
                for smt in tg:
                    val = tg[smt]
                    if val > lastw.get(smt, 0):
                        eng_obj.wait_ge(sems[smt], val)
                        lastw[smt] = val
                fn(eng_obj).then_inc(sems[sem], inc)
            return lastw

        @block.sync
        def _(sync):
            emit(sync, "sync")
            for smt, tot in sorted(totals.items()):
                sync.wait_ge(sems[smt], tot)

        @block.vector
        def _(vector):
            emit(vector, "vector")

        @block.scalar
        def _(scalar):
            emit(scalar, "scalar")

        @block.tensor
        def _(tensor):
            emit(tensor, "tensor")

        @block.gpsimd
        def _(gpsimd):
            emit(gpsimd, "gpsimd")



def _chunk_cols(vec):
    n = vec.shape[0] // 128
    return np.ascontiguousarray(vec.reshape(n, 128).T.astype(np.float32))

def _rope_tables():
    inv_freq = 1.0 / (10000.0 ** (np.arange(0, HD, 2, dtype=np.float64) / HD))
    t = np.arange(T, dtype=np.float64)
    fr = np.outer(t, inv_freq)
    cos = np.cos(fr).T
    sin = np.sin(fr).T
    cos2 = np.vstack([cos, cos]).astype(bfdt)
    sin2 = np.vstack([sin, -sin]).astype(bfdt)
    return np.ascontiguousarray(cos2), np.ascontiguousarray(sin2)

def _consts_host():
    mask = np.where(np.tril(np.ones((128, 128), dtype=bool)), 0.0, NEG).astype(np.float32)
    onesb = np.ones((128, 128), dtype=bfdt)
    onesf = np.ones((128, 1), dtype=np.float32)
    ident = np.eye(128, dtype=bfdt)
    onesrow_f = np.ones((1, 128), dtype=np.float32)
    return mask, onesb, onesf, ident, onesrow_f



def build_fused(dev=False):
    nc = bass.Bass("TRN2", target_bir_lowering=False)
    E = Emit()
    es = ExitStack()

    def din(name, shape, dt=F32):
        return nc.dram_tensor(name, shape, dt, kind="ExternalInput").ap()

    def dout(name, shape, dt=F32):
        return nc.dram_tensor(name, shape, dt, kind="ExternalOutput").ap()

    def dint(name, shape, dt=BF16):
        kind = "ExternalOutput" if dev else "Internal"
        return nc.dram_tensor(name, shape, dt, kind=kind).ap()

    xT = din("xT", [D, T])
    wqT = din("wqT", [D, D])
    wkT = din("wkT", [D, 512])
    wvT = din("wvT", [D, 512])
    woT = din("woT", [D, D])
    bcols = din("bcols", [128, 40])
    bvrow = din("bvrow", [1, 512])
    qg = din("qg", [1, 16])
    ropec = din("ropec", [128, T], BF16)
    ropes = din("ropes", [128, T], BF16)
    maskc = din("maskc", [128, 128])
    onesb_in = din("onesb", [128, 128], BF16)
    identb_in = din("identb", [128, 128], BF16)
    onesf_in = din("onesf", [128, 1])
    onesrowf_in = din("onesrowf", [1, 128])
    sel8_in = din("sel8c", [8, 8 * 128], BF16)
    routerT = din("routerT", [D, 8])
    sgT = din("sgT", [D, NI])
    suT = din("suT", [D, NI])
    sdT = din("sdT", [NI, D])
    wgTs = [din(f"wgT{e}", [D, NI]) for e in range(NEXP)]
    wuTs = [din(f"wuT{e}", [D, NI]) for e in range(NEXP)]
    wdTs = [din(f"wdT{e}", [NI, D]) for e in range(NEXP)]

    outT = dout("outT", [D, T])

    x1T = dint("x1T", [D, T], F32)
    xnT = dint("xnT", [D, T])
    qrawT = dint("qrawT", [D, T])
    QT = dint("QT", [D, T])
    KT = dint("KT", [512, T])
    Vtok = dint("Vtok", [T, 512])
    OT = dint("OT", [D, T])
    hshT = dint("hshT", [NI, T])
    heT = dint("heT", [NI, T])
    accT = dint("accTl0", [D, T], F32)

    sb = lambda name, shape, dt=BF16: es.enter_context(nc.sbuf_tensor(name, shape, dt))
    xns = sb("xns", [128, NC * T])
    wbufA = sb("wbufA", [128, 16 * 512])
    wbufB = sb("wbufB", [128, 16 * 512])
    wbufW = sb("wbufW", [128, 64 * 128])
    wbufC = wbufW
    sA = sb("sA", [128, 2048])
    sB = sb("sB", [128, 2048])
    fA = sb("fA", [128, 512], F32)
    fB = sb("fB", [128, 512], F32)
    ktile = sb("ktile", [128, 2048])
    vtile = sb("vtile", [128, 2048])
    ptile = sb("ptile", [128, 2048])
    sD = ptile
    pttile = sb("pttile", [128, 2048])
    cbs_full = pttile
    qtile = sb("qtile", [128, 256])
    rows = sb("rows", [1, 1536], F32)
    rowsb = sb("rowsb", [1, 2 * T])
    comb8 = sb("comb8", [8, T])
    sel8 = sb("sel8", [8, 8 * 128])
    combtok = sb("combtok", [128, 16 * 8], F32)
    mbs = sb("mbs", [128, 512])
    rbs = sb("rbs", [128, 512])
    cols = sb("cols", [128, 8], F32)
    bvb = sb("bvb", [128, 512])
    maskb = sb("maskb", [128, 128], F32)
    onesb = sb("onesbS", [128, 128])
    identb = sb("identbS", [128, 128])
    onesf = sb("onesfS", [128, 1], F32)
    onesrowf = sb("onesrowfS", [1, 128], F32)
    biasc = sb("biascS", [128, 40], F32)
    qgS = sb("qgS", [1, 16], F32)
    epsc = sb("epsc", [1, 2], F32)
    hbig = sb("hbig", [128, DFF1])
    h8 = sb("h8", [128, DFF1], I8)
    x8 = h8[:, 0:2048]
    fX = sb("fX", [128, 2048], F32)
    scol = sb("scol", [128, 256], F32)
    statc = sb("statc", [128, 16], F32)
    eps128 = sb("eps128S", [128, 1], F32)

    ps = lambda name, shape, dt=F32: es.enter_context(nc.psum_tensor(name, shape, dt))
    pS = ps("pS", [128, 2048])
    pA = ps("pA", [128, 512])
    pB = ps("pB", [128, 512])
    pT = ps("pT", [128, 512], BF16)
    pO = ps("pO", [128, 512])
    pR = pO[0:1, 0:512]

    def sempool(name):
        return es.enter_context(nc.semaphore(f"sp_{name}"))
    block = es.enter_context(nc.Block())

    E.dma(maskb[:], maskc)
    E.dma(onesb[:], onesb_in)
    E.dma(identb[:], identb_in)
    E.dma(onesf[:], onesf_in)
    E.dma(onesrowf[:], onesrowf_in)
    E.dma(biasc[:], bcols)
    E.dma(qgS[:], qg)
    E.gp(lambda e: e.memset(epsc[:, 0:1], 1e-5))
    E.gp(lambda e: e.memset(epsc[:, 1:2], 1.1920929e-07))
    E.gp(lambda e: e.memset(eps128[:], 1e-5))
    E.dma(sel8[:], sel8_in)
    ropec_s = sb("ropecS", [128, T])
    ropes_s = sb("ropesS", [128, T])
    E.dma(ropec_s[:], ropec)
    E.dma(ropes_s[:], ropes)
    E.dma(rows[:, 0:512], bvrow)
    E.dve(lambda e: e.tensor_copy(rowsb[:, 0:512], rows[:, 0:512]))
    E.pe(lambda e: e.matmul(pA[:], onesb[0:1, :], rowsb[0:1, 0:512], start=True, stop=True))
    E.dve(lambda e: e.tensor_copy(bvb[:], pA[:]))

    def wsl(wT_dram, oc, nch, width=128):
        return wT_dram.rearrange("(c p) o -> p c o", p=128)[:, :, oc * width:(oc + 1) * width]

    def bufv(buf, nch, width=128):
        return buf[:, 0:nch * width].rearrange("p (c o) -> p c o", o=width)

    ONE_COL = onesb[:, 0:1]
    ONE_ROW = onesb[0:1, :]

    def bcast_row_bf16(src_row_bf, psum, dst_bf, n):
        E.pe(lambda e: e.matmul(psum[:, 0:n], ONE_ROW, src_row_bf, start=True, stop=True))
        E.dve(lambda e: e.tensor_copy(dst_bf[:, 0:n], psum[:, 0:n]))

    for t4 in range(NT4):
        tsl = slice(t4 * 512, (t4 + 1) * 512)
        for c in range(NC):
            xv = xns[:, c * T + t4 * 512: c * T + (t4 + 1) * 512]
            E.dmag(xv, xT[c * 128:(c + 1) * 128, tsl])
        for c in range(NC):
            xv = xns[:, c * T + t4 * 512: c * T + (t4 + 1) * 512]
            E.pe(lambda e, xv=xv, c=c: e.matmul(pR, ONE_COL, xv, start=(c == 0), stop=(c == NC - 1)))
        E.dve(lambda e: e.tensor_scalar(rows[:, 0:512], pR, 1.0 / D, None, AluOpType.mult))
        for c in range(NC):
            xv = xns[:, c * T + t4 * 512: c * T + (t4 + 1) * 512]
            E.act(lambda e, xv=xv: e.activation(sA[:, 0:512], xv, AF.Square))
            E.pe(lambda e, c=c: e.matmul(pR, ONE_COL, sA[:, 0:512], start=(c == 0), stop=(c == NC - 1)))
        mrow = rows[:, 0:512]
        e2row = rows[:, 512:1024]
        E.dve(lambda e, e2row=e2row: e.tensor_scalar(e2row, pR, 1.0 / D, None, AluOpType.mult))
        msq = rows[:, 1024:1536]
        E.dve(lambda e, mrow=mrow, msq=msq: e.tensor_tensor(msq, mrow, mrow, AluOpType.mult))
        E.dve(lambda e, e2row=e2row, msq=msq: e.tensor_tensor(e2row, e2row, msq, AluOpType.subtract))
        E.act(lambda e, e2row=e2row: e.activation(e2row, e2row, AF.Ln, bias=epsc[:, 0:1]))
        E.act(lambda e, e2row=e2row: e.activation(e2row, e2row, AF.Exp, scale=-0.5))
        E.dve(lambda e, mrow=mrow, t4=t4: e.tensor_copy(rowsb[:, t4 * 512:(t4 + 1) * 512], mrow))
        E.dve(lambda e, e2row=e2row, t4=t4: e.tensor_copy(rowsb[:, T + t4 * 512: T + (t4 + 1) * 512], e2row))
    for t4 in range(NT4):
        bcast_row_bf16(rowsb[0:1, t4 * 512:(t4 + 1) * 512], pA, mbs, 512)
        bcast_row_bf16(rowsb[0:1, T + t4 * 512: T + (t4 + 1) * 512], pB, rbs, 512)
        for c in range(NC):
            xv = xns[:, c * T + t4 * 512: c * T + (t4 + 1) * 512]
            E.dve(lambda e, xv=xv: e.tensor_tensor(sA[:, 0:512], xv, mbs[:], AluOpType.subtract))
            E.dve(lambda e, xv=xv: e.tensor_tensor(xv, sA[:, 0:512], rbs[:], AluOpType.mult))
            if dev:
                E.dma(xnT[c * 128:(c + 1) * 128, t4 * 512:(t4 + 1) * 512], xv)

    def proj(dst_dram, wT_dram, out_chunks, bias_off=None, dst_scale=None):
        for oc in range(out_chunks):
            E.dmag(bufv(wbufA, NC), wsl(wT_dram, oc, NC))
            for t4 in range(NT4):
                for c in range(NC):
                    xv = xns[:, c * T + t4 * 512: c * T + (t4 + 1) * 512]
                    E.pe(lambda e, c=c, xv=xv: e.matmul(
                        pA[:], wbufA[:, c * 128:(c + 1) * 128], xv,
                        start=(c == 0), stop=(c == NC - 1)))
                if bias_off is not None:
                    bc = biasc[:, bias_off + oc: bias_off + oc + 1]
                    E.dve(lambda e, bc=bc: e.tensor_scalar(sA[:, 0:512], pA[:], bc, None, AluOpType.add))
                else:
                    E.dve(lambda e: e.tensor_copy(sA[:, 0:512], pA[:]))
                E.dma(dst_dram[oc * 128:(oc + 1) * 128, t4 * 512:(t4 + 1) * 512], sA[:, 0:512])

    proj(qrawT, wqT, 16, bias_off=0)

    kraw = dint("krawT", [512, T])
    proj(kraw, wkT, 4, bias_off=16)

    SCL = float(1.0 / np.sqrt(HD))

    def rms_rope(src_dram, dst_dram, n_heads, head_of, gain_fold):
        for h in range(n_heads):
            hc = head_of(h)
            E.dma(sA[:], src_dram[hc * 128:(hc + 1) * 128, :])
            for t4 in range(NT4):
                tsl = slice(t4 * 512, (t4 + 1) * 512)
                E.act(lambda e, tsl=tsl: e.activation(sB[:, 0:512], sA[:, tsl], AF.Square))
                E.pe(lambda e: e.matmul(pR, ONE_COL, sB[:, 0:512], start=True, stop=True))
                rr = rows[:, 0:512]
                E.act(lambda e, rr=rr: e.activation(rr, pR, AF.Ln, bias=epsc[:, 1:2], scale=1.0 / HD))
                E.act(lambda e, rr=rr: e.activation(rr, rr, AF.Exp, scale=-0.5))
                if gain_fold is not None:
                    gf = gain_fold(h)
                    E.dve(lambda e, rr=rr, gf=gf: e.tensor_scalar(rr, rr, gf, SCL, AluOpType.mult, AluOpType.mult))
                E.dve(lambda e, rr=rr, t4=t4: e.tensor_copy(rowsb[:, t4 * 512:(t4 + 1) * 512], rr))
            E.dma(sD[0:64, :], sA[64:128, :])
            E.dma(sD[64:128, :], sA[0:64, :])
            E.dve(lambda e: e.tensor_tensor(sB[:], sA[:], ropec_s[:], AluOpType.mult))
            E.dve(lambda e: e.tensor_tensor(sD[:], sD[:], ropes_s[:], AluOpType.mult))
            E.dve(lambda e: e.tensor_tensor(sB[:], sB[:], sD[:], AluOpType.add))
            for t4 in range(NT4):
                tsl = slice(t4 * 512, (t4 + 1) * 512)
                bcast_row_bf16(rowsb[0:1, t4 * 512:(t4 + 1) * 512], pA, rbs, 512)
                E.dve(lambda e, tsl=tsl: e.tensor_tensor(sB[:, tsl], sB[:, tsl], rbs[:], AluOpType.mult))
                E.dma(dst_dram[hc * 128:(hc + 1) * 128, tsl], sB[:, tsl])

    rms_rope(qrawT, QT, 16, lambda h: h,
             gain_fold=lambda h: qgS[0:1, h:h + 1])
    rms_rope(kraw, KT, 4, lambda h: h, gain_fold=None)

    E.dmag(bufv(wbufC, NC, 512), wsl(wvT, 0, NC, 512))
    for t16 in range(NT16):
        for c in range(NC):
            lh = xns[:, c * T + t16 * 128: c * T + (t16 + 1) * 128]
            rh = wbufC[:, c * 512:(c + 1) * 512]
            E.pe(lambda e, lh=lh, rh=rh, c=c: e.matmul(pA[:], lh, rh, start=(c == 0), stop=(c == NC - 1)))
        E.dve(lambda e: e.tensor_copy(sA[:, 0:512], pA[:]))
        E.dve(lambda e: e.tensor_tensor(sA[:, 0:512], sA[:, 0:512], bvb[:], AluOpType.add))
        E.dma(Vtok[t16 * 128:(t16 + 1) * 128, :], sA[:, 0:512])

    E.dmag(bufv(wbufA, NC, 8), wsl(routerT, 0, NC, 8))
    for t16 in range(NT16):
        for c in range(NC):
            lh = xns[:, c * T + t16 * 128: c * T + (t16 + 1) * 128]
            E.pe(lambda e, lh=lh, c=c: e.matmul(pA[:, 0:8], lh, wbufA[:, c * 8:(c + 1) * 8],
                                                start=(c == 0), stop=(c == NC - 1)))
        E.dve(lambda e: e.tensor_reduce(cols[:, 0:1], pA[:, 0:8], AX.X, AluOpType.max))
        E.dve(lambda e: e.tensor_scalar(cols[:, 1:2], cols[:, 0:1], -1.0, None, AluOpType.mult))
        def _exp_router(e):
            e.activation(fA[:, 0:8], pA[:, 0:8], AF.Exp, bias=cols[:, 1:2], accum_out=cols[:, 2:3])
            return e.copy(cols[:, 6:7], cols[:, 2:3])
        E.act(_exp_router)
        E.dve(lambda e: e.reciprocal(cols[:, 3:4], cols[:, 2:3]))
        E.dve(lambda e: e.tensor_scalar(fA[:, 0:8], fA[:, 0:8], cols[:, 3:4], None, AluOpType.mult))
        E.dve(lambda e: e.tensor_reduce(cols[:, 4:5], fA[:, 0:8], AX.X, AluOpType.max))
        E.dve(lambda e: e.tensor_scalar(fB[:, 0:8], fA[:, 0:8], cols[:, 4:5], None, AluOpType.is_ge))
        E.dve(lambda e: e.scalar_tensor_tensor(fB[:, 0:8], fB[:, 0:8], NEG, fA[:, 0:8], AluOpType.mult, AluOpType.add))
        E.dve(lambda e: e.tensor_reduce(cols[:, 5:6], fB[:, 0:8], AX.X, AluOpType.max))
        E.dve(lambda e: e.tensor_scalar(fB[:, 0:8], fA[:, 0:8], cols[:, 5:6], None, AluOpType.is_ge))
        E.dve(lambda e, t16=t16: e.tensor_tensor(combtok[:, t16 * 8:(t16 + 1) * 8], fA[:, 0:8], fB[:, 0:8], AluOpType.mult))
    for t16 in range(NT16):
        E.dve(lambda e, t16=t16: e.tensor_copy(sA[:, 0:8], combtok[:, t16 * 8:(t16 + 1) * 8]))
        E.pe(lambda e: e.transpose(pT[0:8, 0:128], sA[:, 0:8], identb[:]))
        E.dve(lambda e, t16=t16: e.tensor_copy(comb8[0:8, t16 * 128:(t16 + 1) * 128], pT[0:8, 0:128]))

    pSa = pS[:, 0:512]
    pSb = pS[:, 512:1024]
    acc_store_tok = {}

    def gated_ffn(gT, uT, dst_dram, comb_e=None, tag=""):
        cb_tok = None
        if comb_e is not None:
            for t4 in range(NT4):
                E.pe(lambda e, t4=t4, comb_e=comb_e: e.matmul(
                    pO[:], sel8[:, comb_e * 128:(comb_e + 1) * 128],
                    comb8[0:8, t4 * 512:(t4 + 1) * 512], start=True, stop=True))
                cb_tok = E.dve(lambda e, t4=t4: e.tensor_copy(cbs_full[:, t4 * 512:(t4 + 1) * 512], pO[:]))
        pArr = [pA, pSa]
        pBrr = [pB, pSb]
        wlast = [None, None]
        pAev = [None, None]
        pBev = [None, None]
        sBst = [None, None]
        first = True
        it = 0
        for oc in range(NIC):
            pw = oc % 2
            wa_t = E.dmag(bufv(wbufA[:, pw * 2048:(pw + 1) * 2048], NC), wsl(gT, oc, NC),
                          after=None if first else [wlast[pw]], sem=f"wa{pw}")
            wb_t = E.dmag(bufv(wbufB[:, pw * 2048:(pw + 1) * 2048], NC), wsl(uT, oc, NC),
                          after=None if first else [wlast[pw]], sem=f"wb{pw}")
            first = False
            for t4 in range(NT4):
                pp = it % 2
                mmA = None
                for c in range(NC):
                    xv = xns[:, c * T + t4 * 512: c * T + (t4 + 1) * 512]
                    mmA = E.pe(lambda e, c=c, xv=xv, pp=pp, pw=pw: e.matmul(
                        pArr[pp][:], wbufA[:, pw * 2048 + c * 128: pw * 2048 + (c + 1) * 128], xv,
                        start=(c == 0), stop=(c == NC - 1)),
                        after=([wa_t, pAev[pp], cb_tok] if c == 0 else []))
                mmB = None
                for c in range(NC):
                    xv = xns[:, c * T + t4 * 512: c * T + (t4 + 1) * 512]
                    mmB = E.pe(lambda e, c=c, xv=xv, pp=pp, pw=pw: e.matmul(
                        pBrr[pp][:], wbufB[:, pw * 2048 + c * 128: pw * 2048 + (c + 1) * 128], xv,
                        start=(c == 0), stop=(c == NC - 1)),
                        after=([wb_t, pBev[pp]] if c == 0 else []))
                sg = E.act(lambda e, pp=pp: e.activation(sA[:, pp * 512:(pp + 1) * 512], pArr[pp][:], AF.Sigmoid),
                           after=[mmA])
                tt = E.dve(lambda e, pp=pp: e.tensor_tensor(sB[:, pp * 512:(pp + 1) * 512],
                                                            sA[:, pp * 512:(pp + 1) * 512], pBrr[pp][:],
                                                            AluOpType.mult), after=[sg, mmB, sBst[pp]])
                ttl = tt
                if comb_e is not None:
                    ttl = E.dve(lambda e, pp=pp, t4=t4: e.tensor_tensor(
                        sB[:, pp * 512:(pp + 1) * 512], sB[:, pp * 512:(pp + 1) * 512],
                        cbs_full[:, t4 * 512:(t4 + 1) * 512], AluOpType.mult), after=[ttl])
                st = E.dma(dst_dram[oc * 128:(oc + 1) * 128, t4 * 512:(t4 + 1) * 512],
                           sB[:, pp * 512:(pp + 1) * 512], after=[ttl], sem=f"hs{pp}")
                wlast[pw] = mmB
                pAev[pp] = sg
                pBev[pp] = tt
                sBst[pp] = st
                it += 1

    def down_proj(dT, src_dram, first_flag):
        pArr = [pA, pSa]
        wlast = [None, None]
        pAev = [None, None]
        fev = [None, None]
        he_last_mm = None
        first = True
        it = 0
        for t4 in range(NT4):
            tsl = slice(t4 * 512, (t4 + 1) * 512)
            he_toks = []
            for ic in range(NIC):
                he_toks.append(E.dma(wbufW[:, ic * 512:(ic + 1) * 512],
                                     src_dram[ic * 128:(ic + 1) * 128, tsl],
                                     after=None if first else [he_last_mm], sem="he"))
                first = False
            for oc in range(NC):
                pw = oc % 2
                pp = it % 2
                wd_t = E.dmag(bufv(wbufA[:, pw * 2048:(pw + 1) * 2048], NIC), wsl(dT, oc, NIC),
                              after=[wlast[pw]] if wlast[pw] is not None else [he_toks[0]],
                              sem=f"wd{pw}")
                mm = None
                for ic in range(NIC):
                    mm = E.pe(lambda e, ic=ic, pp=pp, pw=pw: e.matmul(
                        pArr[pp][:], wbufA[:, pw * 2048 + ic * 128: pw * 2048 + (ic + 1) * 128],
                        wbufW[:, ic * 512:(ic + 1) * 512],
                        start=(ic == 0), stop=(ic == NIC - 1)),
                        after=(he_toks + [wd_t, pAev[pp]] if ic == 0 else []))
                fbuf = fA if pp == 0 else fB
                prev_store = acc_store_tok.get((oc, t4))
                war = [x for x in [fev[pp], prev_store] if x is not None]
                if first_flag:
                    ld = E.dma(fbuf[:], xT[oc * 128:(oc + 1) * 128, tsl], after=war)
                    xnv = xns[:, oc * T + t4 * 512: oc * T + (t4 + 1) * 512]
                    a1 = E.dve(lambda e, fbuf=fbuf, xnv=xnv: e.tensor_tensor(fbuf[:], fbuf[:], xnv, AluOpType.add),
                               after=[ld])
                else:
                    ld = E.dma(fbuf[:], accT[oc * 128:(oc + 1) * 128, tsl], after=war)
                    a1 = ld
                a2 = E.dve(lambda e, fbuf=fbuf, pp=pp: e.tensor_tensor(fbuf[:], fbuf[:], pArr[pp][:], AluOpType.add),
                           after=[a1, mm])
                st = E.dma(accT[oc * 128:(oc + 1) * 128, tsl], fbuf[:], after=[a2], sem=f"as{pp}")
                acc_store_tok[(oc, t4)] = st
                wlast[pw] = mm
                pAev[pp] = a2
                fev[pp] = st
                he_last_mm = mm
                it += 1

    gated_ffn(sgT, suT, hshT, comb_e=None)
    down_proj(sdT, hshT, first_flag=True)
    for e_i in range(NEXP):
        gated_ffn(wgTs[e_i], wuTs[e_i], heT, comb_e=e_i)
        down_proj(wdTs[e_i], heT, first_flag=False)


    def emit_attention(QTd, KTd, Vtokd, OTd, hpg, ngr):
        kbufs = [ktile, hbig[:, 0:2048]]
        vbufs = [vtile, hbig[:, 2048:4096]]
        kt_tok = [None, None]
        vt_tok = [None, None]
        klast = [None, None]
        vlast = [None, None]
        qS_war = [None, None]
        prev_store = [None, None]
        prev_exp = None
        pO_ev = None
        last_copy_tok = None
        last_tr_tok = None
        pend = [None]
        hctr = [0]

        def flush_pend():
            nonlocal pO_ev, last_copy_tok, last_tr_tok
            p = pend[0]
            if p is None:
                return
            nblk = p["qt"] + 1
            first_after = [p["norm"], last_copy_tok]
            ct = None
            t_ = None
            for kb in range(nblk):
                sl4 = (kb % 4) * 128
                a = first_after if kb % 4 == 0 else []
                t_ = E.pe(lambda e, kb=kb, sl4=sl4: e.transpose(
                    pT[:, sl4:sl4 + 128], ptile[:, kb * 128:(kb + 1) * 128], identb[:]), after=a)
                if kb % 4 == 3 or kb == nblk - 1:
                    base = (kb // 4) * 512
                    wdt = (kb % 4 + 1) * 128
                    ct = E.dve(lambda e, base=base, wdt=wdt: e.tensor_copy(
                        pttile[:, base:base + wdt], pT[:, 0:wdt]), after=[t_])
                    first_after = [ct]
            last_copy_tok = ct
            last_tr_tok = t_
            om = None
            gp = p["gp"]
            for kb in range(nblk):
                a = [ct, vt_tok[gp], pO_ev] if kb == 0 else []
                om = E.pe(lambda e, kb=kb, gp=gp: e.matmul(
                    pO[:, 0:128], vbufs[gp][:, kb * 128:(kb + 1) * 128],
                    pttile[:, kb * 128:(kb + 1) * 128],
                    start=(kb == 0), stop=(kb == nblk - 1)), after=a)
            vlast[gp] = om
            sp = p["sp"]
            ev = E.dve(lambda e, sp=sp: e.tensor_copy(sA[:, sp * 128:(sp + 1) * 128], pO[:, 0:128]),
                       after=[om, prev_store[sp]])
            st = E.dma(OTd[p["h"] * 128:(p["h"] + 1) * 128, p["qt"] * 128:(p["qt"] + 1) * 128],
                       sA[:, sp * 128:(sp + 1) * 128], after=[ev], sem=f"os{sp}")
            pO_ev = ev
            prev_store[sp] = st
            pend[0] = None

        first_load = True
        for qt in range(NT16):
            nk = (qt + 1) * 128
            nk4 = (nk + 511) // 512
            for g in range(ngr):
                gp = (qt * ngr + g) % 2
                kl = E.dma(kbufs[gp][:, 0:nk], KTd[g * 128:(g + 1) * 128, 0:nk],
                           after=None if first_load else [klast[gp]], sem=f"kt{gp}")
                vl = E.dma(vbufs[gp][:, 0:nk].rearrange("p (b d) -> p b d", d=128),
                           Vtokd[0:nk, g * 128:(g + 1) * 128].rearrange("(b p) d -> p b d", p=128),
                           after=None if first_load else [vlast[gp]], sem=f"vt{gp}")
                first_load = False
                kt_tok[gp] = kl
                vt_tok[gp] = vl
                for hh in range(hpg):
                    h = g * hpg + hh
                    qp = hctr[0] % 2
                    ql = E.dma(qtile[:, qp * 128:(qp + 1) * 128],
                               QTd[h * 128:(h + 1) * 128, qt * 128:(qt + 1) * 128],
                               after=[qS_war[qp]] if qS_war[qp] is not None else [kl],
                               sem=f"qs{qp}")
                    sm = None
                    for kb4 in range(nk4):
                        w = min(512, nk - kb4 * 512)
                        a = [ql, kl, prev_exp] if kb4 == 0 else []
                        sm = E.pe(lambda e, kb4=kb4, w=w, qp=qp, gp=gp: e.matmul(
                            pS[:, kb4 * 512: kb4 * 512 + w], qtile[:, qp * 128:(qp + 1) * 128],
                            kbufs[gp][:, kb4 * 512: kb4 * 512 + w],
                            start=True, stop=True), after=a)
                    qS_war[qp] = sm
                    klast[gp] = sm
                    mk = E.dve(lambda e, qt=qt: e.tensor_tensor(pS[:, qt * 128:(qt + 1) * 128],
                                                                pS[:, qt * 128:(qt + 1) * 128],
                                                                maskb[:], AluOpType.add), after=[sm])
                    rd = E.dve(lambda e, nk=nk: e.tensor_reduce(cols[:, 0:1], pS[:, 0:nk], AX.X,
                                                                AluOpType.max), after=[mk])
                    ng = E.dve(lambda e: e.tensor_scalar(cols[:, 1:2], cols[:, 0:1], -1.0, None,
                                                         AluOpType.mult), after=[rd])
                    flush_pend()
                    def _expf(e, nk=nk):
                        e.activation(ptile[:, 0:nk], pS[:, 0:nk], AF.Exp,
                                     bias=cols[:, 1:2], accum_out=cols[:, 2:3])
                        return e.copy(cols[:, 6:7], cols[:, 2:3])
                    ex = E.act(_expf, after=[ng, last_tr_tok])
                    rc = E.dve(lambda e: e.reciprocal(cols[:, 3:4], cols[:, 2:3]), after=[ex])
                    nm = E.dve(lambda e, nk=nk: e.tensor_scalar(ptile[:, 0:nk], ptile[:, 0:nk],
                                                                cols[:, 3:4], None, AluOpType.mult),
                               after=[rc])
                    prev_exp = ex
                    pend[0] = {"qt": qt, "h": h, "gp": gp, "sp": hctr[0] % 2, "norm": nm}
                    hctr[0] += 1
        flush_pend()

    emit_attention(QT, KT, Vtok, OT, 4, KVH0)

    for oc in range(NC):
        E.dmag(bufv(wbufA, NC), wsl(woT, oc, NC))
        for t4 in range(NT4):
            tsl = slice(t4 * 512, (t4 + 1) * 512)
            for c in range(NC):
                E.dma(sA[:, 0:512], OT[c * 128:(c + 1) * 128, tsl])
                E.pe(lambda e, c=c: e.matmul(pA[:], wbufA[:, c * 128:(c + 1) * 128], sA[:, 0:512],
                                             start=(c == 0), stop=(c == NC - 1)))
            E.dma(fA[:], accT[oc * 128:(oc + 1) * 128, tsl])
            bc = biasc[:, 24 + oc: 24 + oc + 1]
            E.dve(lambda e, bc=bc: e.scalar_tensor_tensor(fB[:], pA[:], bc, fA[:], AluOpType.add, AluOpType.add))
            E.dma(x1T[oc * 128:(oc + 1) * 128, tsl], fB[:])


    m1wqT = din("m1wqT", [D, D])
    m1wkT = din("m1wkT", [D, D])
    m1wvT = din("m1wvT", [D, D])
    m1woT = din("m1woT", [D, D])
    m1w1T = din("m1w1T", [D, DFF1])
    m1w2T = din("m1w2T", [D, DFF1])
    m1w3T = din("m1w3T", [DFF1, D])

    xqAtok = dint("xqAtok", [T, D])
    xqFtok = dint("xqFtok", [T, D])
    xqAT = dint("xqAT", [D, T])
    xqFT = dint("xqFT", [D, T])
    wq_q = dint("wq_q", [D, D])
    wk_q = dint("wk_q", [D, D])
    wv_q = dint("wv_q", [D, D])
    wo_q = dint("wo_q", [D, D])
    w1_q = dint("w1_q", [D, DFF1])
    w2_q = dint("w2_q", [D, DFF1])
    w3_q = dint("w3_q", [DFF1, D])
    QT1 = dint("QT1", [D, T])
    KT1 = dint("KT1", [D, T])
    Vtok1 = dint("Vtok1", [T, D])
    OT1 = dint("OT1", [D, T])
    Oqtok = dint("Oqtok", [T, D])
    OqT = dint("OqT", [D, T])
    htok = dint("htok", [T, DFF1])
    hqT = dint("hqT", [DFF1, T])
    accT1 = dint("accTl1", [D, T], F32)
    p11 = pO[0:1, 0:1]

    for t4 in range(NT4):
        tsl = slice(t4 * 512, (t4 + 1) * 512)
        for c in range(NC):
            xv = xns[:, c * T + t4 * 512: c * T + (t4 + 1) * 512]
            E.dmag(xv, x1T[c * 128:(c + 1) * 128, tsl])
        for c in range(NC):
            xv = xns[:, c * T + t4 * 512: c * T + (t4 + 1) * 512]
            E.pe(lambda e, xv=xv, c=c: e.matmul(pR, ONE_COL, xv, start=(c == 0), stop=(c == NC - 1)))
        E.dve(lambda e: e.tensor_scalar(rows[:, 0:512], pR, 1.0 / D, None, AluOpType.mult))
        for c in range(NC):
            xv = xns[:, c * T + t4 * 512: c * T + (t4 + 1) * 512]
            E.act(lambda e, xv=xv: e.activation(sA[:, 0:512], xv, AF.Square))
            E.pe(lambda e, c=c: e.matmul(pR, ONE_COL, sA[:, 0:512], start=(c == 0), stop=(c == NC - 1)))
        mrow = rows[:, 0:512]
        e2row = rows[:, 512:1024]
        msq = rows[:, 1024:1536]
        E.dve(lambda e, e2row=e2row: e.tensor_scalar(e2row, pR, 1.0 / D, None, AluOpType.mult))
        E.dve(lambda e, mrow=mrow, msq=msq: e.tensor_tensor(msq, mrow, mrow, AluOpType.mult))
        E.dve(lambda e, e2row=e2row, msq=msq: e.tensor_tensor(e2row, e2row, msq, AluOpType.subtract))
        E.act(lambda e, e2row=e2row: e.activation(e2row, e2row, AF.Ln, bias=epsc[:, 0:1]))
        E.act(lambda e, e2row=e2row: e.activation(e2row, e2row, AF.Exp, scale=-0.5))
        E.dve(lambda e, mrow=mrow, t4=t4: e.tensor_copy(rowsb[:, t4 * 512:(t4 + 1) * 512], mrow))
        E.dve(lambda e, e2row=e2row, t4=t4: e.tensor_copy(rowsb[:, T + t4 * 512: T + (t4 + 1) * 512], e2row))
    for t4 in range(NT4):
        bcast_row_bf16(rowsb[0:1, t4 * 512:(t4 + 1) * 512], pA, mbs, 512)
        bcast_row_bf16(rowsb[0:1, T + t4 * 512: T + (t4 + 1) * 512], pB, rbs, 512)
        for c in range(NC):
            xv = xns[:, c * T + t4 * 512: c * T + (t4 + 1) * 512]
            E.dve(lambda e, xv=xv: e.tensor_tensor(sA[:, 0:512], xv, mbs[:], AluOpType.subtract))
            E.dve(lambda e, xv=xv: e.tensor_tensor(xv, sA[:, 0:512], rbs[:], AluOpType.mult))

    f3 = lambda t: t.rearrange("p (g i) -> p g i", i=64)
    for t16 in range(NT16):
        rsl = slice(t16 * 128, (t16 + 1) * 128)
        for c in range(NC):
            sl4 = (c % 4) * 128
            E.pe(lambda e, c=c, t16=t16, sl4=sl4: e.transpose(
                pT[:, sl4:sl4 + 128], xns[:, c * T + t16 * 128: c * T + (t16 + 1) * 128], identb[:]))
            if c % 4 == 3:
                base = (c // 4) * 512
                E.dve(lambda e, base=base: e.tensor_copy(sA[:, base:base + 512], pT[:, 0:512]))
        E.dve(lambda e: e.tensor_reduce(cols[:, 3:4], sA[:], AX.X, AluOpType.max, apply_absolute_value=True))
        E.dve(lambda e: e.tensor_scalar(cols[:, 3:4], cols[:, 3:4], 1e-5, None, AluOpType.max))
        E.dve(lambda e: e.reciprocal(cols[:, 4:5], cols[:, 3:4]))
        E.dve(lambda e: e.tensor_scalar(x8, sA[:], cols[:, 4:5], 127.0, AluOpType.mult, AluOpType.mult))
        E.dve(lambda e: e.tensor_scalar(sB[:], x8, cols[:, 3:4], 1.0 / 127.0, AluOpType.mult, AluOpType.mult))
        E.dma(xqAtok[rsl, :], sB[:])
        E.dve(lambda e: e.tensor_reduce(scol[:, 0:32], f3(sA[:]), AX.X, AluOpType.max, apply_absolute_value=True))
        E.dve(lambda e: e.tensor_scalar(scol[:, 0:32], scol[:, 0:32], 1e-5, None, AluOpType.max))
        E.dve(lambda e: e.reciprocal(scol[:, 32:64], scol[:, 0:32]))
        E.dve(lambda e: e.tensor_scalar(scol[:, 32:64], scol[:, 32:64], 127.0, None, AluOpType.mult))
        E.dve(lambda e: e.tensor_tensor(f3(x8), f3(sA[:]),
                                        scol[:, 32:64].unsqueeze(2).broadcast_to([128, 32, 64]), AluOpType.mult))
        E.dve(lambda e: e.tensor_scalar(scol[:, 64:96], scol[:, 0:32], 1.0 / 127.0, None, AluOpType.mult))
        E.dve(lambda e: e.tensor_tensor(f3(sB[:]), f3(x8),
                                        scol[:, 64:96].unsqueeze(2).broadcast_to([128, 32, 64]), AluOpType.mult))
        E.dma(xqFtok[rsl, :], sB[:])

    def transpose_tok_to_fm(src_tok, dst_T):
        for t16 in range(NT16):
            E.dma(sA[:], src_tok[t16 * 128:(t16 + 1) * 128, :])
            for c in range(NC):
                sl4 = (c % 4) * 128
                E.pe(lambda e, c=c, sl4=sl4: e.transpose(pT[:, sl4:sl4 + 128], sA[:, c * 128:(c + 1) * 128], identb[:]))
                if c % 4 == 3:
                    base = (c // 4) * 512
                    E.dve(lambda e, base=base: e.tensor_copy(sB[:, base:base + 512], pT[:, 0:512]))
            for c in range(NC):
                E.dma(dst_T[c * 128:(c + 1) * 128, t16 * 128:(t16 + 1) * 128], sB[:, c * 128:(c + 1) * 128])

    transpose_tok_to_fm(xqAtok, xqAT)
    transpose_tok_to_fm(xqFtok, xqFT)

    def weight_stats2(i, wT_dram, numel, clip_eps):
        rows_n = wT_dram.shape[0]
        wT3 = wT_dram.rearrange("(c p) o -> p c o", p=128)
        nco = wT_dram.shape[1] // 2048
        E.gp(lambda e: e.memset(scol[:, 96:97], 0.0))
        for c in range(rows_n // 128):
            for ob in range(nco):
                E.dma(fX[:], wT3[:, c, ob * 2048:(ob + 1) * 2048])
                def _abs(e):
                    e.activation(fX[:], fX[:], AF.Abs, accum_out=cols[:, 7:8])
                    return e.copy(cols[:, 6:7], cols[:, 7:8])
                E.act(_abs)
                E.dve(lambda e: e.tensor_tensor(scol[:, 96:97], scol[:, 96:97], cols[:, 7:8], AluOpType.add))
        E.pe(lambda e: e.matmul(p11, scol[:, 96:97], onesf[:], start=True, stop=True))
        E.dve(lambda e, i=i, numel=numel: e.tensor_scalar(statc[0:1, 8 + i:9 + i], p11, 1.0 / numel, None, AluOpType.mult))
        if clip_eps is not None:
            E.dve(lambda e, i=i, clip_eps=clip_eps: e.tensor_scalar(
                statc[0:1, 8 + i:9 + i], statc[0:1, 8 + i:9 + i], clip_eps, None, AluOpType.max))
        E.pe(lambda e, i=i: e.matmul(pO[:, 0:1], onesrowf[0:1, :], statc[0:1, 8 + i:9 + i], start=True, stop=True))
        E.dve(lambda e, i=i: e.tensor_copy(statc[:, i:i + 1], pO[:, 0:1]))
        E.dve(lambda e, i=i: e.reciprocal(statc[:, 8 + i:9 + i], statc[:, i:i + 1]))

    WSTATS = [(m1wqT, D * D), (m1wkT, D * D), (m1wvT, D * D), (m1woT, D * D),
              (m1w1T, D * DFF1), (m1w2T, D * DFF1), (m1w3T, DFF1 * D)]
    for i, (wt, ne) in enumerate(WSTATS):
        weight_stats2(i, wt, ne, 1e-5 if i < 4 else None)

    def quant_weight_ternary(i, wT_dram, dst):
        rows_n = wT_dram.shape[0]
        wT3 = wT_dram.rearrange("(c p) o -> p c o", p=128)
        dst3 = dst.rearrange("(c p) o -> p c o", p=128)
        nco = wT_dram.shape[1] // 2048
        for c in range(rows_n // 128):
            for ob in range(nco):
                E.dma(fX[:], wT3[:, c, ob * 2048:(ob + 1) * 2048])
                E.dve(lambda e, i=i: e.tensor_scalar(fX[:], fX[:], statc[:, 8 + i:9 + i], 1.0, AluOpType.mult, AluOpType.min))
                E.dve(lambda e: e.tensor_scalar(x8, fX[:], -1.0, None, AluOpType.max))
                E.dve(lambda e: e.tensor_copy(sA[:], x8))
                E.dma(dst3[:, c, ob * 2048:(ob + 1) * 2048], sA[:])

    def quant_weight_sign(wT_dram, dst):
        rows_n = wT_dram.shape[0]
        wT3 = wT_dram.rearrange("(c p) o -> p c o", p=128)
        dst3 = dst.rearrange("(c p) o -> p c o", p=128)
        nco = wT_dram.shape[1] // 2048
        for c in range(rows_n // 128):
            for ob in range(nco):
                E.dma(fX[:], wT3[:, c, ob * 2048:(ob + 1) * 2048])
                E.act(lambda e: e.activation(sA[:], fX[:], AF.Sign))
                E.dma(dst3[:, c, ob * 2048:(ob + 1) * 2048], sA[:])

    quant_weight_ternary(0, m1wqT, wq_q)
    quant_weight_ternary(1, m1wkT, wk_q)
    quant_weight_ternary(2, m1wvT, wv_q)
    quant_weight_ternary(3, m1woT, wo_q)
    quant_weight_sign(m1w1T, w1_q)
    quant_weight_sign(m1w2T, w2_q)
    quant_weight_sign(m1w3T, w3_q)

    for c in range(NC):
        E.dma(xns[:, c * T:(c + 1) * T], xqAT[c * 128:(c + 1) * 128, :])

    def proj_l1(dst_dram, wq_dram, dq_i, extra_scale):
        for oc in range(NC):
            E.dma(bufv(wbufA, NC), wsl(wq_dram, oc, NC))
            for t4 in range(NT4):
                for c in range(NC):
                    xv = xns[:, c * T + t4 * 512: c * T + (t4 + 1) * 512]
                    E.pe(lambda e, c=c, xv=xv: e.matmul(pA[:], wbufA[:, c * 128:(c + 1) * 128], xv,
                                                        start=(c == 0), stop=(c == NC - 1)))
                E.dve(lambda e, dq_i=dq_i, extra_scale=extra_scale: e.tensor_scalar(
                    sA[:, 0:512], pA[:], statc[:, dq_i:dq_i + 1], extra_scale, AluOpType.mult, AluOpType.mult))
                E.dma(dst_dram[oc * 128:(oc + 1) * 128, t4 * 512:(t4 + 1) * 512], sA[:, 0:512])

    proj_l1(QT1, wq_q, 0, SCL)
    proj_l1(KT1, wk_q, 1, 1.0)

    for vb in range(4):
        E.dma(bufv(wbufA, NC, 512), wsl(wv_q, vb, NC, 512))
        for t16 in range(NT16):
            for c in range(NC):
                lh = xns[:, c * T + t16 * 128: c * T + (t16 + 1) * 128]
                rh = wbufA[:, c * 512:(c + 1) * 512]
                E.pe(lambda e, lh=lh, rh=rh, c=c: e.matmul(pA[:], lh, rh, start=(c == 0), stop=(c == NC - 1)))
            E.dve(lambda e: e.tensor_scalar(sA[:, 0:512], pA[:], statc[:, 2:3], None, AluOpType.mult))
            E.dma(Vtok1[t16 * 128:(t16 + 1) * 128, vb * 512:(vb + 1) * 512], sA[:, 0:512])

    for c in range(NC):
        E.dma(xns[:, c * T:(c + 1) * T], xqFT[c * 128:(c + 1) * 128, :])

    for ffb in range(16):
        E.dma(bufv(wbufA, NC, 512), wsl(w1_q, ffb, NC, 512))
        E.dma(bufv(wbufB, NC, 512), wsl(w2_q, ffb, NC, 512))
        for t16 in range(NT16):
            for c in range(NC):
                lh = xns[:, c * T + t16 * 128: c * T + (t16 + 1) * 128]
                E.pe(lambda e, lh=lh, c=c: e.matmul(pA[:], lh, wbufA[:, c * 512:(c + 1) * 512],
                                                    start=(c == 0), stop=(c == NC - 1)))
            for c in range(NC):
                lh = xns[:, c * T + t16 * 128: c * T + (t16 + 1) * 128]
                E.pe(lambda e, lh=lh, c=c: e.matmul(pB[:], lh, wbufB[:, c * 512:(c + 1) * 512],
                                                    start=(c == 0), stop=(c == NC - 1)))
            E.act(lambda e: e.activation(sA[:, 0:512], pA[:], AF.Silu, scale=statc[:, 4:5]))
            E.dve(lambda e: e.tensor_scalar(sB[:, 0:512], pB[:], statc[:, 5:6], None, AluOpType.mult))
            E.dve(lambda e: e.tensor_tensor(sA[:, 0:512], sA[:, 0:512], sB[:, 0:512], AluOpType.mult))
            E.dma(htok[t16 * 128:(t16 + 1) * 128, ffb * 512:(ffb + 1) * 512], sA[:, 0:512])

    f3h = lambda t: t.rearrange("p (g i) -> p g i", i=64)
    for t16 in range(NT16):
        rsl = slice(t16 * 128, (t16 + 1) * 128)
        E.dma(hbig[:], htok[rsl, :])
        E.dve(lambda e: e.tensor_reduce(scol[:, 0:128], f3h(hbig[:]), AX.X, AluOpType.max, apply_absolute_value=True))
        E.dve(lambda e: e.tensor_scalar(scol[:, 0:128], scol[:, 0:128], 1e-5, None, AluOpType.max))
        E.dve(lambda e: e.reciprocal(scol[:, 128:256], scol[:, 0:128]))
        E.dve(lambda e: e.tensor_scalar(scol[:, 128:256], scol[:, 128:256], 127.0, None, AluOpType.mult))
        E.dve(lambda e: e.tensor_tensor(f3h(h8[:]), f3h(hbig[:]),
                                        scol[:, 128:256].unsqueeze(2).broadcast_to([128, 128, 64]), AluOpType.mult))
        E.dve(lambda e: e.tensor_scalar(scol[:, 128:256], scol[:, 0:128], 1.0 / 127.0, None, AluOpType.mult))
        E.dve(lambda e: e.tensor_tensor(f3h(hbig[:]), f3h(h8[:]),
                                        scol[:, 128:256].unsqueeze(2).broadcast_to([128, 128, 64]), AluOpType.mult))
        for fc in range(NFC):
            sl4 = (fc % 4) * 128
            E.pe(lambda e, fc=fc, sl4=sl4: e.transpose(pT[:, sl4:sl4 + 128], hbig[:, fc * 128:(fc + 1) * 128], identb[:]))
            if fc % 4 == 3:
                base4 = fc - 3
                E.dve(lambda e: e.tensor_copy(sB[:, 0:512], pT[:, 0:512]))
                for j in range(4):
                    E.dma(hqT[(base4 + j) * 128:(base4 + j + 1) * 128, rsl], sB[:, j * 128:(j + 1) * 128])

    for t4 in range(NT4):
        tsl = slice(t4 * 512, (t4 + 1) * 512)
        for fc in range(NFC):
            E.dma(xns[:, fc * 512:(fc + 1) * 512], hqT[fc * 128:(fc + 1) * 128, tsl])
        for oc in range(NC):
            E.dma(bufv(wbufW, NFC), wsl(m1w3T, oc, NFC) if False else wsl(w3_q, oc, NFC))
            for fc in range(NFC):
                E.pe(lambda e, fc=fc: e.matmul(pA[:], wbufW[:, fc * 128:(fc + 1) * 128],
                                               xns[:, fc * 512:(fc + 1) * 512],
                                               start=(fc == 0), stop=(fc == NFC - 1)))
            E.dma(fA[:], x1T[oc * 128:(oc + 1) * 128, tsl])
            E.dve(lambda e: e.tensor_scalar(fB[:], pA[:], statc[:, 6:7], None, AluOpType.mult))
            E.dve(lambda e: e.tensor_tensor(fB[:], fB[:], fA[:], AluOpType.add))
            E.dma(accT1[oc * 128:(oc + 1) * 128, tsl], fB[:])

    emit_attention(QT1, KT1, Vtok1, OT1, 1, NH)

    for t16 in range(NT16):
        rsl = slice(t16 * 128, (t16 + 1) * 128)
        for c in range(NC):
            E.dma(sA[:, c * 128:(c + 1) * 128], OT1[c * 128:(c + 1) * 128, rsl])
        for c in range(NC):
            sl4 = (c % 4) * 128
            E.pe(lambda e, c=c, sl4=sl4: e.transpose(pT[:, sl4:sl4 + 128], sA[:, c * 128:(c + 1) * 128], identb[:]))
            if c % 4 == 3:
                base = (c // 4) * 512
                E.dve(lambda e, base=base: e.tensor_copy(sB[:, base:base + 512], pT[:, 0:512]))
        E.dve(lambda e: e.tensor_reduce(cols[:, 3:4], sB[:], AX.X, AluOpType.max, apply_absolute_value=True))
        E.dve(lambda e: e.tensor_scalar(cols[:, 3:4], cols[:, 3:4], 1e-5, None, AluOpType.max))
        E.dve(lambda e: e.reciprocal(cols[:, 4:5], cols[:, 3:4]))
        E.dve(lambda e: e.tensor_scalar(x8, sB[:], cols[:, 4:5], 127.0, AluOpType.mult, AluOpType.mult))
        E.dve(lambda e: e.tensor_scalar(sB[:], x8, cols[:, 3:4], 1.0 / 127.0, AluOpType.mult, AluOpType.mult))
        E.dma(Oqtok[rsl, :], sB[:])
    transpose_tok_to_fm(Oqtok, OqT)

    for oc in range(NC):
        E.dma(bufv(wbufA, NC), wsl(wo_q, oc, NC))
        for t4 in range(NT4):
            tsl = slice(t4 * 512, (t4 + 1) * 512)
            for c in range(NC):
                E.dma(sA[:, 0:512], OqT[c * 128:(c + 1) * 128, tsl])
                E.pe(lambda e, c=c: e.matmul(pA[:], wbufA[:, c * 128:(c + 1) * 128], sA[:, 0:512],
                                             start=(c == 0), stop=(c == NC - 1)))
            E.dma(fA[:], accT1[oc * 128:(oc + 1) * 128, tsl])
            E.dve(lambda e: e.scalar_tensor_tensor(fB[:], pA[:], statc[:, 3:4], fA[:],
                                                   AluOpType.mult, AluOpType.add))
            E.dma(outT[oc * 128:(oc + 1) * 128, tsl], fB[:])

    E.finalize(nc, block, sempool)
    es.close()
    return nc



def stage_l0(inputs, b):
    f32 = np.float32
    x = np.asarray(inputs["L0_attn_x"][b], dtype=f32)
    mask, onesb, onesf, ident, onesrowf = _consts_host()
    cos2, sin2 = _rope_tables()
    m = {
        "xT": np.ascontiguousarray(x.T),
        "wqT": np.ascontiguousarray(np.asarray(inputs["L0_attn_Wq"], f32).T),
        "wkT": np.ascontiguousarray(np.asarray(inputs["L0_attn_Wk"], f32).T),
        "wvT": np.ascontiguousarray(np.asarray(inputs["L0_attn_Wv"], f32).T),
        "woT": np.ascontiguousarray(np.asarray(inputs["L0_attn_Wo"], f32).T),
        "bcols": np.concatenate([
            _chunk_cols(np.asarray(inputs["L0_attn_bq"], f32)),
            _chunk_cols(np.asarray(inputs["L0_attn_bk"], f32)),
            _chunk_cols(np.asarray(inputs["L0_attn_bv"], f32)),
            _chunk_cols(np.asarray(inputs["L0_attn_bo"], f32))], axis=1),
        "bvrow": np.ascontiguousarray(np.asarray(inputs["L0_attn_bv"], f32)[None, :]),
        "qg": np.ascontiguousarray(np.asarray(inputs["L0_attn_q_gain"], f32)[None, :]),
        "ropec": cos2, "ropes": sin2,
        "maskc": mask, "onesb": onesb, "identb": ident,
        "onesf": onesf, "onesrowf": onesrowf,
        "routerT": np.ascontiguousarray(np.asarray(inputs["L0_ffn_router_w"], f32).T),
        "sgT": np.ascontiguousarray(np.asarray(inputs["L0_ffn_shared_gate"], f32).T),
        "suT": np.ascontiguousarray(np.asarray(inputs["L0_ffn_shared_up"], f32).T),
        "sdT": np.ascontiguousarray(np.asarray(inputs["L0_ffn_shared_down"], f32).T),
    }
    for e in range(NEXP):
        m[f"wgT{e}"] = np.ascontiguousarray(np.asarray(inputs["L0_ffn_w_gate"][e], f32).T)
        m[f"wuT{e}"] = np.ascontiguousarray(np.asarray(inputs["L0_ffn_w_up"][e], f32).T)
        m[f"wdT{e}"] = np.ascontiguousarray(np.asarray(inputs["L0_ffn_w_down"][e], f32).T)
    return m




def _sel8_host():
    m = np.zeros((8, 8 * 128), dtype=bfdt)
    for e in range(8):
        m[e, e * 128:(e + 1) * 128] = 1.0
    return m


def stage_fused(inputs, b):
    m = stage_l0(inputs, b)
    f32 = np.float32
    m["sel8c"] = _sel8_host()
    m["m1wqT"] = np.ascontiguousarray(np.asarray(inputs["L1_attn_w_q"], f32).T)
    m["m1wkT"] = np.ascontiguousarray(np.asarray(inputs["L1_attn_w_k"], f32).T)
    m["m1wvT"] = np.ascontiguousarray(np.asarray(inputs["L1_attn_w_v"], f32).T)
    m["m1woT"] = np.ascontiguousarray(np.asarray(inputs["L1_attn_w_o"], f32).T)
    m["m1w1T"] = np.ascontiguousarray(np.asarray(inputs["L1_ffn_w1"], f32).T)
    m["m1w2T"] = np.ascontiguousarray(np.asarray(inputs["L1_ffn_w2"], f32).T)
    m["m1w3T"] = np.ascontiguousarray(np.asarray(inputs["L1_ffn_w3"], f32).T)
    return m


def run_fused(inputs, cores=8, dev=False):
    nc = build_fused(dev=dev)
    in_maps = [stage_fused(inputs, b) for b in range(cores)]
    res = run_bass_kernel_spmd(nc, in_maps, core_ids=list(range(cores)))
    return res.results


def kernel(**inputs):
    n = 8
    res = run_fused(inputs, cores=n, dev=False)
    out = np.stack([np.asarray(res[b]["outT"], dtype=np.float32).T for b in range(n)], axis=0)
    return out


# version 06
# speedup vs baseline: 1.7999x; 1.2515x over previous
import numpy as np
import ml_dtypes
import concourse.bass as bass
import concourse.mybir as mybir
from concourse.bass_utils import run_bass_kernel_spmd
from concourse.alu_op_type import AluOpType
from contextlib import ExitStack

F32 = mybir.dt.float32
BF16 = mybir.dt.bfloat16
I32 = mybir.dt.int32
I8 = mybir.dt.int8
AF = mybir.ActivationFunctionType
AX = mybir.AxisListType
bfdt = ml_dtypes.bfloat16

T = 2048
D = 2048
NC = 16
NT4 = 4
NT16 = 16
HD = 128
NH = 16
KVH0 = 4
DFF1 = 8192
NFC = DFF1 // 128
NI = 1408
NIC = 11
NEXP = 8
NEG = -1e30


class Emit:

    def __init__(self):
        self.steps = []

    def _add(self, eng, kind, fn, inc, after, sem):
        self.steps.append((eng, kind, fn, inc, after, sem))
        return len(self.steps) - 1

    def dma(self, dst, src, after=None, sem="Q"):
        return self._add("sync", "dma", lambda e: e.dma_start(dst, src), 16, after, sem)

    def dmag(self, dst, src, after=None, sem="GD"):
        return self._add("gpsimd", "dma", lambda e: e.dma_start(dst, src), 16, after, sem)

    def pe(self, fn, after=None):
        return self._add("tensor", "c", fn, 1, after, "T")

    def dve(self, fn, after=None):
        return self._add("vector", "c", fn, 1, after, "V")

    def act(self, fn, after=None):
        return self._add("scalar", "c", fn, 1, after, "S")

    def gp(self, fn, after=None):
        return self._add("gpsimd", "c", fn, 1, after, "G")

    def finalize(self, nc, block, sempool):
        steps = self.steps
        n = len(steps)
        cumsem = {}
        completion = []
        for eng, kind, fn, inc, after, sem in steps:
            cumsem[sem] = cumsem.get(sem, 0) + inc
            completion.append((sem, cumsem[sem]))
        totals = dict(cumsem)
        running = {}
        targets = [None] * n
        for i, (eng, kind, fn, inc, after, sem) in enumerate(steps):
            if after is None:
                tg = dict(running)
            else:
                toks = after if isinstance(after, (list, tuple)) else [after]
                tg = {}
                for t in toks:
                    if t is None:
                        continue
                    smt, val = completion[t]
                    if tg.get(smt, 0) < val:
                        tg[smt] = val
            targets[i] = tg
            running[sem] = running.get(sem, 0) + inc

        sems = {name: sempool(name) for name in totals}

        def emit(eng_obj, name):
            lastw = {}
            for (eng, kind, fn, inc, after, sem), tg in zip(steps, targets):
                if eng != name:
                    continue
                for smt in tg:
                    val = tg[smt]
                    if val > lastw.get(smt, 0):
                        eng_obj.wait_ge(sems[smt], val)
                        lastw[smt] = val
                fn(eng_obj).then_inc(sems[sem], inc)
            return lastw

        @block.sync
        def _(sync):
            emit(sync, "sync")
            for smt, tot in sorted(totals.items()):
                sync.wait_ge(sems[smt], tot)

        @block.vector
        def _(vector):
            emit(vector, "vector")

        @block.scalar
        def _(scalar):
            emit(scalar, "scalar")

        @block.tensor
        def _(tensor):
            emit(tensor, "tensor")

        @block.gpsimd
        def _(gpsimd):
            emit(gpsimd, "gpsimd")



def _chunk_cols(vec):
    n = vec.shape[0] // 128
    return np.ascontiguousarray(vec.reshape(n, 128).T.astype(np.float32))

def _rope_tables():
    inv_freq = 1.0 / (10000.0 ** (np.arange(0, HD, 2, dtype=np.float64) / HD))
    t = np.arange(T, dtype=np.float64)
    fr = np.outer(t, inv_freq)
    cos = np.cos(fr).T
    sin = np.sin(fr).T
    cos2 = np.vstack([cos, cos]).astype(bfdt)
    sin2 = np.vstack([sin, -sin]).astype(bfdt)
    return np.ascontiguousarray(cos2), np.ascontiguousarray(sin2)

def _consts_host():
    mask = np.where(np.tril(np.ones((128, 128), dtype=bool)), 0.0, NEG).astype(np.float32)
    onesb = np.ones((128, 128), dtype=bfdt)
    onesf = np.ones((128, 1), dtype=np.float32)
    ident = np.eye(128, dtype=bfdt)
    onesrow_f = np.ones((1, 128), dtype=np.float32)
    return mask, onesb, onesf, ident, onesrow_f



def build_fused(dev=False):
    nc = bass.Bass("TRN2", target_bir_lowering=False)
    E = Emit()
    es = ExitStack()

    def din(name, shape, dt=F32):
        return nc.dram_tensor(name, shape, dt, kind="ExternalInput").ap()

    def dout(name, shape, dt=F32):
        return nc.dram_tensor(name, shape, dt, kind="ExternalOutput").ap()

    def dint(name, shape, dt=BF16):
        kind = "ExternalOutput" if dev else "Internal"
        return nc.dram_tensor(name, shape, dt, kind=kind).ap()

    xT = din("xT", [D, T])
    wqT = din("wqT", [D, D])
    wkT = din("wkT", [D, 512])
    wvT = din("wvT", [D, 512])
    woT = din("woT", [D, D])
    bcols = din("bcols", [128, 40])
    bvrow = din("bvrow", [1, 512])
    qg = din("qg", [1, 16])
    ropec = din("ropec", [128, T], BF16)
    ropes = din("ropes", [128, T], BF16)
    maskc = din("maskc", [128, 128])
    onesb_in = din("onesb", [128, 128], BF16)
    identb_in = din("identb", [128, 128], BF16)
    onesf_in = din("onesf", [128, 1])
    onesrowf_in = din("onesrowf", [1, 128])
    sel8_in = din("sel8c", [8, 8 * 128], BF16)
    routerT = din("routerT", [D, 8])
    sgT = din("sgT", [D, NI])
    suT = din("suT", [D, NI])
    sdT = din("sdT", [NI, D])
    wgTs = [din(f"wgT{e}", [D, NI]) for e in range(NEXP)]
    wuTs = [din(f"wuT{e}", [D, NI]) for e in range(NEXP)]
    wdTs = [din(f"wdT{e}", [NI, D]) for e in range(NEXP)]

    outT = dout("outT", [D, T])

    x1T = dint("x1T", [D, T], F32)
    xnT = dint("xnT", [D, T])
    qrawT = dint("qrawT", [D, T])
    QT = dint("QT", [D, T])
    KT = dint("KT", [512, T])
    Vtok = dint("Vtok", [T, 512])
    OT = dint("OT", [D, T])
    hshT = dint("hshT", [NI, T])
    heT = dint("heT", [NI, T])
    accT = dint("accTl0", [D, T], F32)

    sb = lambda name, shape, dt=BF16: es.enter_context(nc.sbuf_tensor(name, shape, dt))
    xns = sb("xns", [128, NC * T])
    wbufA = sb("wbufA", [128, 16 * 512])
    wbufB = sb("wbufB", [128, 16 * 512])
    wbufW = sb("wbufW", [128, 64 * 128])
    wbufC = wbufW
    sA = sb("sA", [128, 2048])
    sB = sb("sB", [128, 2048])
    fA = sb("fA", [128, 512], F32)
    fB = sb("fB", [128, 512], F32)
    ktile = sb("ktile", [128, 2048])
    vtile = sb("vtile", [128, 2048])
    ptile = sb("ptile", [128, 2048])
    sD = ptile
    pttile = sb("pttile", [128, 2048])
    cbs_full = pttile
    qtile = sb("qtile", [128, 256])
    rows = sb("rows", [1, 1536], F32)
    rowsb = sb("rowsb", [1, 2 * T])
    comb8 = sb("comb8", [8, T])
    sel8 = sb("sel8", [8, 8 * 128])
    combtok = sb("combtok", [128, 16 * 8], F32)
    mbs = sb("mbs", [128, 512])
    rbs = sb("rbs", [128, 512])
    cols = sb("cols", [128, 8], F32)
    bvb = sb("bvb", [128, 512])
    maskb = sb("maskb", [128, 128], F32)
    onesb = sb("onesbS", [128, 128])
    identb = sb("identbS", [128, 128])
    onesf = sb("onesfS", [128, 1], F32)
    onesrowf = sb("onesrowfS", [1, 128], F32)
    biasc = sb("biascS", [128, 40], F32)
    qgS = sb("qgS", [1, 16], F32)
    epsc = sb("epsc", [1, 2], F32)
    hbig = sb("hbig", [128, DFF1])
    h8 = sb("h8", [128, DFF1], I8)
    x8 = h8[:, 0:2048]
    fX = sb("fX", [128, 2048], F32)
    scol = sb("scol", [128, 256], F32)
    statc = sb("statc", [128, 16], F32)
    eps128 = sb("eps128S", [128, 1], F32)

    ps = lambda name, shape, dt=F32: es.enter_context(nc.psum_tensor(name, shape, dt))
    pS = ps("pS", [128, 2048])
    pA = ps("pA", [128, 512])
    pB = ps("pB", [128, 512])
    pT = ps("pT", [128, 512], BF16)
    pO = ps("pO", [128, 512])
    pR = pO[0:1, 0:512]
    pSa = pS[:, 0:512]
    pSb = pS[:, 512:1024]
    acc_store_tok = {}

    def sempool(name):
        return es.enter_context(nc.semaphore(f"sp_{name}"))
    block = es.enter_context(nc.Block())

    E.dma(maskb[:], maskc)
    E.dma(onesb[:], onesb_in)
    E.dma(identb[:], identb_in)
    E.dma(onesf[:], onesf_in)
    E.dma(onesrowf[:], onesrowf_in)
    E.dma(biasc[:], bcols)
    E.dma(qgS[:], qg)
    E.gp(lambda e: e.memset(epsc[:, 0:1], 1e-5))
    E.gp(lambda e: e.memset(epsc[:, 1:2], 1.1920929e-07))
    E.gp(lambda e: e.memset(eps128[:], 1e-5))
    E.dma(sel8[:], sel8_in)
    ropec_s = sb("ropecS", [128, T])
    ropes_s = sb("ropesS", [128, T])
    E.dma(ropec_s[:], ropec)
    E.dma(ropes_s[:], ropes)
    E.dma(rows[:, 0:512], bvrow)
    E.dve(lambda e: e.tensor_copy(rowsb[:, 0:512], rows[:, 0:512]))
    E.pe(lambda e: e.matmul(pA[:], onesb[0:1, :], rowsb[0:1, 0:512], start=True, stop=True))
    E.dve(lambda e: e.tensor_copy(bvb[:], pA[:]))

    def wsl(wT_dram, oc, nch, width=128):
        return wT_dram.rearrange("(c p) o -> p c o", p=128)[:, :, oc * width:(oc + 1) * width]

    def bufv(buf, nch, width=128):
        return buf[:, 0:nch * width].rearrange("p (c o) -> p c o", o=width)

    ONE_COL = onesb[:, 0:1]
    ONE_ROW = onesb[0:1, :]

    def bcast_row_bf16(src_row_bf, psum, dst_bf, n):
        E.pe(lambda e: e.matmul(psum[:, 0:n], ONE_ROW, src_row_bf, start=True, stop=True))
        E.dve(lambda e: e.tensor_copy(dst_bf[:, 0:n], psum[:, 0:n]))

    for t4 in range(NT4):
        tsl = slice(t4 * 512, (t4 + 1) * 512)
        for c in range(NC):
            xv = xns[:, c * T + t4 * 512: c * T + (t4 + 1) * 512]
            E.dmag(xv, xT[c * 128:(c + 1) * 128, tsl])
        for c in range(NC):
            xv = xns[:, c * T + t4 * 512: c * T + (t4 + 1) * 512]
            E.pe(lambda e, xv=xv, c=c: e.matmul(pR, ONE_COL, xv, start=(c == 0), stop=(c == NC - 1)))
        E.dve(lambda e: e.tensor_scalar(rows[:, 0:512], pR, 1.0 / D, None, AluOpType.mult))
        for c in range(NC):
            xv = xns[:, c * T + t4 * 512: c * T + (t4 + 1) * 512]
            E.act(lambda e, xv=xv: e.activation(sA[:, 0:512], xv, AF.Square))
            E.pe(lambda e, c=c: e.matmul(pR, ONE_COL, sA[:, 0:512], start=(c == 0), stop=(c == NC - 1)))
        mrow = rows[:, 0:512]
        e2row = rows[:, 512:1024]
        E.dve(lambda e, e2row=e2row: e.tensor_scalar(e2row, pR, 1.0 / D, None, AluOpType.mult))
        msq = rows[:, 1024:1536]
        E.dve(lambda e, mrow=mrow, msq=msq: e.tensor_tensor(msq, mrow, mrow, AluOpType.mult))
        E.dve(lambda e, e2row=e2row, msq=msq: e.tensor_tensor(e2row, e2row, msq, AluOpType.subtract))
        E.act(lambda e, e2row=e2row: e.activation(e2row, e2row, AF.Ln, bias=epsc[:, 0:1]))
        E.act(lambda e, e2row=e2row: e.activation(e2row, e2row, AF.Exp, scale=-0.5))
        E.dve(lambda e, mrow=mrow, t4=t4: e.tensor_copy(rowsb[:, t4 * 512:(t4 + 1) * 512], mrow))
        E.dve(lambda e, e2row=e2row, t4=t4: e.tensor_copy(rowsb[:, T + t4 * 512: T + (t4 + 1) * 512], e2row))
    for t4 in range(NT4):
        bcast_row_bf16(rowsb[0:1, t4 * 512:(t4 + 1) * 512], pA, mbs, 512)
        bcast_row_bf16(rowsb[0:1, T + t4 * 512: T + (t4 + 1) * 512], pB, rbs, 512)
        for c in range(NC):
            xv = xns[:, c * T + t4 * 512: c * T + (t4 + 1) * 512]
            E.dve(lambda e, xv=xv: e.tensor_tensor(sA[:, 0:512], xv, mbs[:], AluOpType.subtract))
            E.dve(lambda e, xv=xv: e.tensor_tensor(xv, sA[:, 0:512], rbs[:], AluOpType.mult))
            if dev:
                E.dma(xnT[c * 128:(c + 1) * 128, t4 * 512:(t4 + 1) * 512], xv)

    def proj(dst_dram, wT_dram, out_chunks, bias_off=None, scale_col=None, extra_scale=None):
        pArr = [pA, pSa]
        wlast = [None, None]
        pAev = [None, None]
        sst = [None, None]
        first = True
        it = 0
        for oc in range(out_chunks):
            pw = oc % 2
            wa_t = E.dmag(bufv(wbufA[:, pw * 2048:(pw + 1) * 2048], NC), wsl(wT_dram, oc, NC),
                          after=None if first else [wlast[pw]], sem=f"wa{pw}")
            first = False
            for t4 in range(NT4):
                pp = it % 2
                mm = None
                for c in range(NC):
                    xv = xns[:, c * T + t4 * 512: c * T + (t4 + 1) * 512]
                    mm = E.pe(lambda e, c=c, xv=xv, pp=pp, pw=pw: e.matmul(
                        pArr[pp][:], wbufA[:, pw * 2048 + c * 128: pw * 2048 + (c + 1) * 128], xv,
                        start=(c == 0), stop=(c == NC - 1)),
                        after=([wa_t, pAev[pp]] if c == 0 else []))
                if bias_off is not None:
                    bc = biasc[:, bias_off + oc: bias_off + oc + 1]
                    ev = E.dve(lambda e, bc=bc, pp=pp: e.tensor_scalar(
                        sA[:, pp * 512:(pp + 1) * 512], pArr[pp][:], bc, None, AluOpType.add),
                        after=[mm, sst[pp]])
                elif scale_col is not None:
                    ev = E.dve(lambda e, pp=pp: e.tensor_scalar(
                        sA[:, pp * 512:(pp + 1) * 512], pArr[pp][:], scale_col, extra_scale,
                        AluOpType.mult, AluOpType.mult if extra_scale is not None else AluOpType.bypass),
                        after=[mm, sst[pp]])
                else:
                    ev = E.dve(lambda e, pp=pp: e.tensor_copy(sA[:, pp * 512:(pp + 1) * 512], pArr[pp][:]),
                               after=[mm, sst[pp]])
                st = E.dma(dst_dram[oc * 128:(oc + 1) * 128, t4 * 512:(t4 + 1) * 512],
                           sA[:, pp * 512:(pp + 1) * 512], after=[ev], sem=f"hs{pp}")
                wlast[pw] = mm
                pAev[pp] = ev
                sst[pp] = st
                it += 1

    proj(qrawT, wqT, 16, bias_off=0)

    kraw = dint("krawT", [512, T])
    proj(kraw, wkT, 4, bias_off=16)

    SCL = float(1.0 / np.sqrt(HD))

    def rms_rope(src_dram, dst_dram, n_heads, head_of, gain_fold):
        for h in range(n_heads):
            hc = head_of(h)
            E.dma(sA[:], src_dram[hc * 128:(hc + 1) * 128, :])
            for t4 in range(NT4):
                tsl = slice(t4 * 512, (t4 + 1) * 512)
                E.act(lambda e, tsl=tsl: e.activation(sB[:, 0:512], sA[:, tsl], AF.Square))
                E.pe(lambda e: e.matmul(pR, ONE_COL, sB[:, 0:512], start=True, stop=True))
                rr = rows[:, 0:512]
                E.act(lambda e, rr=rr: e.activation(rr, pR, AF.Ln, bias=epsc[:, 1:2], scale=1.0 / HD))
                E.act(lambda e, rr=rr: e.activation(rr, rr, AF.Exp, scale=-0.5))
                if gain_fold is not None:
                    gf = gain_fold(h)
                    E.dve(lambda e, rr=rr, gf=gf: e.tensor_scalar(rr, rr, gf, SCL, AluOpType.mult, AluOpType.mult))
                E.dve(lambda e, rr=rr, t4=t4: e.tensor_copy(rowsb[:, t4 * 512:(t4 + 1) * 512], rr))
            E.dma(sD[0:64, :], sA[64:128, :])
            E.dma(sD[64:128, :], sA[0:64, :])
            E.dve(lambda e: e.tensor_tensor(sB[:], sA[:], ropec_s[:], AluOpType.mult))
            E.dve(lambda e: e.tensor_tensor(sD[:], sD[:], ropes_s[:], AluOpType.mult))
            E.dve(lambda e: e.tensor_tensor(sB[:], sB[:], sD[:], AluOpType.add))
            for t4 in range(NT4):
                tsl = slice(t4 * 512, (t4 + 1) * 512)
                bcast_row_bf16(rowsb[0:1, t4 * 512:(t4 + 1) * 512], pA, rbs, 512)
                E.dve(lambda e, tsl=tsl: e.tensor_tensor(sB[:, tsl], sB[:, tsl], rbs[:], AluOpType.mult))
                E.dma(dst_dram[hc * 128:(hc + 1) * 128, tsl], sB[:, tsl])

    rms_rope(qrawT, QT, 16, lambda h: h,
             gain_fold=lambda h: qgS[0:1, h:h + 1])
    rms_rope(kraw, KT, 4, lambda h: h, gain_fold=None)

    E.dmag(bufv(wbufC, NC, 512), wsl(wvT, 0, NC, 512))
    for t16 in range(NT16):
        for c in range(NC):
            lh = xns[:, c * T + t16 * 128: c * T + (t16 + 1) * 128]
            rh = wbufC[:, c * 512:(c + 1) * 512]
            E.pe(lambda e, lh=lh, rh=rh, c=c: e.matmul(pA[:], lh, rh, start=(c == 0), stop=(c == NC - 1)))
        E.dve(lambda e: e.tensor_copy(sA[:, 0:512], pA[:]))
        E.dve(lambda e: e.tensor_tensor(sA[:, 0:512], sA[:, 0:512], bvb[:], AluOpType.add))
        E.dma(Vtok[t16 * 128:(t16 + 1) * 128, :], sA[:, 0:512])

    E.dmag(bufv(wbufA, NC, 8), wsl(routerT, 0, NC, 8))
    for t16 in range(NT16):
        for c in range(NC):
            lh = xns[:, c * T + t16 * 128: c * T + (t16 + 1) * 128]
            E.pe(lambda e, lh=lh, c=c: e.matmul(pA[:, 0:8], lh, wbufA[:, c * 8:(c + 1) * 8],
                                                start=(c == 0), stop=(c == NC - 1)))
        E.dve(lambda e: e.tensor_reduce(cols[:, 0:1], pA[:, 0:8], AX.X, AluOpType.max))
        E.dve(lambda e: e.tensor_scalar(cols[:, 1:2], cols[:, 0:1], -1.0, None, AluOpType.mult))
        def _exp_router(e):
            e.activation(fA[:, 0:8], pA[:, 0:8], AF.Exp, bias=cols[:, 1:2], accum_out=cols[:, 2:3])
            return e.copy(cols[:, 6:7], cols[:, 2:3])
        E.act(_exp_router)
        E.dve(lambda e: e.reciprocal(cols[:, 3:4], cols[:, 2:3]))
        E.dve(lambda e: e.tensor_scalar(fA[:, 0:8], fA[:, 0:8], cols[:, 3:4], None, AluOpType.mult))
        E.dve(lambda e: e.tensor_reduce(cols[:, 4:5], fA[:, 0:8], AX.X, AluOpType.max))
        E.dve(lambda e: e.tensor_scalar(fB[:, 0:8], fA[:, 0:8], cols[:, 4:5], None, AluOpType.is_ge))
        E.dve(lambda e: e.scalar_tensor_tensor(fB[:, 0:8], fB[:, 0:8], NEG, fA[:, 0:8], AluOpType.mult, AluOpType.add))
        E.dve(lambda e: e.tensor_reduce(cols[:, 5:6], fB[:, 0:8], AX.X, AluOpType.max))
        E.dve(lambda e: e.tensor_scalar(fB[:, 0:8], fA[:, 0:8], cols[:, 5:6], None, AluOpType.is_ge))
        E.dve(lambda e, t16=t16: e.tensor_tensor(combtok[:, t16 * 8:(t16 + 1) * 8], fA[:, 0:8], fB[:, 0:8], AluOpType.mult))
    for t16 in range(NT16):
        E.dve(lambda e, t16=t16: e.tensor_copy(sA[:, 0:8], combtok[:, t16 * 8:(t16 + 1) * 8]))
        E.pe(lambda e: e.transpose(pT[0:8, 0:128], sA[:, 0:8], identb[:]))
        E.dve(lambda e, t16=t16: e.tensor_copy(comb8[0:8, t16 * 128:(t16 + 1) * 128], pT[0:8, 0:128]))

    def gated_ffn(gT, uT, dst_dram, comb_e=None, tag=""):
        cb_tok = None
        if comb_e is not None:
            for t4 in range(NT4):
                E.pe(lambda e, t4=t4, comb_e=comb_e: e.matmul(
                    pO[:], sel8[:, comb_e * 128:(comb_e + 1) * 128],
                    comb8[0:8, t4 * 512:(t4 + 1) * 512], start=True, stop=True))
                cb_tok = E.dve(lambda e, t4=t4: e.tensor_copy(cbs_full[:, t4 * 512:(t4 + 1) * 512], pO[:]))
        pArr = [pA, pSa]
        pBrr = [pB, pSb]
        wlast = [None, None]
        pAev = [None, None]
        pBev = [None, None]
        sBst = [None, None]
        first = True
        it = 0
        for oc in range(NIC):
            pw = oc % 2
            wa_t = E.dmag(bufv(wbufA[:, pw * 2048:(pw + 1) * 2048], NC), wsl(gT, oc, NC),
                          after=None if first else [wlast[pw]], sem=f"wa{pw}")
            wb_t = E.dmag(bufv(wbufB[:, pw * 2048:(pw + 1) * 2048], NC), wsl(uT, oc, NC),
                          after=None if first else [wlast[pw]], sem=f"wb{pw}")
            first = False
            for t4 in range(NT4):
                pp = it % 2
                mmA = None
                for c in range(NC):
                    xv = xns[:, c * T + t4 * 512: c * T + (t4 + 1) * 512]
                    mmA = E.pe(lambda e, c=c, xv=xv, pp=pp, pw=pw: e.matmul(
                        pArr[pp][:], wbufA[:, pw * 2048 + c * 128: pw * 2048 + (c + 1) * 128], xv,
                        start=(c == 0), stop=(c == NC - 1)),
                        after=([wa_t, pAev[pp], cb_tok] if c == 0 else []))
                mmB = None
                for c in range(NC):
                    xv = xns[:, c * T + t4 * 512: c * T + (t4 + 1) * 512]
                    mmB = E.pe(lambda e, c=c, xv=xv, pp=pp, pw=pw: e.matmul(
                        pBrr[pp][:], wbufB[:, pw * 2048 + c * 128: pw * 2048 + (c + 1) * 128], xv,
                        start=(c == 0), stop=(c == NC - 1)),
                        after=([wb_t, pBev[pp]] if c == 0 else []))
                sg = E.act(lambda e, pp=pp: e.activation(sA[:, pp * 512:(pp + 1) * 512], pArr[pp][:], AF.Sigmoid),
                           after=[mmA])
                tt = E.dve(lambda e, pp=pp: e.tensor_tensor(sB[:, pp * 512:(pp + 1) * 512],
                                                            sA[:, pp * 512:(pp + 1) * 512], pBrr[pp][:],
                                                            AluOpType.mult), after=[sg, mmB, sBst[pp]])
                ttl = tt
                if comb_e is not None:
                    ttl = E.dve(lambda e, pp=pp, t4=t4: e.tensor_tensor(
                        sB[:, pp * 512:(pp + 1) * 512], sB[:, pp * 512:(pp + 1) * 512],
                        cbs_full[:, t4 * 512:(t4 + 1) * 512], AluOpType.mult), after=[ttl])
                st = E.dma(dst_dram[oc * 128:(oc + 1) * 128, t4 * 512:(t4 + 1) * 512],
                           sB[:, pp * 512:(pp + 1) * 512], after=[ttl], sem=f"hs{pp}")
                wlast[pw] = mmB
                pAev[pp] = sg
                pBev[pp] = tt
                sBst[pp] = st
                it += 1

    def down_proj(dT, src_dram, first_flag):
        pArr = [pA, pSa]
        wlast = [None, None]
        pAev = [None, None]
        fev = [None, None]
        he_last_mm = None
        first = True
        it = 0
        for t4 in range(NT4):
            tsl = slice(t4 * 512, (t4 + 1) * 512)
            he_toks = []
            for ic in range(NIC):
                he_toks.append(E.dma(wbufW[:, ic * 512:(ic + 1) * 512],
                                     src_dram[ic * 128:(ic + 1) * 128, tsl],
                                     after=None if first else [he_last_mm], sem="he"))
                first = False
            for oc in range(NC):
                pw = oc % 2
                pp = it % 2
                wd_t = E.dmag(bufv(wbufA[:, pw * 2048:(pw + 1) * 2048], NIC), wsl(dT, oc, NIC),
                              after=[wlast[pw]] if wlast[pw] is not None else [he_toks[0]],
                              sem=f"wd{pw}")
                mm = None
                for ic in range(NIC):
                    mm = E.pe(lambda e, ic=ic, pp=pp, pw=pw: e.matmul(
                        pArr[pp][:], wbufA[:, pw * 2048 + ic * 128: pw * 2048 + (ic + 1) * 128],
                        wbufW[:, ic * 512:(ic + 1) * 512],
                        start=(ic == 0), stop=(ic == NIC - 1)),
                        after=(he_toks + [wd_t, pAev[pp]] if ic == 0 else []))
                fbuf = fA if pp == 0 else fB
                prev_store = acc_store_tok.get((oc, t4))
                war = [x for x in [fev[pp], prev_store] if x is not None]
                if first_flag:
                    ld = E.dma(fbuf[:], xT[oc * 128:(oc + 1) * 128, tsl], after=war)
                    xnv = xns[:, oc * T + t4 * 512: oc * T + (t4 + 1) * 512]
                    a1 = E.dve(lambda e, fbuf=fbuf, xnv=xnv: e.tensor_tensor(fbuf[:], fbuf[:], xnv, AluOpType.add),
                               after=[ld])
                else:
                    ld = E.dma(fbuf[:], accT[oc * 128:(oc + 1) * 128, tsl], after=war)
                    a1 = ld
                a2 = E.dve(lambda e, fbuf=fbuf, pp=pp: e.tensor_tensor(fbuf[:], fbuf[:], pArr[pp][:], AluOpType.add),
                           after=[a1, mm])
                st = E.dma(accT[oc * 128:(oc + 1) * 128, tsl], fbuf[:], after=[a2], sem=f"as{pp}")
                acc_store_tok[(oc, t4)] = st
                wlast[pw] = mm
                pAev[pp] = a2
                fev[pp] = st
                he_last_mm = mm
                it += 1

    gated_ffn(sgT, suT, hshT, comb_e=None)
    down_proj(sdT, hshT, first_flag=True)
    for e_i in range(NEXP):
        gated_ffn(wgTs[e_i], wuTs[e_i], heT, comb_e=e_i)
        down_proj(wdTs[e_i], heT, first_flag=False)


    def emit_attention(QTd, KTd, Vtokd, OTd, hpg, ngr):
        kbufs = [ktile, hbig[:, 0:2048]]
        vbufs = [vtile, hbig[:, 2048:4096]]
        kt_tok = [None, None]
        vt_tok = [None, None]
        klast = [None, None]
        vlast = [None, None]
        qS_war = [None, None]
        prev_store = [None, None]
        prev_exp = None
        pO_ev = None
        last_copy_tok = None
        last_tr_tok = None
        pend = [None]
        hctr = [0]

        def flush_pend():
            nonlocal pO_ev, last_copy_tok, last_tr_tok
            p = pend[0]
            if p is None:
                return
            nblk = p["qt"] + 1
            first_after = [p["norm"], last_copy_tok]
            ct = None
            t_ = None
            for kb in range(nblk):
                sl4 = (kb % 4) * 128
                a = first_after if kb % 4 == 0 else []
                t_ = E.pe(lambda e, kb=kb, sl4=sl4: e.transpose(
                    pT[:, sl4:sl4 + 128], ptile[:, kb * 128:(kb + 1) * 128], identb[:]), after=a)
                if kb % 4 == 3 or kb == nblk - 1:
                    base = (kb // 4) * 512
                    wdt = (kb % 4 + 1) * 128
                    ct = E.dve(lambda e, base=base, wdt=wdt: e.tensor_copy(
                        pttile[:, base:base + wdt], pT[:, 0:wdt]), after=[t_])
                    first_after = [ct]
            last_copy_tok = ct
            last_tr_tok = t_
            om = None
            gp = p["gp"]
            for kb in range(nblk):
                a = [ct, vt_tok[gp], pO_ev] if kb == 0 else []
                om = E.pe(lambda e, kb=kb, gp=gp: e.matmul(
                    pO[:, 0:128], vbufs[gp][:, kb * 128:(kb + 1) * 128],
                    pttile[:, kb * 128:(kb + 1) * 128],
                    start=(kb == 0), stop=(kb == nblk - 1)), after=a)
            vlast[gp] = om
            sp = p["sp"]
            ev = E.dve(lambda e, sp=sp: e.tensor_copy(sA[:, sp * 128:(sp + 1) * 128], pO[:, 0:128]),
                       after=[om, prev_store[sp]])
            st = E.dma(OTd[p["h"] * 128:(p["h"] + 1) * 128, p["qt"] * 128:(p["qt"] + 1) * 128],
                       sA[:, sp * 128:(sp + 1) * 128], after=[ev], sem=f"os{sp}")
            pO_ev = ev
            prev_store[sp] = st
            pend[0] = None

        first_load = True
        for qt in range(NT16):
            nk = (qt + 1) * 128
            nk4 = (nk + 511) // 512
            for g in range(ngr):
                gp = (qt * ngr + g) % 2
                kl = E.dma(kbufs[gp][:, 0:nk], KTd[g * 128:(g + 1) * 128, 0:nk],
                           after=None if first_load else [klast[gp]], sem=f"kt{gp}")
                vl = E.dma(vbufs[gp][:, 0:nk].rearrange("p (b d) -> p b d", d=128),
                           Vtokd[0:nk, g * 128:(g + 1) * 128].rearrange("(b p) d -> p b d", p=128),
                           after=None if first_load else [vlast[gp]], sem=f"vt{gp}")
                first_load = False
                kt_tok[gp] = kl
                vt_tok[gp] = vl
                for hh in range(hpg):
                    h = g * hpg + hh
                    qp = hctr[0] % 2
                    ql = E.dma(qtile[:, qp * 128:(qp + 1) * 128],
                               QTd[h * 128:(h + 1) * 128, qt * 128:(qt + 1) * 128],
                               after=[qS_war[qp]] if qS_war[qp] is not None else [kl],
                               sem=f"qs{qp}")
                    sm = None
                    for kb4 in range(nk4):
                        w = min(512, nk - kb4 * 512)
                        a = [ql, kl, prev_exp] if kb4 == 0 else []
                        sm = E.pe(lambda e, kb4=kb4, w=w, qp=qp, gp=gp: e.matmul(
                            pS[:, kb4 * 512: kb4 * 512 + w], qtile[:, qp * 128:(qp + 1) * 128],
                            kbufs[gp][:, kb4 * 512: kb4 * 512 + w],
                            start=True, stop=True), after=a)
                    qS_war[qp] = sm
                    klast[gp] = sm
                    mk = E.dve(lambda e, qt=qt: e.tensor_tensor(pS[:, qt * 128:(qt + 1) * 128],
                                                                pS[:, qt * 128:(qt + 1) * 128],
                                                                maskb[:], AluOpType.add), after=[sm])
                    rd = E.dve(lambda e, nk=nk: e.tensor_reduce(cols[:, 0:1], pS[:, 0:nk], AX.X,
                                                                AluOpType.max), after=[mk])
                    ng = E.dve(lambda e: e.tensor_scalar(cols[:, 1:2], cols[:, 0:1], -1.0, None,
                                                         AluOpType.mult), after=[rd])
                    flush_pend()
                    def _expf(e, nk=nk):
                        e.activation(ptile[:, 0:nk], pS[:, 0:nk], AF.Exp,
                                     bias=cols[:, 1:2], accum_out=cols[:, 2:3])
                        return e.copy(cols[:, 6:7], cols[:, 2:3])
                    ex = E.act(_expf, after=[ng, last_tr_tok])
                    rc = E.dve(lambda e: e.reciprocal(cols[:, 3:4], cols[:, 2:3]), after=[ex])
                    nm = E.dve(lambda e, nk=nk: e.tensor_scalar(ptile[:, 0:nk], ptile[:, 0:nk],
                                                                cols[:, 3:4], None, AluOpType.mult),
                               after=[rc])
                    prev_exp = ex
                    pend[0] = {"qt": qt, "h": h, "gp": gp, "sp": hctr[0] % 2, "norm": nm}
                    hctr[0] += 1
        flush_pend()

    emit_attention(QT, KT, Vtok, OT, 4, KVH0)

    for oc in range(NC):
        E.dmag(bufv(wbufA, NC), wsl(woT, oc, NC))
        for t4 in range(NT4):
            tsl = slice(t4 * 512, (t4 + 1) * 512)
            for c in range(NC):
                E.dma(sA[:, 0:512], OT[c * 128:(c + 1) * 128, tsl])
                E.pe(lambda e, c=c: e.matmul(pA[:], wbufA[:, c * 128:(c + 1) * 128], sA[:, 0:512],
                                             start=(c == 0), stop=(c == NC - 1)))
            E.dma(fA[:], accT[oc * 128:(oc + 1) * 128, tsl])
            bc = biasc[:, 24 + oc: 24 + oc + 1]
            E.dve(lambda e, bc=bc: e.scalar_tensor_tensor(fB[:], pA[:], bc, fA[:], AluOpType.add, AluOpType.add))
            E.dma(x1T[oc * 128:(oc + 1) * 128, tsl], fB[:])


    m1wqT = din("m1wqT", [D, D])
    m1wkT = din("m1wkT", [D, D])
    m1wvT = din("m1wvT", [D, D])
    m1woT = din("m1woT", [D, D])
    m1w1T = din("m1w1T", [D, DFF1])
    m1w2T = din("m1w2T", [D, DFF1])
    m1w3T = din("m1w3T", [DFF1, D])

    xqAtok = dint("xqAtok", [T, D])
    xqFtok = dint("xqFtok", [T, D])
    xqAT = dint("xqAT", [D, T])
    xqFT = dint("xqFT", [D, T])
    wq_q = dint("wq_q", [D, D])
    wk_q = dint("wk_q", [D, D])
    wv_q = dint("wv_q", [D, D])
    wo_q = dint("wo_q", [D, D])
    w1_q = dint("w1_q", [D, DFF1])
    w2_q = dint("w2_q", [D, DFF1])
    w3_q = dint("w3_q", [DFF1, D])
    QT1 = dint("QT1", [D, T])
    KT1 = dint("KT1", [D, T])
    Vtok1 = dint("Vtok1", [T, D])
    OT1 = dint("OT1", [D, T])
    Oqtok = dint("Oqtok", [T, D])
    OqT = dint("OqT", [D, T])
    htok = dint("htok", [T, DFF1])
    hqT = dint("hqT", [DFF1, T])
    accT1 = dint("accTl1", [D, T], F32)
    p11 = pO[0:1, 0:1]

    for t4 in range(NT4):
        tsl = slice(t4 * 512, (t4 + 1) * 512)
        for c in range(NC):
            xv = xns[:, c * T + t4 * 512: c * T + (t4 + 1) * 512]
            E.dmag(xv, x1T[c * 128:(c + 1) * 128, tsl])
        for c in range(NC):
            xv = xns[:, c * T + t4 * 512: c * T + (t4 + 1) * 512]
            E.pe(lambda e, xv=xv, c=c: e.matmul(pR, ONE_COL, xv, start=(c == 0), stop=(c == NC - 1)))
        E.dve(lambda e: e.tensor_scalar(rows[:, 0:512], pR, 1.0 / D, None, AluOpType.mult))
        for c in range(NC):
            xv = xns[:, c * T + t4 * 512: c * T + (t4 + 1) * 512]
            E.act(lambda e, xv=xv: e.activation(sA[:, 0:512], xv, AF.Square))
            E.pe(lambda e, c=c: e.matmul(pR, ONE_COL, sA[:, 0:512], start=(c == 0), stop=(c == NC - 1)))
        mrow = rows[:, 0:512]
        e2row = rows[:, 512:1024]
        msq = rows[:, 1024:1536]
        E.dve(lambda e, e2row=e2row: e.tensor_scalar(e2row, pR, 1.0 / D, None, AluOpType.mult))
        E.dve(lambda e, mrow=mrow, msq=msq: e.tensor_tensor(msq, mrow, mrow, AluOpType.mult))
        E.dve(lambda e, e2row=e2row, msq=msq: e.tensor_tensor(e2row, e2row, msq, AluOpType.subtract))
        E.act(lambda e, e2row=e2row: e.activation(e2row, e2row, AF.Ln, bias=epsc[:, 0:1]))
        E.act(lambda e, e2row=e2row: e.activation(e2row, e2row, AF.Exp, scale=-0.5))
        E.dve(lambda e, mrow=mrow, t4=t4: e.tensor_copy(rowsb[:, t4 * 512:(t4 + 1) * 512], mrow))
        E.dve(lambda e, e2row=e2row, t4=t4: e.tensor_copy(rowsb[:, T + t4 * 512: T + (t4 + 1) * 512], e2row))
    for t4 in range(NT4):
        bcast_row_bf16(rowsb[0:1, t4 * 512:(t4 + 1) * 512], pA, mbs, 512)
        bcast_row_bf16(rowsb[0:1, T + t4 * 512: T + (t4 + 1) * 512], pB, rbs, 512)
        for c in range(NC):
            xv = xns[:, c * T + t4 * 512: c * T + (t4 + 1) * 512]
            E.dve(lambda e, xv=xv: e.tensor_tensor(sA[:, 0:512], xv, mbs[:], AluOpType.subtract))
            E.dve(lambda e, xv=xv: e.tensor_tensor(xv, sA[:, 0:512], rbs[:], AluOpType.mult))

    f3 = lambda t: t.rearrange("p (g i) -> p g i", i=64)
    for t16 in range(NT16):
        rsl = slice(t16 * 128, (t16 + 1) * 128)
        for c in range(NC):
            sl4 = (c % 4) * 128
            E.pe(lambda e, c=c, t16=t16, sl4=sl4: e.transpose(
                pT[:, sl4:sl4 + 128], xns[:, c * T + t16 * 128: c * T + (t16 + 1) * 128], identb[:]))
            if c % 4 == 3:
                base = (c // 4) * 512
                E.dve(lambda e, base=base: e.tensor_copy(sA[:, base:base + 512], pT[:, 0:512]))
        E.dve(lambda e: e.tensor_reduce(cols[:, 3:4], sA[:], AX.X, AluOpType.max, apply_absolute_value=True))
        E.dve(lambda e: e.tensor_scalar(cols[:, 3:4], cols[:, 3:4], 1e-5, None, AluOpType.max))
        E.dve(lambda e: e.reciprocal(cols[:, 4:5], cols[:, 3:4]))
        E.dve(lambda e: e.tensor_scalar(x8, sA[:], cols[:, 4:5], 127.0, AluOpType.mult, AluOpType.mult))
        E.dve(lambda e: e.tensor_scalar(sB[:], x8, cols[:, 3:4], 1.0 / 127.0, AluOpType.mult, AluOpType.mult))
        E.dma(xqAtok[rsl, :], sB[:])
        E.dve(lambda e: e.tensor_reduce(scol[:, 0:32], f3(sA[:]), AX.X, AluOpType.max, apply_absolute_value=True))
        E.dve(lambda e: e.tensor_scalar(scol[:, 0:32], scol[:, 0:32], 1e-5, None, AluOpType.max))
        E.dve(lambda e: e.reciprocal(scol[:, 32:64], scol[:, 0:32]))
        E.dve(lambda e: e.tensor_scalar(scol[:, 32:64], scol[:, 32:64], 127.0, None, AluOpType.mult))
        E.dve(lambda e: e.tensor_tensor(f3(x8), f3(sA[:]),
                                        scol[:, 32:64].unsqueeze(2).broadcast_to([128, 32, 64]), AluOpType.mult))
        E.dve(lambda e: e.tensor_scalar(scol[:, 64:96], scol[:, 0:32], 1.0 / 127.0, None, AluOpType.mult))
        E.dve(lambda e: e.tensor_tensor(f3(sB[:]), f3(x8),
                                        scol[:, 64:96].unsqueeze(2).broadcast_to([128, 32, 64]), AluOpType.mult))
        E.dma(xqFtok[rsl, :], sB[:])

    def transpose_tok_to_fm(src_tok, dst_T):
        for t16 in range(NT16):
            E.dma(sA[:], src_tok[t16 * 128:(t16 + 1) * 128, :])
            for c in range(NC):
                sl4 = (c % 4) * 128
                E.pe(lambda e, c=c, sl4=sl4: e.transpose(pT[:, sl4:sl4 + 128], sA[:, c * 128:(c + 1) * 128], identb[:]))
                if c % 4 == 3:
                    base = (c // 4) * 512
                    E.dve(lambda e, base=base: e.tensor_copy(sB[:, base:base + 512], pT[:, 0:512]))
            for c in range(NC):
                E.dma(dst_T[c * 128:(c + 1) * 128, t16 * 128:(t16 + 1) * 128], sB[:, c * 128:(c + 1) * 128])

    transpose_tok_to_fm(xqAtok, xqAT)
    transpose_tok_to_fm(xqFtok, xqFT)

    def weight_stats2(i, wT_dram, numel, clip_eps):
        rows_n = wT_dram.shape[0]
        wT3 = wT_dram.rearrange("(c p) o -> p c o", p=128)
        nco = wT_dram.shape[1] // 1024
        E.gp(lambda e: e.memset(scol[:, 96:97], 0.0))
        ab_prev = [None, None]
        add_prev = [None, None]
        wit = 0
        for c in range(rows_n // 128):
            for ob in range(nco):
                pp = wit % 2
                fh = fX[:, pp * 1024:(pp + 1) * 1024]
                ch = cols[:, 7:8] if pp == 0 else cols[:, 5:6]
                ld = E.dma(fh, wT3[:, c, ob * 1024:(ob + 1) * 1024],
                           after=[ab_prev[pp]] if ab_prev[pp] is not None else None,
                           sem=f"ws{pp}")
                def _abs(e, fh=fh, ch=ch):
                    e.activation(fh, fh, AF.Abs, accum_out=ch)
                    return e.copy(cols[:, 6:7], ch)
                ab = E.act(_abs, after=[ld, add_prev[pp]])
                ad = E.dve(lambda e, ch=ch: e.tensor_tensor(scol[:, 96:97], scol[:, 96:97], ch,
                                                            AluOpType.add), after=[ab])
                ab_prev[pp] = ab
                add_prev[pp] = ad
                wit += 1
        E.pe(lambda e: e.matmul(p11, scol[:, 96:97], onesf[:], start=True, stop=True))
        E.dve(lambda e, i=i, numel=numel: e.tensor_scalar(statc[0:1, 8 + i:9 + i], p11, 1.0 / numel, None, AluOpType.mult))
        if clip_eps is not None:
            E.dve(lambda e, i=i, clip_eps=clip_eps: e.tensor_scalar(
                statc[0:1, 8 + i:9 + i], statc[0:1, 8 + i:9 + i], clip_eps, None, AluOpType.max))
        E.pe(lambda e, i=i: e.matmul(pO[:, 0:1], onesrowf[0:1, :], statc[0:1, 8 + i:9 + i], start=True, stop=True))
        E.dve(lambda e, i=i: e.tensor_copy(statc[:, i:i + 1], pO[:, 0:1]))
        E.dve(lambda e, i=i: e.reciprocal(statc[:, 8 + i:9 + i], statc[:, i:i + 1]))

    WSTATS = [(m1wqT, D * D), (m1wkT, D * D), (m1wvT, D * D), (m1woT, D * D),
              (m1w1T, D * DFF1), (m1w2T, D * DFF1), (m1w3T, DFF1 * D)]
    for i, (wt, ne) in enumerate(WSTATS):
        weight_stats2(i, wt, ne, 1e-5 if i < 4 else None)

    def quant_weight_ternary(i, wT_dram, dst):
        rows_n = wT_dram.shape[0]
        wT3 = wT_dram.rearrange("(c p) o -> p c o", p=128)
        dst3 = dst.rearrange("(c p) o -> p c o", p=128)
        nco = wT_dram.shape[1] // 1024
        t1_prev = [None, None]
        st_prev = [None, None]
        wit = 0
        first = True
        for c in range(rows_n // 128):
            for ob in range(nco):
                pp = wit % 2
                fh = fX[:, pp * 1024:(pp + 1) * 1024]
                xh = x8.rearrange("p n -> p n")[:, pp * 1024:(pp + 1) * 1024]
                sh = sA[:, pp * 1024:(pp + 1) * 1024]
                ld = E.dma(fh, wT3[:, c, ob * 1024:(ob + 1) * 1024],
                           after=(None if first else [t1_prev[pp]]), sem=f"ws{pp}")
                first = False
                q1 = E.dve(lambda e, i=i, fh=fh: e.tensor_scalar(fh, fh, statc[:, 8 + i:9 + i], 1.0,
                                                                 AluOpType.mult, AluOpType.min), after=[ld])
                q2 = E.dve(lambda e, fh=fh, xh=xh: e.tensor_scalar(xh, fh, -1.0, None, AluOpType.max),
                           after=[q1])
                q3 = E.dve(lambda e, xh=xh, sh=sh: e.tensor_copy(sh, xh),
                           after=[q2, st_prev[pp]])
                st = E.dma(dst3[:, c, ob * 1024:(ob + 1) * 1024], sh, after=[q3], sem=f"qs{pp}")
                t1_prev[pp] = q1
                st_prev[pp] = st
                wit += 1

    def quant_weight_sign(wT_dram, dst):
        rows_n = wT_dram.shape[0]
        wT3 = wT_dram.rearrange("(c p) o -> p c o", p=128)
        dst3 = dst.rearrange("(c p) o -> p c o", p=128)
        nco = wT_dram.shape[1] // 1024
        sg_prev = [None, None]
        st_prev = [None, None]
        wit = 0
        first = True
        for c in range(rows_n // 128):
            for ob in range(nco):
                pp = wit % 2
                fh = fX[:, pp * 1024:(pp + 1) * 1024]
                sh = sA[:, pp * 1024:(pp + 1) * 1024]
                ld = E.dma(fh, wT3[:, c, ob * 1024:(ob + 1) * 1024],
                           after=(None if first else [sg_prev[pp]]), sem=f"ws{pp}")
                first = False
                sg = E.act(lambda e, fh=fh, sh=sh: e.activation(sh, fh, AF.Sign),
                           after=[ld, st_prev[pp]])
                st = E.dma(dst3[:, c, ob * 1024:(ob + 1) * 1024], sh, after=[sg], sem=f"qs{pp}")
                sg_prev[pp] = sg
                st_prev[pp] = st
                wit += 1

    quant_weight_ternary(0, m1wqT, wq_q)
    quant_weight_ternary(1, m1wkT, wk_q)
    quant_weight_ternary(2, m1wvT, wv_q)
    quant_weight_ternary(3, m1woT, wo_q)
    quant_weight_sign(m1w1T, w1_q)
    quant_weight_sign(m1w2T, w2_q)
    quant_weight_sign(m1w3T, w3_q)

    for c in range(NC):
        E.dma(xns[:, c * T:(c + 1) * T], xqAT[c * 128:(c + 1) * 128, :])

    proj(QT1, wq_q, NC, scale_col=statc[:, 0:1], extra_scale=SCL)
    proj(KT1, wk_q, NC, scale_col=statc[:, 1:2], extra_scale=1.0)

    for vb in range(4):
        E.dma(bufv(wbufA, NC, 512), wsl(wv_q, vb, NC, 512))
        for t16 in range(NT16):
            for c in range(NC):
                lh = xns[:, c * T + t16 * 128: c * T + (t16 + 1) * 128]
                rh = wbufA[:, c * 512:(c + 1) * 512]
                E.pe(lambda e, lh=lh, rh=rh, c=c: e.matmul(pA[:], lh, rh, start=(c == 0), stop=(c == NC - 1)))
            E.dve(lambda e: e.tensor_scalar(sA[:, 0:512], pA[:], statc[:, 2:3], None, AluOpType.mult))
            E.dma(Vtok1[t16 * 128:(t16 + 1) * 128, vb * 512:(vb + 1) * 512], sA[:, 0:512])

    for c in range(NC):
        E.dma(xns[:, c * T:(c + 1) * T], xqFT[c * 128:(c + 1) * 128, :])

    y_pArr = [pA, pSa]
    y_pBrr = [pB, pSb]
    y_wl = None
    y_pAev = [None, None]
    y_pBev = [None, None]
    y_sst = [None, None]
    y_first = True
    y_it = 0
    for ffb in range(16):
        w1t = E.dma(bufv(wbufA, NC, 512), wsl(w1_q, ffb, NC, 512),
                    after=None if y_first else [y_wl], sem="wa0")
        w2t = E.dma(bufv(wbufB, NC, 512), wsl(w2_q, ffb, NC, 512),
                    after=None if y_first else [y_wl], sem="wb0")
        y_first = False
        for t16 in range(NT16):
            pp = y_it % 2
            mmA = None
            for c in range(NC):
                lh = xns[:, c * T + t16 * 128: c * T + (t16 + 1) * 128]
                mmA = E.pe(lambda e, lh=lh, c=c, pp=pp: e.matmul(
                    y_pArr[pp][:], lh, wbufA[:, c * 512:(c + 1) * 512],
                    start=(c == 0), stop=(c == NC - 1)),
                    after=([w1t, y_pAev[pp]] if c == 0 else []))
            mmB = None
            for c in range(NC):
                lh = xns[:, c * T + t16 * 128: c * T + (t16 + 1) * 128]
                mmB = E.pe(lambda e, lh=lh, c=c, pp=pp: e.matmul(
                    y_pBrr[pp][:], lh, wbufB[:, c * 512:(c + 1) * 512],
                    start=(c == 0), stop=(c == NC - 1)),
                    after=([w2t, y_pBev[pp]] if c == 0 else []))
            sg = E.act(lambda e, pp=pp: e.activation(sA[:, pp * 512:(pp + 1) * 512], y_pArr[pp][:],
                                                     AF.Silu, scale=statc[:, 4:5]), after=[mmA])
            t1_ = E.dve(lambda e, pp=pp: e.tensor_scalar(sB[:, pp * 512:(pp + 1) * 512], y_pBrr[pp][:],
                                                         statc[:, 5:6], None, AluOpType.mult),
                        after=[mmB, y_sst[pp]])
            t2_ = E.dve(lambda e, pp=pp: e.tensor_tensor(sB[:, pp * 512:(pp + 1) * 512],
                                                         sA[:, pp * 512:(pp + 1) * 512],
                                                         sB[:, pp * 512:(pp + 1) * 512], AluOpType.mult),
                        after=[sg, t1_])
            st = E.dma(htok[t16 * 128:(t16 + 1) * 128, ffb * 512:(ffb + 1) * 512],
                       sB[:, pp * 512:(pp + 1) * 512], after=[t2_], sem=f"hs{pp}")
            y_wl = mmB
            y_pAev[pp] = sg
            y_pBev[pp] = t1_
            y_sst[pp] = st
            y_it += 1

    f3h = lambda t: t.rearrange("p (g i) -> p g i", i=64)
    for t16 in range(NT16):
        rsl = slice(t16 * 128, (t16 + 1) * 128)
        E.dma(hbig[:], htok[rsl, :])
        E.dve(lambda e: e.tensor_reduce(scol[:, 0:128], f3h(hbig[:]), AX.X, AluOpType.max, apply_absolute_value=True))
        E.dve(lambda e: e.tensor_scalar(scol[:, 0:128], scol[:, 0:128], 1e-5, None, AluOpType.max))
        E.dve(lambda e: e.reciprocal(scol[:, 128:256], scol[:, 0:128]))
        E.dve(lambda e: e.tensor_scalar(scol[:, 128:256], scol[:, 128:256], 127.0, None, AluOpType.mult))
        E.dve(lambda e: e.tensor_tensor(f3h(h8[:]), f3h(hbig[:]),
                                        scol[:, 128:256].unsqueeze(2).broadcast_to([128, 128, 64]), AluOpType.mult))
        E.dve(lambda e: e.tensor_scalar(scol[:, 128:256], scol[:, 0:128], 1.0 / 127.0, None, AluOpType.mult))
        E.dve(lambda e: e.tensor_tensor(f3h(hbig[:]), f3h(h8[:]),
                                        scol[:, 128:256].unsqueeze(2).broadcast_to([128, 128, 64]), AluOpType.mult))
        for fc in range(NFC):
            sl4 = (fc % 4) * 128
            E.pe(lambda e, fc=fc, sl4=sl4: e.transpose(pT[:, sl4:sl4 + 128], hbig[:, fc * 128:(fc + 1) * 128], identb[:]))
            if fc % 4 == 3:
                base4 = fc - 3
                E.dve(lambda e: e.tensor_copy(sB[:, 0:512], pT[:, 0:512]))
                for j in range(4):
                    E.dma(hqT[(base4 + j) * 128:(base4 + j + 1) * 128, rsl], sB[:, j * 128:(j + 1) * 128])

    y3_pArr = [pA, pSa]
    y3_pAev = [None, None]
    y3_fst = [None, None]
    y3_wl = None
    y3_lastmm = None
    y3_first = True
    y3_it = 0
    for t4 in range(NT4):
        tsl = slice(t4 * 512, (t4 + 1) * 512)
        hq_toks = []
        for fc in range(NFC):
            hq_toks.append(E.dma(xns[:, fc * 512:(fc + 1) * 512], hqT[fc * 128:(fc + 1) * 128, tsl],
                                 after=None if y3_first else [y3_lastmm], sem="he"))
            y3_first = False
        for oc in range(NC):
            pp = y3_it % 2
            wt = E.dma(bufv(wbufW, NFC), wsl(w3_q, oc, NFC),
                       after=[y3_wl] if y3_wl is not None else [hq_toks[0]], sem="wd0")
            mm = None
            for fc in range(NFC):
                mm = E.pe(lambda e, fc=fc, pp=pp: e.matmul(
                    y3_pArr[pp][:], wbufW[:, fc * 128:(fc + 1) * 128],
                    xns[:, fc * 512:(fc + 1) * 512],
                    start=(fc == 0), stop=(fc == NFC - 1)),
                    after=(hq_toks + [wt, y3_pAev[pp]] if fc == 0 else []))
            ldst = fX[:, pp * 512:(pp + 1) * 512]
            odst = fX[:, 1024 + pp * 512: 1024 + (pp + 1) * 512]
            ld = E.dma(ldst, x1T[oc * 128:(oc + 1) * 128, tsl],
                       after=[y3_fst[pp]] if y3_fst[pp] is not None else [])
            ev = E.dve(lambda e, pp=pp, ldst=ldst, odst=odst: e.scalar_tensor_tensor(
                odst, y3_pArr[pp][:], statc[:, 6:7], ldst, AluOpType.mult, AluOpType.add),
                after=[mm, ld])
            st = E.dma(accT1[oc * 128:(oc + 1) * 128, tsl], odst, after=[ev], sem=f"xo{pp}")
            y3_wl = mm
            y3_lastmm = mm
            y3_pAev[pp] = ev
            y3_fst[pp] = st
            y3_it += 1

    emit_attention(QT1, KT1, Vtok1, OT1, 1, NH)

    for t16 in range(NT16):
        rsl = slice(t16 * 128, (t16 + 1) * 128)
        for c in range(NC):
            E.dma(sA[:, c * 128:(c + 1) * 128], OT1[c * 128:(c + 1) * 128, rsl])
        for c in range(NC):
            sl4 = (c % 4) * 128
            E.pe(lambda e, c=c, sl4=sl4: e.transpose(pT[:, sl4:sl4 + 128], sA[:, c * 128:(c + 1) * 128], identb[:]))
            if c % 4 == 3:
                base = (c // 4) * 512
                E.dve(lambda e, base=base: e.tensor_copy(sB[:, base:base + 512], pT[:, 0:512]))
        E.dve(lambda e: e.tensor_reduce(cols[:, 3:4], sB[:], AX.X, AluOpType.max, apply_absolute_value=True))
        E.dve(lambda e: e.tensor_scalar(cols[:, 3:4], cols[:, 3:4], 1e-5, None, AluOpType.max))
        E.dve(lambda e: e.reciprocal(cols[:, 4:5], cols[:, 3:4]))
        E.dve(lambda e: e.tensor_scalar(x8, sB[:], cols[:, 4:5], 127.0, AluOpType.mult, AluOpType.mult))
        E.dve(lambda e: e.tensor_scalar(sB[:], x8, cols[:, 3:4], 1.0 / 127.0, AluOpType.mult, AluOpType.mult))
        E.dma(Oqtok[rsl, :], sB[:])
    transpose_tok_to_fm(Oqtok, OqT)

    for oc in range(NC):
        E.dma(bufv(wbufA, NC), wsl(wo_q, oc, NC))
        for t4 in range(NT4):
            tsl = slice(t4 * 512, (t4 + 1) * 512)
            for c in range(NC):
                E.dma(sA[:, 0:512], OqT[c * 128:(c + 1) * 128, tsl])
                E.pe(lambda e, c=c: e.matmul(pA[:], wbufA[:, c * 128:(c + 1) * 128], sA[:, 0:512],
                                             start=(c == 0), stop=(c == NC - 1)))
            E.dma(fA[:], accT1[oc * 128:(oc + 1) * 128, tsl])
            E.dve(lambda e: e.scalar_tensor_tensor(fB[:], pA[:], statc[:, 3:4], fA[:],
                                                   AluOpType.mult, AluOpType.add))
            E.dma(outT[oc * 128:(oc + 1) * 128, tsl], fB[:])

    E.finalize(nc, block, sempool)
    es.close()
    return nc



def stage_l0(inputs, b):
    f32 = np.float32
    x = np.asarray(inputs["L0_attn_x"][b], dtype=f32)
    mask, onesb, onesf, ident, onesrowf = _consts_host()
    cos2, sin2 = _rope_tables()
    m = {
        "xT": np.ascontiguousarray(x.T),
        "wqT": np.ascontiguousarray(np.asarray(inputs["L0_attn_Wq"], f32).T),
        "wkT": np.ascontiguousarray(np.asarray(inputs["L0_attn_Wk"], f32).T),
        "wvT": np.ascontiguousarray(np.asarray(inputs["L0_attn_Wv"], f32).T),
        "woT": np.ascontiguousarray(np.asarray(inputs["L0_attn_Wo"], f32).T),
        "bcols": np.concatenate([
            _chunk_cols(np.asarray(inputs["L0_attn_bq"], f32)),
            _chunk_cols(np.asarray(inputs["L0_attn_bk"], f32)),
            _chunk_cols(np.asarray(inputs["L0_attn_bv"], f32)),
            _chunk_cols(np.asarray(inputs["L0_attn_bo"], f32))], axis=1),
        "bvrow": np.ascontiguousarray(np.asarray(inputs["L0_attn_bv"], f32)[None, :]),
        "qg": np.ascontiguousarray(np.asarray(inputs["L0_attn_q_gain"], f32)[None, :]),
        "ropec": cos2, "ropes": sin2,
        "maskc": mask, "onesb": onesb, "identb": ident,
        "onesf": onesf, "onesrowf": onesrowf,
        "routerT": np.ascontiguousarray(np.asarray(inputs["L0_ffn_router_w"], f32).T),
        "sgT": np.ascontiguousarray(np.asarray(inputs["L0_ffn_shared_gate"], f32).T),
        "suT": np.ascontiguousarray(np.asarray(inputs["L0_ffn_shared_up"], f32).T),
        "sdT": np.ascontiguousarray(np.asarray(inputs["L0_ffn_shared_down"], f32).T),
    }
    for e in range(NEXP):
        m[f"wgT{e}"] = np.ascontiguousarray(np.asarray(inputs["L0_ffn_w_gate"][e], f32).T)
        m[f"wuT{e}"] = np.ascontiguousarray(np.asarray(inputs["L0_ffn_w_up"][e], f32).T)
        m[f"wdT{e}"] = np.ascontiguousarray(np.asarray(inputs["L0_ffn_w_down"][e], f32).T)
    return m




def _sel8_host():
    m = np.zeros((8, 8 * 128), dtype=bfdt)
    for e in range(8):
        m[e, e * 128:(e + 1) * 128] = 1.0
    return m


def stage_fused(inputs, b):
    m = stage_l0(inputs, b)
    f32 = np.float32
    m["sel8c"] = _sel8_host()
    m["m1wqT"] = np.ascontiguousarray(np.asarray(inputs["L1_attn_w_q"], f32).T)
    m["m1wkT"] = np.ascontiguousarray(np.asarray(inputs["L1_attn_w_k"], f32).T)
    m["m1wvT"] = np.ascontiguousarray(np.asarray(inputs["L1_attn_w_v"], f32).T)
    m["m1woT"] = np.ascontiguousarray(np.asarray(inputs["L1_attn_w_o"], f32).T)
    m["m1w1T"] = np.ascontiguousarray(np.asarray(inputs["L1_ffn_w1"], f32).T)
    m["m1w2T"] = np.ascontiguousarray(np.asarray(inputs["L1_ffn_w2"], f32).T)
    m["m1w3T"] = np.ascontiguousarray(np.asarray(inputs["L1_ffn_w3"], f32).T)
    return m


def run_fused(inputs, cores=8, dev=False):
    nc = build_fused(dev=dev)
    in_maps = [stage_fused(inputs, b) for b in range(cores)]
    res = run_bass_kernel_spmd(nc, in_maps, core_ids=list(range(cores)))
    return res.results


def kernel(**inputs):
    n = 8
    res = run_fused(inputs, cores=n, dev=False)
    out = np.stack([np.asarray(res[b]["outT"], dtype=np.float32).T for b in range(n)], axis=0)
    return out


# version 07 (final)
# speedup vs baseline: 1.8534x; 1.0297x over previous
import numpy as np
import ml_dtypes
import concourse.bass as bass
import concourse.mybir as mybir
from concourse.bass_utils import run_bass_kernel_spmd
from concourse.alu_op_type import AluOpType
from contextlib import ExitStack

F32 = mybir.dt.float32
BF16 = mybir.dt.bfloat16
I32 = mybir.dt.int32
I8 = mybir.dt.int8
AF = mybir.ActivationFunctionType
AX = mybir.AxisListType
bfdt = ml_dtypes.bfloat16

T = 2048
D = 2048
NC = 16
NT4 = 4
NT16 = 16
HD = 128
NH = 16
KVH0 = 4
DFF1 = 8192
NFC = DFF1 // 128
NI = 1408
NIC = 11
NEXP = 8
NEG = -1e30


class Emit:

    def __init__(self):
        self.steps = []

    def _add(self, eng, kind, fn, inc, after, sem):
        self.steps.append((eng, kind, fn, inc, after, sem))
        return len(self.steps) - 1

    def dma(self, dst, src, after=None, sem="Q"):
        return self._add("sync", "dma", lambda e: e.dma_start(dst, src), 16, after, sem)

    def dmag(self, dst, src, after=None, sem="GD"):
        return self._add("gpsimd", "dma", lambda e: e.dma_start(dst, src), 16, after, sem)

    def pe(self, fn, after=None):
        return self._add("tensor", "c", fn, 1, after, "T")

    def dve(self, fn, after=None):
        return self._add("vector", "c", fn, 1, after, "V")

    def act(self, fn, after=None):
        return self._add("scalar", "c", fn, 1, after, "S")

    def gp(self, fn, after=None):
        return self._add("gpsimd", "c", fn, 1, after, "G")

    def finalize(self, nc, block, sempool):
        steps = self.steps
        n = len(steps)
        cumsem = {}
        completion = []
        for eng, kind, fn, inc, after, sem in steps:
            cumsem[sem] = cumsem.get(sem, 0) + inc
            completion.append((sem, cumsem[sem]))
        totals = dict(cumsem)
        running = {}
        targets = [None] * n
        for i, (eng, kind, fn, inc, after, sem) in enumerate(steps):
            if after is None:
                tg = dict(running)
            else:
                toks = after if isinstance(after, (list, tuple)) else [after]
                tg = {}
                for t in toks:
                    if t is None:
                        continue
                    smt, val = completion[t]
                    if tg.get(smt, 0) < val:
                        tg[smt] = val
            targets[i] = tg
            running[sem] = running.get(sem, 0) + inc

        sems = {name: sempool(name) for name in totals}

        def emit(eng_obj, name):
            lastw = {}
            for (eng, kind, fn, inc, after, sem), tg in zip(steps, targets):
                if eng != name:
                    continue
                for smt in tg:
                    val = tg[smt]
                    if val > lastw.get(smt, 0):
                        eng_obj.wait_ge(sems[smt], val)
                        lastw[smt] = val
                fn(eng_obj).then_inc(sems[sem], inc)
            return lastw

        @block.sync
        def _(sync):
            emit(sync, "sync")
            for smt, tot in sorted(totals.items()):
                sync.wait_ge(sems[smt], tot)

        @block.vector
        def _(vector):
            emit(vector, "vector")

        @block.scalar
        def _(scalar):
            emit(scalar, "scalar")

        @block.tensor
        def _(tensor):
            emit(tensor, "tensor")

        @block.gpsimd
        def _(gpsimd):
            emit(gpsimd, "gpsimd")



def _chunk_cols(vec):
    n = vec.shape[0] // 128
    return np.ascontiguousarray(vec.reshape(n, 128).T.astype(np.float32))

def _rope_tables():
    inv_freq = 1.0 / (10000.0 ** (np.arange(0, HD, 2, dtype=np.float64) / HD))
    t = np.arange(T, dtype=np.float64)
    fr = np.outer(t, inv_freq)
    cos = np.cos(fr).T
    sin = np.sin(fr).T
    cos2 = np.vstack([cos, cos]).astype(bfdt)
    sin2 = np.vstack([sin, -sin]).astype(bfdt)
    return np.ascontiguousarray(cos2), np.ascontiguousarray(sin2)

def _consts_host():
    mask = np.where(np.tril(np.ones((128, 128), dtype=bool)), 0.0, NEG).astype(np.float32)
    onesb = np.ones((128, 128), dtype=bfdt)
    onesf = np.ones((128, 1), dtype=np.float32)
    ident = np.eye(128, dtype=bfdt)
    onesrow_f = np.ones((1, 128), dtype=np.float32)
    return mask, onesb, onesf, ident, onesrow_f



def build_fused(dev=False):
    nc = bass.Bass("TRN2", target_bir_lowering=False)
    E = Emit()
    es = ExitStack()

    def din(name, shape, dt=F32):
        return nc.dram_tensor(name, shape, dt, kind="ExternalInput").ap()

    def dout(name, shape, dt=F32):
        return nc.dram_tensor(name, shape, dt, kind="ExternalOutput").ap()

    def dint(name, shape, dt=BF16):
        kind = "ExternalOutput" if dev else "Internal"
        return nc.dram_tensor(name, shape, dt, kind=kind).ap()

    xT = din("xT", [D, T])
    wqT = din("wqT", [D, D])
    wkT = din("wkT", [D, 512])
    wvT = din("wvT", [D, 512])
    woT = din("woT", [D, D])
    bcols = din("bcols", [128, 40])
    bvrow = din("bvrow", [1, 512])
    qg = din("qg", [1, 16])
    ropec = din("ropec", [128, T], BF16)
    ropes = din("ropes", [128, T], BF16)
    maskc = din("maskc", [128, 128])
    onesb_in = din("onesb", [128, 128], BF16)
    identb_in = din("identb", [128, 128], BF16)
    onesf_in = din("onesf", [128, 1])
    onesrowf_in = din("onesrowf", [1, 128])
    sel8_in = din("sel8c", [8, 8 * 128], BF16)
    routerT = din("routerT", [D, 8])
    sgT = din("sgT", [D, NI])
    suT = din("suT", [D, NI])
    sdT = din("sdT", [NI, D])
    wgTs = [din(f"wgT{e}", [D, NI]) for e in range(NEXP)]
    wuTs = [din(f"wuT{e}", [D, NI]) for e in range(NEXP)]
    wdTs = [din(f"wdT{e}", [NI, D]) for e in range(NEXP)]

    outT = dout("outT", [D, T])

    x1T = dint("x1T", [D, T], F32)
    xnT = dint("xnT", [D, T])
    qrawT = dint("qrawT", [D, T])
    QT = dint("QT", [D, T])
    KT = dint("KT", [512, T])
    Vtok = dint("Vtok", [T, 512])
    OT = dint("OT", [D, T])
    hshT = dint("hshT", [NI, T])
    heT = dint("heT", [NI, T])
    accT = dint("accTl0", [D, T], F32)

    sb = lambda name, shape, dt=BF16: es.enter_context(nc.sbuf_tensor(name, shape, dt))
    xns = sb("xns", [128, NC * T])
    wbufA = sb("wbufA", [128, 16 * 512])
    wbufB = sb("wbufB", [128, 16 * 512])
    wbufW = sb("wbufW", [128, 64 * 128])
    wbufC = wbufW
    sA = sb("sA", [128, 2048])
    sB = sb("sB", [128, 2048])
    fA = sb("fA", [128, 512], F32)
    fB = sb("fB", [128, 512], F32)
    ktile = sb("ktile", [128, 2048])
    vtile = sb("vtile", [128, 2048])
    ptile = sb("ptile", [128, 2048])
    sD = ptile
    pttile = sb("pttile", [128, 2048])
    cbs_full = pttile
    qtile = sb("qtile", [128, 256])
    rows = sb("rows", [1, 1536], F32)
    rowsb = sb("rowsb", [1, 2 * T])
    comb8 = sb("comb8", [8, T])
    sel8 = sb("sel8", [8, 8 * 128])
    combtok = sb("combtok", [128, 16 * 8], F32)
    mbs = sb("mbs", [128, 512])
    rbs = sb("rbs", [128, 512])
    cols = sb("cols", [128, 8], F32)
    bvb = sb("bvb", [128, 512])
    maskb = sb("maskb", [128, 128], F32)
    onesb = sb("onesbS", [128, 128])
    identb = sb("identbS", [128, 128])
    onesf = sb("onesfS", [128, 1], F32)
    onesrowf = sb("onesrowfS", [1, 128], F32)
    biasc = sb("biascS", [128, 40], F32)
    qgS = sb("qgS", [1, 16], F32)
    epsc = sb("epsc", [1, 2], F32)
    hbig = sb("hbig", [128, DFF1])
    h8 = sb("h8", [128, DFF1], I8)
    x8 = h8[:, 0:2048]
    fX = sb("fX", [128, 2048], F32)
    scol = sb("scol", [128, 256], F32)
    statc = sb("statc", [128, 16], F32)
    eps128 = sb("eps128S", [128, 1], F32)

    ps = lambda name, shape, dt=F32: es.enter_context(nc.psum_tensor(name, shape, dt))
    pS = ps("pS", [128, 2048])
    pA = ps("pA", [128, 512])
    pB = ps("pB", [128, 512])
    pT = ps("pT", [128, 512], BF16)
    pO = ps("pO", [128, 512])
    pR = pO[0:1, 0:512]
    pSa = pS[:, 0:512]
    pSb = pS[:, 512:1024]
    acc_store_tok = {}

    def sempool(name):
        return es.enter_context(nc.semaphore(f"sp_{name}"))
    block = es.enter_context(nc.Block())

    E.dma(maskb[:], maskc)
    E.dma(onesb[:], onesb_in)
    E.dma(identb[:], identb_in)
    E.dma(onesf[:], onesf_in)
    E.dma(onesrowf[:], onesrowf_in)
    E.dma(biasc[:], bcols)
    E.dma(qgS[:], qg)
    E.gp(lambda e: e.memset(epsc[:, 0:1], 1e-5))
    E.gp(lambda e: e.memset(epsc[:, 1:2], 1.1920929e-07))
    E.gp(lambda e: e.memset(eps128[:], 1e-5))
    E.dma(sel8[:], sel8_in)
    ropec_s = sb("ropecS", [128, T])
    ropes_s = sb("ropesS", [128, T])
    E.dma(ropec_s[:], ropec)
    E.dma(ropes_s[:], ropes)
    E.dma(rows[:, 0:512], bvrow)
    E.dve(lambda e: e.tensor_copy(rowsb[:, 0:512], rows[:, 0:512]))
    E.pe(lambda e: e.matmul(pA[:], onesb[0:1, :], rowsb[0:1, 0:512], start=True, stop=True))
    E.dve(lambda e: e.tensor_copy(bvb[:], pA[:]))

    def wsl(wT_dram, oc, nch, width=128):
        return wT_dram.rearrange("(c p) o -> p c o", p=128)[:, :, oc * width:(oc + 1) * width]

    def bufv(buf, nch, width=128):
        return buf[:, 0:nch * width].rearrange("p (c o) -> p c o", o=width)

    ONE_COL = onesb[:, 0:1]
    ONE_ROW = onesb[0:1, :]

    def bcast_row_bf16(src_row_bf, psum, dst_bf, n):
        E.pe(lambda e: e.matmul(psum[:, 0:n], ONE_ROW, src_row_bf, start=True, stop=True))
        E.dve(lambda e: e.tensor_copy(dst_bf[:, 0:n], psum[:, 0:n]))

    for t4 in range(NT4):
        tsl = slice(t4 * 512, (t4 + 1) * 512)
        for c in range(NC):
            xv = xns[:, c * T + t4 * 512: c * T + (t4 + 1) * 512]
            E.dmag(xv, xT[c * 128:(c + 1) * 128, tsl])
        for c in range(NC):
            xv = xns[:, c * T + t4 * 512: c * T + (t4 + 1) * 512]
            E.pe(lambda e, xv=xv, c=c: e.matmul(pR, ONE_COL, xv, start=(c == 0), stop=(c == NC - 1)))
        E.dve(lambda e: e.tensor_scalar(rows[:, 0:512], pR, 1.0 / D, None, AluOpType.mult))
        for c in range(NC):
            xv = xns[:, c * T + t4 * 512: c * T + (t4 + 1) * 512]
            E.act(lambda e, xv=xv: e.activation(sA[:, 0:512], xv, AF.Square))
            E.pe(lambda e, c=c: e.matmul(pR, ONE_COL, sA[:, 0:512], start=(c == 0), stop=(c == NC - 1)))
        mrow = rows[:, 0:512]
        e2row = rows[:, 512:1024]
        E.dve(lambda e, e2row=e2row: e.tensor_scalar(e2row, pR, 1.0 / D, None, AluOpType.mult))
        msq = rows[:, 1024:1536]
        E.dve(lambda e, mrow=mrow, msq=msq: e.tensor_tensor(msq, mrow, mrow, AluOpType.mult))
        E.dve(lambda e, e2row=e2row, msq=msq: e.tensor_tensor(e2row, e2row, msq, AluOpType.subtract))
        E.act(lambda e, e2row=e2row: e.activation(e2row, e2row, AF.Ln, bias=epsc[:, 0:1]))
        E.act(lambda e, e2row=e2row: e.activation(e2row, e2row, AF.Exp, scale=-0.5))
        E.dve(lambda e, mrow=mrow, t4=t4: e.tensor_copy(rowsb[:, t4 * 512:(t4 + 1) * 512], mrow))
        E.dve(lambda e, e2row=e2row, t4=t4: e.tensor_copy(rowsb[:, T + t4 * 512: T + (t4 + 1) * 512], e2row))
    for t4 in range(NT4):
        bcast_row_bf16(rowsb[0:1, t4 * 512:(t4 + 1) * 512], pA, mbs, 512)
        bcast_row_bf16(rowsb[0:1, T + t4 * 512: T + (t4 + 1) * 512], pB, rbs, 512)
        for c in range(NC):
            xv = xns[:, c * T + t4 * 512: c * T + (t4 + 1) * 512]
            E.dve(lambda e, xv=xv: e.tensor_tensor(sA[:, 0:512], xv, mbs[:], AluOpType.subtract))
            E.dve(lambda e, xv=xv: e.tensor_tensor(xv, sA[:, 0:512], rbs[:], AluOpType.mult))
            if dev:
                E.dma(xnT[c * 128:(c + 1) * 128, t4 * 512:(t4 + 1) * 512], xv)

    def proj(dst_dram, wT_dram, out_chunks, bias_off=None, scale_col=None, extra_scale=None):
        pArr = [pA, pSa]
        wlast = [None, None]
        pAev = [None, None]
        sst = [None, None]
        first = True
        it = 0
        for oc in range(out_chunks):
            pw = oc % 2
            wa_t = E.dmag(bufv(wbufA[:, pw * 2048:(pw + 1) * 2048], NC), wsl(wT_dram, oc, NC),
                          after=None if first else [wlast[pw]], sem=f"wa{pw}")
            first = False
            for t4 in range(NT4):
                pp = it % 2
                mm = None
                for c in range(NC):
                    xv = xns[:, c * T + t4 * 512: c * T + (t4 + 1) * 512]
                    mm = E.pe(lambda e, c=c, xv=xv, pp=pp, pw=pw: e.matmul(
                        pArr[pp][:], wbufA[:, pw * 2048 + c * 128: pw * 2048 + (c + 1) * 128], xv,
                        start=(c == 0), stop=(c == NC - 1)),
                        after=([wa_t, pAev[pp]] if c == 0 else []))
                if bias_off is not None:
                    bc = biasc[:, bias_off + oc: bias_off + oc + 1]
                    ev = E.dve(lambda e, bc=bc, pp=pp: e.tensor_scalar(
                        sA[:, pp * 512:(pp + 1) * 512], pArr[pp][:], bc, None, AluOpType.add),
                        after=[mm, sst[pp]])
                elif scale_col is not None:
                    ev = E.dve(lambda e, pp=pp: e.tensor_scalar(
                        sA[:, pp * 512:(pp + 1) * 512], pArr[pp][:], scale_col, extra_scale,
                        AluOpType.mult, AluOpType.mult if extra_scale is not None else AluOpType.bypass),
                        after=[mm, sst[pp]])
                else:
                    ev = E.dve(lambda e, pp=pp: e.tensor_copy(sA[:, pp * 512:(pp + 1) * 512], pArr[pp][:]),
                               after=[mm, sst[pp]])
                st = E.dma(dst_dram[oc * 128:(oc + 1) * 128, t4 * 512:(t4 + 1) * 512],
                           sA[:, pp * 512:(pp + 1) * 512], after=[ev], sem=f"hs{pp}")
                wlast[pw] = mm
                pAev[pp] = ev
                sst[pp] = st
                it += 1

    proj(qrawT, wqT, 16, bias_off=0)

    kraw = dint("krawT", [512, T])
    proj(kraw, wkT, 4, bias_off=16)

    SCL = float(1.0 / np.sqrt(HD))

    def rms_rope(src_dram, dst_dram, n_heads, head_of, gain_fold):
        bufs = [(sA, sB, ptile), (hbig[:, 0:2048], hbig[:, 2048:4096], hbig[:, 4096:6144])]
        ld_war = [None, None]
        st_prev = [None, None]
        first = True
        for h in range(n_heads):
            hc = head_of(h)
            pp = h % 2
            qb, tb, db = bufs[pp]
            ld = E.dma(qb[:], src_dram[hc * 128:(hc + 1) * 128, :],
                       after=None if first else [ld_war[pp]], sem=f"rr{pp}")
            first = False
            rr_toks = []
            l1_prev = None
            for t4 in range(NT4):
                tsl = slice(t4 * 512, (t4 + 1) * 512)
                sq = E.act(lambda e, tsl=tsl, qb=qb, tb=tb: e.activation(tb[:, 0:512], qb[:, tsl], AF.Square),
                           after=[ld])
                mmt = E.pe(lambda e, tb=tb: e.matmul(pR, ONE_COL, tb[:, 0:512], start=True, stop=True),
                           after=[sq, l1_prev])
                l1 = E.act(lambda e: e.activation(rows[:, 0:512], pR, AF.Ln, bias=epsc[:, 1:2], scale=1.0 / HD),
                           after=[mmt])
                l1_prev = l1
                l2 = E.act(lambda e: e.activation(rows[:, 0:512], rows[:, 0:512], AF.Exp, scale=-0.5),
                           after=[l1])
                if gain_fold is not None:
                    gf = gain_fold(h)
                    l2 = E.dve(lambda e, gf=gf: e.tensor_scalar(rows[:, 0:512], rows[:, 0:512], gf, SCL,
                                                                AluOpType.mult, AluOpType.mult), after=[l2])
                rr_toks.append(E.dve(lambda e, t4=t4: e.tensor_copy(
                    rowsb[:, t4 * 512:(t4 + 1) * 512], rows[:, 0:512]), after=[l2]))
            sw1 = E.dma(db[0:64, :], qb[64:128, :], after=[ld])
            sw2 = E.dma(db[64:128, :], qb[0:64, :], after=[ld])
            m1 = E.dve(lambda e, qb=qb, tb=tb: e.tensor_tensor(tb[:], qb[:], ropec_s[:], AluOpType.mult),
                       after=[sq if False else ld] + rr_toks[-1:])
            m2 = E.dve(lambda e, db=db: e.tensor_tensor(db[:], db[:], ropes_s[:], AluOpType.mult),
                       after=[sw1, sw2])
            m3 = E.dve(lambda e, tb=tb, db=db: e.tensor_tensor(tb[:], tb[:], db[:], AluOpType.add),
                       after=[m1, m2])
            stt_ = None
            cp_prev = None
            for t4 in range(NT4):
                tsl = slice(t4 * 512, (t4 + 1) * 512)
                bc = E.pe(lambda e, t4=t4: e.matmul(pA[:], ONE_ROW, rowsb[0:1, t4 * 512:(t4 + 1) * 512],
                                                    start=True, stop=True), after=[rr_toks[t4], cp_prev])
                cp = E.dve(lambda e: e.tensor_copy(rbs[:], pA[:]), after=[bc])
                cp_prev = cp
                mt = E.dve(lambda e, tsl=tsl, tb=tb: e.tensor_tensor(tb[:, tsl], tb[:, tsl], rbs[:],
                                                                     AluOpType.mult), after=[cp, m3])
                stt_ = E.dma(dst_dram[hc * 128:(hc + 1) * 128, tsl], tb[:, tsl],
                             after=[mt, st_prev[pp]], sem=f"rs{pp}")
                st_prev[pp] = stt_
            ld_war[pp] = m3

    rms_rope(qrawT, QT, 16, lambda h: h,
             gain_fold=lambda h: qgS[0:1, h:h + 1])
    rms_rope(kraw, KT, 4, lambda h: h, gain_fold=None)

    E.dmag(bufv(wbufC, NC, 512), wsl(wvT, 0, NC, 512))
    v_pArr = [pA, pSa]
    v_ev = [None, None]
    v_st = [None, None]
    for t16 in range(NT16):
        pp = t16 % 2
        mm = None
        for c in range(NC):
            lh = xns[:, c * T + t16 * 128: c * T + (t16 + 1) * 128]
            rh = wbufC[:, c * 512:(c + 1) * 512]
            mm = E.pe(lambda e, lh=lh, rh=rh, c=c, pp=pp: e.matmul(
                v_pArr[pp][:], lh, rh, start=(c == 0), stop=(c == NC - 1)),
                after=(None if t16 == 0 else ([v_ev[pp]] if c == 0 else [])))
        ev = E.dve(lambda e, pp=pp: e.scalar_tensor_tensor(sA[:, pp * 512:(pp + 1) * 512],
                                                           v_pArr[pp][:], 1.0, bvb[:],
                                                           AluOpType.mult, AluOpType.add),
                   after=[mm, v_st[pp]])
        st = E.dma(Vtok[t16 * 128:(t16 + 1) * 128, :], sA[:, pp * 512:(pp + 1) * 512],
                   after=[ev], sem=f"hs{pp}")
        v_ev[pp] = ev
        v_st[pp] = st

    E.dmag(bufv(wbufA, NC, 8), wsl(routerT, 0, NC, 8))
    for t16 in range(NT16):
        for c in range(NC):
            lh = xns[:, c * T + t16 * 128: c * T + (t16 + 1) * 128]
            E.pe(lambda e, lh=lh, c=c: e.matmul(pA[:, 0:8], lh, wbufA[:, c * 8:(c + 1) * 8],
                                                start=(c == 0), stop=(c == NC - 1)))
        E.dve(lambda e: e.tensor_reduce(cols[:, 0:1], pA[:, 0:8], AX.X, AluOpType.max))
        E.dve(lambda e: e.tensor_scalar(cols[:, 1:2], cols[:, 0:1], -1.0, None, AluOpType.mult))
        def _exp_router(e):
            e.activation(fA[:, 0:8], pA[:, 0:8], AF.Exp, bias=cols[:, 1:2], accum_out=cols[:, 2:3])
            return e.copy(cols[:, 6:7], cols[:, 2:3])
        E.act(_exp_router)
        E.dve(lambda e: e.reciprocal(cols[:, 3:4], cols[:, 2:3]))
        E.dve(lambda e: e.tensor_scalar(fA[:, 0:8], fA[:, 0:8], cols[:, 3:4], None, AluOpType.mult))
        E.dve(lambda e: e.tensor_reduce(cols[:, 4:5], fA[:, 0:8], AX.X, AluOpType.max))
        E.dve(lambda e: e.tensor_scalar(fB[:, 0:8], fA[:, 0:8], cols[:, 4:5], None, AluOpType.is_ge))
        E.dve(lambda e: e.scalar_tensor_tensor(fB[:, 0:8], fB[:, 0:8], NEG, fA[:, 0:8], AluOpType.mult, AluOpType.add))
        E.dve(lambda e: e.tensor_reduce(cols[:, 5:6], fB[:, 0:8], AX.X, AluOpType.max))
        E.dve(lambda e: e.tensor_scalar(fB[:, 0:8], fA[:, 0:8], cols[:, 5:6], None, AluOpType.is_ge))
        E.dve(lambda e, t16=t16: e.tensor_tensor(combtok[:, t16 * 8:(t16 + 1) * 8], fA[:, 0:8], fB[:, 0:8], AluOpType.mult))
    for t16 in range(NT16):
        E.dve(lambda e, t16=t16: e.tensor_copy(sA[:, 0:8], combtok[:, t16 * 8:(t16 + 1) * 8]))
        E.pe(lambda e: e.transpose(pT[0:8, 0:128], sA[:, 0:8], identb[:]))
        E.dve(lambda e, t16=t16: e.tensor_copy(comb8[0:8, t16 * 128:(t16 + 1) * 128], pT[0:8, 0:128]))

    def gated_ffn(gT, uT, dst_dram, comb_e=None, tag=""):
        cb_tok = None
        if comb_e is not None:
            for t4 in range(NT4):
                E.pe(lambda e, t4=t4, comb_e=comb_e: e.matmul(
                    pO[:], sel8[:, comb_e * 128:(comb_e + 1) * 128],
                    comb8[0:8, t4 * 512:(t4 + 1) * 512], start=True, stop=True))
                cb_tok = E.dve(lambda e, t4=t4: e.tensor_copy(cbs_full[:, t4 * 512:(t4 + 1) * 512], pO[:]))
        pArr = [pA, pSa]
        pBrr = [pB, pSb]
        wlast = [None, None]
        pAev = [None, None]
        pBev = [None, None]
        sBst = [None, None]
        first = True
        it = 0
        for oc in range(NIC):
            pw = oc % 2
            wa_t = E.dmag(bufv(wbufA[:, pw * 2048:(pw + 1) * 2048], NC), wsl(gT, oc, NC),
                          after=None if first else [wlast[pw]], sem=f"wa{pw}")
            wb_t = E.dmag(bufv(wbufB[:, pw * 2048:(pw + 1) * 2048], NC), wsl(uT, oc, NC),
                          after=None if first else [wlast[pw]], sem=f"wb{pw}")
            first = False
            for t4 in range(NT4):
                pp = it % 2
                mmA = None
                for c in range(NC):
                    xv = xns[:, c * T + t4 * 512: c * T + (t4 + 1) * 512]
                    mmA = E.pe(lambda e, c=c, xv=xv, pp=pp, pw=pw: e.matmul(
                        pArr[pp][:], wbufA[:, pw * 2048 + c * 128: pw * 2048 + (c + 1) * 128], xv,
                        start=(c == 0), stop=(c == NC - 1)),
                        after=([wa_t, pAev[pp], cb_tok] if c == 0 else []))
                mmB = None
                for c in range(NC):
                    xv = xns[:, c * T + t4 * 512: c * T + (t4 + 1) * 512]
                    mmB = E.pe(lambda e, c=c, xv=xv, pp=pp, pw=pw: e.matmul(
                        pBrr[pp][:], wbufB[:, pw * 2048 + c * 128: pw * 2048 + (c + 1) * 128], xv,
                        start=(c == 0), stop=(c == NC - 1)),
                        after=([wb_t, pBev[pp]] if c == 0 else []))
                sg = E.act(lambda e, pp=pp: e.activation(sA[:, pp * 512:(pp + 1) * 512], pArr[pp][:], AF.Sigmoid),
                           after=[mmA])
                tt = E.dve(lambda e, pp=pp: e.tensor_tensor(sB[:, pp * 512:(pp + 1) * 512],
                                                            sA[:, pp * 512:(pp + 1) * 512], pBrr[pp][:],
                                                            AluOpType.mult), after=[sg, mmB, sBst[pp]])
                ttl = tt
                if comb_e is not None:
                    ttl = E.dve(lambda e, pp=pp, t4=t4: e.tensor_tensor(
                        sB[:, pp * 512:(pp + 1) * 512], sB[:, pp * 512:(pp + 1) * 512],
                        cbs_full[:, t4 * 512:(t4 + 1) * 512], AluOpType.mult), after=[ttl])
                st = E.dma(dst_dram[oc * 128:(oc + 1) * 128, t4 * 512:(t4 + 1) * 512],
                           sB[:, pp * 512:(pp + 1) * 512], after=[ttl], sem=f"hs{pp}")
                wlast[pw] = mmB
                pAev[pp] = sg
                pBev[pp] = tt
                sBst[pp] = st
                it += 1

    def down_proj(dT, src_dram, first_flag):
        pArr = [pA, pSa]
        wlast = [None, None]
        pAev = [None, None]
        fev = [None, None]
        he_last_mm = None
        first = True
        it = 0
        for t4 in range(NT4):
            tsl = slice(t4 * 512, (t4 + 1) * 512)
            he_toks = []
            for ic in range(NIC):
                he_toks.append(E.dma(wbufW[:, ic * 512:(ic + 1) * 512],
                                     src_dram[ic * 128:(ic + 1) * 128, tsl],
                                     after=None if first else [he_last_mm], sem="he"))
                first = False
            for oc in range(NC):
                pw = oc % 2
                pp = it % 2
                wd_t = E.dmag(bufv(wbufA[:, pw * 2048:(pw + 1) * 2048], NIC), wsl(dT, oc, NIC),
                              after=[wlast[pw]] if wlast[pw] is not None else [he_toks[0]],
                              sem=f"wd{pw}")
                mm = None
                for ic in range(NIC):
                    mm = E.pe(lambda e, ic=ic, pp=pp, pw=pw: e.matmul(
                        pArr[pp][:], wbufA[:, pw * 2048 + ic * 128: pw * 2048 + (ic + 1) * 128],
                        wbufW[:, ic * 512:(ic + 1) * 512],
                        start=(ic == 0), stop=(ic == NIC - 1)),
                        after=(he_toks + [wd_t, pAev[pp]] if ic == 0 else []))
                fbuf = fA if pp == 0 else fB
                prev_store = acc_store_tok.get((oc, t4))
                war = [x for x in [fev[pp], prev_store] if x is not None]
                if first_flag:
                    ld = E.dma(fbuf[:], xT[oc * 128:(oc + 1) * 128, tsl], after=war)
                    xnv = xns[:, oc * T + t4 * 512: oc * T + (t4 + 1) * 512]
                    a1 = E.dve(lambda e, fbuf=fbuf, xnv=xnv: e.tensor_tensor(fbuf[:], fbuf[:], xnv, AluOpType.add),
                               after=[ld])
                else:
                    ld = E.dma(fbuf[:], accT[oc * 128:(oc + 1) * 128, tsl], after=war)
                    a1 = ld
                a2 = E.dve(lambda e, fbuf=fbuf, pp=pp: e.tensor_tensor(fbuf[:], fbuf[:], pArr[pp][:], AluOpType.add),
                           after=[a1, mm])
                st = E.dma(accT[oc * 128:(oc + 1) * 128, tsl], fbuf[:], after=[a2], sem=f"as{pp}")
                acc_store_tok[(oc, t4)] = st
                wlast[pw] = mm
                pAev[pp] = a2
                fev[pp] = st
                he_last_mm = mm
                it += 1

    gated_ffn(sgT, suT, hshT, comb_e=None)
    down_proj(sdT, hshT, first_flag=True)
    for e_i in range(NEXP):
        gated_ffn(wgTs[e_i], wuTs[e_i], heT, comb_e=e_i)
        down_proj(wdTs[e_i], heT, first_flag=False)


    def emit_attention(QTd, KTd, Vtokd, OTd, hpg, ngr):
        kbufs = [ktile, hbig[:, 0:2048]]
        vbufs = [vtile, hbig[:, 2048:4096]]
        kt_tok = [None, None]
        vt_tok = [None, None]
        klast = [None, None]
        vlast = [None, None]
        qS_war = [None, None]
        prev_store = [None, None]
        prev_exp = None
        pO_ev = [None, None]
        last_copy_tok = None
        last_tr_tok = None
        pend = [None]
        hctr = [0]

        def flush_pend():
            nonlocal pO_ev, last_copy_tok, last_tr_tok
            p = pend[0]
            if p is None:
                return
            nblk = p["qt"] + 1
            first_after = [p["norm"], last_copy_tok]
            ct = None
            t_ = None
            for kb in range(nblk):
                sl4 = (kb % 4) * 128
                a = first_after if kb % 4 == 0 else []
                t_ = E.pe(lambda e, kb=kb, sl4=sl4: e.transpose(
                    pT[:, sl4:sl4 + 128], ptile[:, kb * 128:(kb + 1) * 128], identb[:]), after=a)
                if kb % 4 == 3 or kb == nblk - 1:
                    base = (kb // 4) * 512
                    wdt = (kb % 4 + 1) * 128
                    ct = E.dve(lambda e, base=base, wdt=wdt: e.tensor_copy(
                        pttile[:, base:base + wdt], pT[:, 0:wdt]), after=[t_])
                    first_after = [ct]
            last_copy_tok = ct
            last_tr_tok = t_
            om = None
            gp = p["gp"]
            sp = p["sp"]
            for kb in range(nblk):
                a = [ct, vt_tok[gp], pO_ev[sp]] if kb == 0 else []
                om = E.pe(lambda e, kb=kb, gp=gp, sp=sp: e.matmul(
                    pO[:, sp * 128:(sp + 1) * 128], vbufs[gp][:, kb * 128:(kb + 1) * 128],
                    pttile[:, kb * 128:(kb + 1) * 128],
                    start=(kb == 0), stop=(kb == nblk - 1)), after=a)
            vlast[gp] = om
            ev = E.dve(lambda e, sp=sp: e.tensor_copy(sA[:, sp * 128:(sp + 1) * 128],
                                                      pO[:, sp * 128:(sp + 1) * 128]),
                       after=[om, prev_store[sp]])
            st = E.dma(OTd[p["h"] * 128:(p["h"] + 1) * 128, p["qt"] * 128:(p["qt"] + 1) * 128],
                       sA[:, sp * 128:(sp + 1) * 128], after=[ev], sem=f"os{sp}")
            pO_ev[sp] = ev
            prev_store[sp] = st
            pend[0] = None

        first_load = True
        for qt in range(NT16):
            nk = (qt + 1) * 128
            nk4 = (nk + 511) // 512
            for g in range(ngr):
                gp = (qt * ngr + g) % 2
                kl = E.dma(kbufs[gp][:, 0:nk], KTd[g * 128:(g + 1) * 128, 0:nk],
                           after=None if first_load else [klast[gp]], sem=f"kt{gp}")
                vl = E.dma(vbufs[gp][:, 0:nk].rearrange("p (b d) -> p b d", d=128),
                           Vtokd[0:nk, g * 128:(g + 1) * 128].rearrange("(b p) d -> p b d", p=128),
                           after=None if first_load else [vlast[gp]], sem=f"vt{gp}")
                first_load = False
                kt_tok[gp] = kl
                vt_tok[gp] = vl
                for hh in range(hpg):
                    h = g * hpg + hh
                    qp = hctr[0] % 2
                    ql = E.dma(qtile[:, qp * 128:(qp + 1) * 128],
                               QTd[h * 128:(h + 1) * 128, qt * 128:(qt + 1) * 128],
                               after=[qS_war[qp]] if qS_war[qp] is not None else [kl],
                               sem=f"qs{qp}")
                    sm = None
                    for kb4 in range(nk4):
                        w = min(512, nk - kb4 * 512)
                        a = [ql, kl, prev_exp] if kb4 == 0 else []
                        sm = E.pe(lambda e, kb4=kb4, w=w, qp=qp, gp=gp: e.matmul(
                            pS[:, kb4 * 512: kb4 * 512 + w], qtile[:, qp * 128:(qp + 1) * 128],
                            kbufs[gp][:, kb4 * 512: kb4 * 512 + w],
                            start=True, stop=True), after=a)
                    qS_war[qp] = sm
                    klast[gp] = sm
                    mk = E.dve(lambda e, qt=qt: e.tensor_tensor(pS[:, qt * 128:(qt + 1) * 128],
                                                                pS[:, qt * 128:(qt + 1) * 128],
                                                                maskb[:], AluOpType.add), after=[sm])
                    rd = E.dve(lambda e, nk=nk: e.tensor_reduce(cols[:, 0:1], pS[:, 0:nk], AX.X,
                                                                AluOpType.max), after=[mk])
                    ng = E.dve(lambda e: e.tensor_scalar(cols[:, 1:2], cols[:, 0:1], -1.0, None,
                                                         AluOpType.mult), after=[rd])
                    flush_pend()
                    def _expf(e, nk=nk):
                        e.activation(ptile[:, 0:nk], pS[:, 0:nk], AF.Exp,
                                     bias=cols[:, 1:2], accum_out=cols[:, 2:3])
                        return e.copy(cols[:, 6:7], cols[:, 2:3])
                    ex = E.act(_expf, after=[ng, last_tr_tok])
                    rc = E.dve(lambda e: e.reciprocal(cols[:, 3:4], cols[:, 2:3]), after=[ex])
                    nm = E.dve(lambda e, nk=nk: e.tensor_scalar(ptile[:, 0:nk], ptile[:, 0:nk],
                                                                cols[:, 3:4], None, AluOpType.mult),
                               after=[rc])
                    prev_exp = ex
                    pend[0] = {"qt": qt, "h": h, "gp": gp, "sp": hctr[0] % 2, "norm": nm}
                    hctr[0] += 1
        flush_pend()

    emit_attention(QT, KT, Vtok, OT, 4, KVH0)

    for oc in range(NC):
        E.dmag(bufv(wbufA, NC), wsl(woT, oc, NC))
        for t4 in range(NT4):
            tsl = slice(t4 * 512, (t4 + 1) * 512)
            for c in range(NC):
                E.dma(sA[:, 0:512], OT[c * 128:(c + 1) * 128, tsl])
                E.pe(lambda e, c=c: e.matmul(pA[:], wbufA[:, c * 128:(c + 1) * 128], sA[:, 0:512],
                                             start=(c == 0), stop=(c == NC - 1)))
            E.dma(fA[:], accT[oc * 128:(oc + 1) * 128, tsl])
            bc = biasc[:, 24 + oc: 24 + oc + 1]
            E.dve(lambda e, bc=bc: e.scalar_tensor_tensor(fB[:], pA[:], bc, fA[:], AluOpType.add, AluOpType.add))
            E.dma(x1T[oc * 128:(oc + 1) * 128, tsl], fB[:])


    m1wqT = din("m1wqT", [D, D])
    m1wkT = din("m1wkT", [D, D])
    m1wvT = din("m1wvT", [D, D])
    m1woT = din("m1woT", [D, D])
    m1w1T = din("m1w1T", [D, DFF1])
    m1w2T = din("m1w2T", [D, DFF1])
    m1w3T = din("m1w3T", [DFF1, D])

    xqAtok = dint("xqAtok", [T, D])
    xqFtok = dint("xqFtok", [T, D])
    xqAT = dint("xqAT", [D, T])
    xqFT = dint("xqFT", [D, T])
    wq_q = dint("wq_q", [D, D])
    wk_q = dint("wk_q", [D, D])
    wv_q = dint("wv_q", [D, D])
    wo_q = dint("wo_q", [D, D])
    w1_q = dint("w1_q", [D, DFF1])
    w2_q = dint("w2_q", [D, DFF1])
    w3_q = dint("w3_q", [DFF1, D])
    QT1 = dint("QT1", [D, T])
    KT1 = dint("KT1", [D, T])
    Vtok1 = dint("Vtok1", [T, D])
    OT1 = dint("OT1", [D, T])
    Oqtok = dint("Oqtok", [T, D])
    OqT = dint("OqT", [D, T])
    htok = dint("htok", [T, DFF1])
    hqT = dint("hqT", [DFF1, T])
    accT1 = dint("accTl1", [D, T], F32)
    p11 = pO[0:1, 0:1]

    for t4 in range(NT4):
        tsl = slice(t4 * 512, (t4 + 1) * 512)
        for c in range(NC):
            xv = xns[:, c * T + t4 * 512: c * T + (t4 + 1) * 512]
            E.dmag(xv, x1T[c * 128:(c + 1) * 128, tsl])
        for c in range(NC):
            xv = xns[:, c * T + t4 * 512: c * T + (t4 + 1) * 512]
            E.pe(lambda e, xv=xv, c=c: e.matmul(pR, ONE_COL, xv, start=(c == 0), stop=(c == NC - 1)))
        E.dve(lambda e: e.tensor_scalar(rows[:, 0:512], pR, 1.0 / D, None, AluOpType.mult))
        for c in range(NC):
            xv = xns[:, c * T + t4 * 512: c * T + (t4 + 1) * 512]
            E.act(lambda e, xv=xv: e.activation(sA[:, 0:512], xv, AF.Square))
            E.pe(lambda e, c=c: e.matmul(pR, ONE_COL, sA[:, 0:512], start=(c == 0), stop=(c == NC - 1)))
        mrow = rows[:, 0:512]
        e2row = rows[:, 512:1024]
        msq = rows[:, 1024:1536]
        E.dve(lambda e, e2row=e2row: e.tensor_scalar(e2row, pR, 1.0 / D, None, AluOpType.mult))
        E.dve(lambda e, mrow=mrow, msq=msq: e.tensor_tensor(msq, mrow, mrow, AluOpType.mult))
        E.dve(lambda e, e2row=e2row, msq=msq: e.tensor_tensor(e2row, e2row, msq, AluOpType.subtract))
        E.act(lambda e, e2row=e2row: e.activation(e2row, e2row, AF.Ln, bias=epsc[:, 0:1]))
        E.act(lambda e, e2row=e2row: e.activation(e2row, e2row, AF.Exp, scale=-0.5))
        E.dve(lambda e, mrow=mrow, t4=t4: e.tensor_copy(rowsb[:, t4 * 512:(t4 + 1) * 512], mrow))
        E.dve(lambda e, e2row=e2row, t4=t4: e.tensor_copy(rowsb[:, T + t4 * 512: T + (t4 + 1) * 512], e2row))
    for t4 in range(NT4):
        bcast_row_bf16(rowsb[0:1, t4 * 512:(t4 + 1) * 512], pA, mbs, 512)
        bcast_row_bf16(rowsb[0:1, T + t4 * 512: T + (t4 + 1) * 512], pB, rbs, 512)
        for c in range(NC):
            xv = xns[:, c * T + t4 * 512: c * T + (t4 + 1) * 512]
            E.dve(lambda e, xv=xv: e.tensor_tensor(sA[:, 0:512], xv, mbs[:], AluOpType.subtract))
            E.dve(lambda e, xv=xv: e.tensor_tensor(xv, sA[:, 0:512], rbs[:], AluOpType.mult))

    f3 = lambda t: t.rearrange("p (g i) -> p g i", i=64)
    for t16 in range(NT16):
        rsl = slice(t16 * 128, (t16 + 1) * 128)
        for c in range(NC):
            sl4 = (c % 4) * 128
            E.pe(lambda e, c=c, t16=t16, sl4=sl4: e.transpose(
                pT[:, sl4:sl4 + 128], xns[:, c * T + t16 * 128: c * T + (t16 + 1) * 128], identb[:]))
            if c % 4 == 3:
                base = (c // 4) * 512
                E.dve(lambda e, base=base: e.tensor_copy(sA[:, base:base + 512], pT[:, 0:512]))
        E.dve(lambda e: e.tensor_reduce(cols[:, 3:4], sA[:], AX.X, AluOpType.max, apply_absolute_value=True))
        E.dve(lambda e: e.tensor_scalar(cols[:, 3:4], cols[:, 3:4], 1e-5, None, AluOpType.max))
        E.dve(lambda e: e.reciprocal(cols[:, 4:5], cols[:, 3:4]))
        E.dve(lambda e: e.tensor_scalar(x8, sA[:], cols[:, 4:5], 127.0, AluOpType.mult, AluOpType.mult))
        E.dve(lambda e: e.tensor_scalar(sB[:], x8, cols[:, 3:4], 1.0 / 127.0, AluOpType.mult, AluOpType.mult))
        E.dma(xqAtok[rsl, :], sB[:])
        E.dve(lambda e: e.tensor_reduce(scol[:, 0:32], f3(sA[:]), AX.X, AluOpType.max, apply_absolute_value=True))
        E.dve(lambda e: e.tensor_scalar(scol[:, 0:32], scol[:, 0:32], 1e-5, None, AluOpType.max))
        E.dve(lambda e: e.reciprocal(scol[:, 32:64], scol[:, 0:32]))
        E.dve(lambda e: e.tensor_scalar(scol[:, 32:64], scol[:, 32:64], 127.0, None, AluOpType.mult))
        E.dve(lambda e: e.tensor_tensor(f3(x8), f3(sA[:]),
                                        scol[:, 32:64].unsqueeze(2).broadcast_to([128, 32, 64]), AluOpType.mult))
        E.dve(lambda e: e.tensor_scalar(scol[:, 64:96], scol[:, 0:32], 1.0 / 127.0, None, AluOpType.mult))
        E.dve(lambda e: e.tensor_tensor(f3(sB[:]), f3(x8),
                                        scol[:, 64:96].unsqueeze(2).broadcast_to([128, 32, 64]), AluOpType.mult))
        E.dma(xqFtok[rsl, :], sB[:])

    def transpose_tok_to_fm(src_tok, dst_T):
        for t16 in range(NT16):
            E.dma(sA[:], src_tok[t16 * 128:(t16 + 1) * 128, :])
            for c in range(NC):
                sl4 = (c % 4) * 128
                E.pe(lambda e, c=c, sl4=sl4: e.transpose(pT[:, sl4:sl4 + 128], sA[:, c * 128:(c + 1) * 128], identb[:]))
                if c % 4 == 3:
                    base = (c // 4) * 512
                    E.dve(lambda e, base=base: e.tensor_copy(sB[:, base:base + 512], pT[:, 0:512]))
            for c in range(NC):
                E.dma(dst_T[c * 128:(c + 1) * 128, t16 * 128:(t16 + 1) * 128], sB[:, c * 128:(c + 1) * 128])

    transpose_tok_to_fm(xqAtok, xqAT)
    transpose_tok_to_fm(xqFtok, xqFT)

    def weight_stats2(i, wT_dram, numel, clip_eps):
        rows_n = wT_dram.shape[0]
        wT3 = wT_dram.rearrange("(c p) o -> p c o", p=128)
        nco = wT_dram.shape[1] // 1024
        E.gp(lambda e: e.memset(scol[:, 96:97], 0.0))
        ab_prev = [None, None]
        add_prev = [None, None]
        wit = 0
        for c in range(rows_n // 128):
            for ob in range(nco):
                pp = wit % 2
                fh = fX[:, pp * 1024:(pp + 1) * 1024]
                ch = cols[:, 7:8] if pp == 0 else cols[:, 5:6]
                ld = E.dma(fh, wT3[:, c, ob * 1024:(ob + 1) * 1024],
                           after=[ab_prev[pp]] if ab_prev[pp] is not None else None,
                           sem=f"ws{pp}")
                def _abs(e, fh=fh, ch=ch):
                    e.activation(fh, fh, AF.Abs, accum_out=ch)
                    return e.copy(cols[:, 6:7], ch)
                ab = E.act(_abs, after=[ld, add_prev[pp]])
                ad = E.dve(lambda e, ch=ch: e.tensor_tensor(scol[:, 96:97], scol[:, 96:97], ch,
                                                            AluOpType.add), after=[ab])
                ab_prev[pp] = ab
                add_prev[pp] = ad
                wit += 1
        E.pe(lambda e: e.matmul(p11, scol[:, 96:97], onesf[:], start=True, stop=True))
        E.dve(lambda e, i=i, numel=numel: e.tensor_scalar(statc[0:1, 8 + i:9 + i], p11, 1.0 / numel, None, AluOpType.mult))
        if clip_eps is not None:
            E.dve(lambda e, i=i, clip_eps=clip_eps: e.tensor_scalar(
                statc[0:1, 8 + i:9 + i], statc[0:1, 8 + i:9 + i], clip_eps, None, AluOpType.max))
        E.pe(lambda e, i=i: e.matmul(pO[:, 0:1], onesrowf[0:1, :], statc[0:1, 8 + i:9 + i], start=True, stop=True))
        E.dve(lambda e, i=i: e.tensor_copy(statc[:, i:i + 1], pO[:, 0:1]))
        E.dve(lambda e, i=i: e.reciprocal(statc[:, 8 + i:9 + i], statc[:, i:i + 1]))

    WSTATS = [(m1wqT, D * D), (m1wkT, D * D), (m1wvT, D * D), (m1woT, D * D),
              (m1w1T, D * DFF1), (m1w2T, D * DFF1), (m1w3T, DFF1 * D)]
    for i, (wt, ne) in enumerate(WSTATS):
        weight_stats2(i, wt, ne, 1e-5 if i < 4 else None)

    def quant_weight_ternary(i, wT_dram, dst):
        rows_n = wT_dram.shape[0]
        wT3 = wT_dram.rearrange("(c p) o -> p c o", p=128)
        dst3 = dst.rearrange("(c p) o -> p c o", p=128)
        nco = wT_dram.shape[1] // 1024
        t1_prev = [None, None]
        st_prev = [None, None]
        wit = 0
        first = True
        for c in range(rows_n // 128):
            for ob in range(nco):
                pp = wit % 2
                fh = fX[:, pp * 1024:(pp + 1) * 1024]
                xh = x8.rearrange("p n -> p n")[:, pp * 1024:(pp + 1) * 1024]
                sh = sA[:, pp * 1024:(pp + 1) * 1024]
                ld = E.dma(fh, wT3[:, c, ob * 1024:(ob + 1) * 1024],
                           after=(None if first else [t1_prev[pp]]), sem=f"ws{pp}")
                first = False
                q1 = E.dve(lambda e, i=i, fh=fh: e.tensor_scalar(fh, fh, statc[:, 8 + i:9 + i], 1.0,
                                                                 AluOpType.mult, AluOpType.min), after=[ld])
                q2 = E.dve(lambda e, fh=fh, xh=xh: e.tensor_scalar(xh, fh, -1.0, None, AluOpType.max),
                           after=[q1])
                q3 = E.dve(lambda e, xh=xh, sh=sh: e.tensor_copy(sh, xh),
                           after=[q2, st_prev[pp]])
                st = E.dma(dst3[:, c, ob * 1024:(ob + 1) * 1024], sh, after=[q3], sem=f"qs{pp}")
                t1_prev[pp] = q1
                st_prev[pp] = st
                wit += 1

    def quant_weight_sign(wT_dram, dst):
        rows_n = wT_dram.shape[0]
        wT3 = wT_dram.rearrange("(c p) o -> p c o", p=128)
        dst3 = dst.rearrange("(c p) o -> p c o", p=128)
        nco = wT_dram.shape[1] // 1024
        sg_prev = [None, None]
        st_prev = [None, None]
        wit = 0
        first = True
        for c in range(rows_n // 128):
            for ob in range(nco):
                pp = wit % 2
                fh = fX[:, pp * 1024:(pp + 1) * 1024]
                sh = sA[:, pp * 1024:(pp + 1) * 1024]
                ld = E.dma(fh, wT3[:, c, ob * 1024:(ob + 1) * 1024],
                           after=(None if first else [sg_prev[pp]]), sem=f"ws{pp}")
                first = False
                sg = E.act(lambda e, fh=fh, sh=sh: e.activation(sh, fh, AF.Sign),
                           after=[ld, st_prev[pp]])
                st = E.dma(dst3[:, c, ob * 1024:(ob + 1) * 1024], sh, after=[sg], sem=f"qs{pp}")
                sg_prev[pp] = sg
                st_prev[pp] = st
                wit += 1

    quant_weight_ternary(0, m1wqT, wq_q)
    quant_weight_ternary(1, m1wkT, wk_q)
    quant_weight_ternary(2, m1wvT, wv_q)
    quant_weight_ternary(3, m1woT, wo_q)
    quant_weight_sign(m1w1T, w1_q)
    quant_weight_sign(m1w2T, w2_q)
    quant_weight_sign(m1w3T, w3_q)

    for c in range(NC):
        E.dma(xns[:, c * T:(c + 1) * T], xqAT[c * 128:(c + 1) * 128, :])

    proj(QT1, wq_q, NC, scale_col=statc[:, 0:1], extra_scale=SCL)
    proj(KT1, wk_q, NC, scale_col=statc[:, 1:2], extra_scale=1.0)

    v1_pArr = [pA, pSa]
    v1_ev = [None, None]
    v1_st = [None, None]
    v1_wl = None
    v1_it = 0
    for vb in range(4):
        wt = E.dma(bufv(wbufA, NC, 512), wsl(wv_q, vb, NC, 512),
                   after=None if vb == 0 else [v1_wl], sem="wa0")
        for t16 in range(NT16):
            pp = v1_it % 2
            mm = None
            for c in range(NC):
                lh = xns[:, c * T + t16 * 128: c * T + (t16 + 1) * 128]
                rh = wbufA[:, c * 512:(c + 1) * 512]
                mm = E.pe(lambda e, lh=lh, rh=rh, c=c, pp=pp: e.matmul(
                    v1_pArr[pp][:], lh, rh, start=(c == 0), stop=(c == NC - 1)),
                    after=([wt, v1_ev[pp]] if c == 0 else []))
            ev = E.dve(lambda e, pp=pp: e.tensor_scalar(sA[:, pp * 512:(pp + 1) * 512], v1_pArr[pp][:],
                                                        statc[:, 2:3], None, AluOpType.mult),
                       after=[mm, v1_st[pp]])
            st = E.dma(Vtok1[t16 * 128:(t16 + 1) * 128, vb * 512:(vb + 1) * 512],
                       sA[:, pp * 512:(pp + 1) * 512], after=[ev], sem=f"hs{pp}")
            v1_wl = mm
            v1_ev[pp] = ev
            v1_st[pp] = st
            v1_it += 1

    for c in range(NC):
        E.dma(xns[:, c * T:(c + 1) * T], xqFT[c * 128:(c + 1) * 128, :])

    y_pArr = [pA, pSa]
    y_pBrr = [pB, pSb]
    y_wl = None
    y_pAev = [None, None]
    y_pBev = [None, None]
    y_sst = [None, None]
    y_first = True
    y_it = 0
    for ffb in range(16):
        w1t = E.dma(bufv(wbufA, NC, 512), wsl(w1_q, ffb, NC, 512),
                    after=None if y_first else [y_wl], sem="wa0")
        w2t = E.dma(bufv(wbufB, NC, 512), wsl(w2_q, ffb, NC, 512),
                    after=None if y_first else [y_wl], sem="wb0")
        y_first = False
        for t16 in range(NT16):
            pp = y_it % 2
            mmA = None
            for c in range(NC):
                lh = xns[:, c * T + t16 * 128: c * T + (t16 + 1) * 128]
                mmA = E.pe(lambda e, lh=lh, c=c, pp=pp: e.matmul(
                    y_pArr[pp][:], lh, wbufA[:, c * 512:(c + 1) * 512],
                    start=(c == 0), stop=(c == NC - 1)),
                    after=([w1t, y_pAev[pp]] if c == 0 else []))
            mmB = None
            for c in range(NC):
                lh = xns[:, c * T + t16 * 128: c * T + (t16 + 1) * 128]
                mmB = E.pe(lambda e, lh=lh, c=c, pp=pp: e.matmul(
                    y_pBrr[pp][:], lh, wbufB[:, c * 512:(c + 1) * 512],
                    start=(c == 0), stop=(c == NC - 1)),
                    after=([w2t, y_pBev[pp]] if c == 0 else []))
            sg = E.act(lambda e, pp=pp: e.activation(sA[:, pp * 512:(pp + 1) * 512], y_pArr[pp][:],
                                                     AF.Silu, scale=statc[:, 4:5]), after=[mmA])
            t1_ = E.dve(lambda e, pp=pp: e.tensor_scalar(sB[:, pp * 512:(pp + 1) * 512], y_pBrr[pp][:],
                                                         statc[:, 5:6], None, AluOpType.mult),
                        after=[mmB, y_sst[pp]])
            t2_ = E.dve(lambda e, pp=pp: e.tensor_tensor(sB[:, pp * 512:(pp + 1) * 512],
                                                         sA[:, pp * 512:(pp + 1) * 512],
                                                         sB[:, pp * 512:(pp + 1) * 512], AluOpType.mult),
                        after=[sg, t1_])
            st = E.dma(htok[t16 * 128:(t16 + 1) * 128, ffb * 512:(ffb + 1) * 512],
                       sB[:, pp * 512:(pp + 1) * 512], after=[t2_], sem=f"hs{pp}")
            y_wl = mmB
            y_pAev[pp] = sg
            y_pBev[pp] = t1_
            y_sst[pp] = st
            y_it += 1

    f3h = lambda t: t.rearrange("p (g i) -> p g i", i=64)
    for t16 in range(NT16):
        rsl = slice(t16 * 128, (t16 + 1) * 128)
        E.dma(hbig[:], htok[rsl, :])
        E.dve(lambda e: e.tensor_reduce(scol[:, 0:128], f3h(hbig[:]), AX.X, AluOpType.max, apply_absolute_value=True))
        E.dve(lambda e: e.tensor_scalar(scol[:, 0:128], scol[:, 0:128], 1e-5, None, AluOpType.max))
        E.dve(lambda e: e.reciprocal(scol[:, 128:256], scol[:, 0:128]))
        E.dve(lambda e: e.tensor_scalar(scol[:, 128:256], scol[:, 128:256], 127.0, None, AluOpType.mult))
        E.dve(lambda e: e.tensor_tensor(f3h(h8[:]), f3h(hbig[:]),
                                        scol[:, 128:256].unsqueeze(2).broadcast_to([128, 128, 64]), AluOpType.mult))
        E.dve(lambda e: e.tensor_scalar(scol[:, 128:256], scol[:, 0:128], 1.0 / 127.0, None, AluOpType.mult))
        E.dve(lambda e: e.tensor_tensor(f3h(hbig[:]), f3h(h8[:]),
                                        scol[:, 128:256].unsqueeze(2).broadcast_to([128, 128, 64]), AluOpType.mult))
        for fc in range(NFC):
            sl4 = (fc % 4) * 128
            E.pe(lambda e, fc=fc, sl4=sl4: e.transpose(pT[:, sl4:sl4 + 128], hbig[:, fc * 128:(fc + 1) * 128], identb[:]))
            if fc % 4 == 3:
                base4 = fc - 3
                E.dve(lambda e: e.tensor_copy(sB[:, 0:512], pT[:, 0:512]))
                for j in range(4):
                    E.dma(hqT[(base4 + j) * 128:(base4 + j + 1) * 128, rsl], sB[:, j * 128:(j + 1) * 128])

    y3_pArr = [pA, pSa]
    y3_pAev = [None, None]
    y3_fst = [None, None]
    y3_wl = None
    y3_lastmm = None
    y3_first = True
    y3_it = 0
    for t4 in range(NT4):
        tsl = slice(t4 * 512, (t4 + 1) * 512)
        hq_toks = []
        for fc in range(NFC):
            hq_toks.append(E.dma(xns[:, fc * 512:(fc + 1) * 512], hqT[fc * 128:(fc + 1) * 128, tsl],
                                 after=None if y3_first else [y3_lastmm], sem="he"))
            y3_first = False
        for oc in range(NC):
            pp = y3_it % 2
            wt = E.dma(bufv(wbufW, NFC), wsl(w3_q, oc, NFC),
                       after=[y3_wl] if y3_wl is not None else [hq_toks[0]], sem="wd0")
            mm = None
            for fc in range(NFC):
                mm = E.pe(lambda e, fc=fc, pp=pp: e.matmul(
                    y3_pArr[pp][:], wbufW[:, fc * 128:(fc + 1) * 128],
                    xns[:, fc * 512:(fc + 1) * 512],
                    start=(fc == 0), stop=(fc == NFC - 1)),
                    after=(hq_toks + [wt, y3_pAev[pp]] if fc == 0 else []))
            ldst = fX[:, pp * 512:(pp + 1) * 512]
            odst = fX[:, 1024 + pp * 512: 1024 + (pp + 1) * 512]
            ld = E.dma(ldst, x1T[oc * 128:(oc + 1) * 128, tsl],
                       after=[y3_fst[pp]] if y3_fst[pp] is not None else [])
            ev = E.dve(lambda e, pp=pp, ldst=ldst, odst=odst: e.scalar_tensor_tensor(
                odst, y3_pArr[pp][:], statc[:, 6:7], ldst, AluOpType.mult, AluOpType.add),
                after=[mm, ld])
            st = E.dma(accT1[oc * 128:(oc + 1) * 128, tsl], odst, after=[ev], sem=f"xo{pp}")
            y3_wl = mm
            y3_lastmm = mm
            y3_pAev[pp] = ev
            y3_fst[pp] = st
            y3_it += 1

    emit_attention(QT1, KT1, Vtok1, OT1, 1, NH)

    for t16 in range(NT16):
        rsl = slice(t16 * 128, (t16 + 1) * 128)
        for c in range(NC):
            E.dma(sA[:, c * 128:(c + 1) * 128], OT1[c * 128:(c + 1) * 128, rsl])
        for c in range(NC):
            sl4 = (c % 4) * 128
            E.pe(lambda e, c=c, sl4=sl4: e.transpose(pT[:, sl4:sl4 + 128], sA[:, c * 128:(c + 1) * 128], identb[:]))
            if c % 4 == 3:
                base = (c // 4) * 512
                E.dve(lambda e, base=base: e.tensor_copy(sB[:, base:base + 512], pT[:, 0:512]))
        E.dve(lambda e: e.tensor_reduce(cols[:, 3:4], sB[:], AX.X, AluOpType.max, apply_absolute_value=True))
        E.dve(lambda e: e.tensor_scalar(cols[:, 3:4], cols[:, 3:4], 1e-5, None, AluOpType.max))
        E.dve(lambda e: e.reciprocal(cols[:, 4:5], cols[:, 3:4]))
        E.dve(lambda e: e.tensor_scalar(x8, sB[:], cols[:, 4:5], 127.0, AluOpType.mult, AluOpType.mult))
        E.dve(lambda e: e.tensor_scalar(sB[:], x8, cols[:, 3:4], 1.0 / 127.0, AluOpType.mult, AluOpType.mult))
        E.dma(Oqtok[rsl, :], sB[:])
    transpose_tok_to_fm(Oqtok, OqT)

    for oc in range(NC):
        E.dma(bufv(wbufA, NC), wsl(wo_q, oc, NC))
        for t4 in range(NT4):
            tsl = slice(t4 * 512, (t4 + 1) * 512)
            for c in range(NC):
                E.dma(sA[:, 0:512], OqT[c * 128:(c + 1) * 128, tsl])
                E.pe(lambda e, c=c: e.matmul(pA[:], wbufA[:, c * 128:(c + 1) * 128], sA[:, 0:512],
                                             start=(c == 0), stop=(c == NC - 1)))
            E.dma(fA[:], accT1[oc * 128:(oc + 1) * 128, tsl])
            E.dve(lambda e: e.scalar_tensor_tensor(fB[:], pA[:], statc[:, 3:4], fA[:],
                                                   AluOpType.mult, AluOpType.add))
            E.dma(outT[oc * 128:(oc + 1) * 128, tsl], fB[:])

    E.finalize(nc, block, sempool)
    es.close()
    return nc



def stage_l0(inputs, b):
    f32 = np.float32
    x = np.asarray(inputs["L0_attn_x"][b], dtype=f32)
    mask, onesb, onesf, ident, onesrowf = _consts_host()
    cos2, sin2 = _rope_tables()
    m = {
        "xT": np.ascontiguousarray(x.T),
        "wqT": np.ascontiguousarray(np.asarray(inputs["L0_attn_Wq"], f32).T),
        "wkT": np.ascontiguousarray(np.asarray(inputs["L0_attn_Wk"], f32).T),
        "wvT": np.ascontiguousarray(np.asarray(inputs["L0_attn_Wv"], f32).T),
        "woT": np.ascontiguousarray(np.asarray(inputs["L0_attn_Wo"], f32).T),
        "bcols": np.concatenate([
            _chunk_cols(np.asarray(inputs["L0_attn_bq"], f32)),
            _chunk_cols(np.asarray(inputs["L0_attn_bk"], f32)),
            _chunk_cols(np.asarray(inputs["L0_attn_bv"], f32)),
            _chunk_cols(np.asarray(inputs["L0_attn_bo"], f32))], axis=1),
        "bvrow": np.ascontiguousarray(np.asarray(inputs["L0_attn_bv"], f32)[None, :]),
        "qg": np.ascontiguousarray(np.asarray(inputs["L0_attn_q_gain"], f32)[None, :]),
        "ropec": cos2, "ropes": sin2,
        "maskc": mask, "onesb": onesb, "identb": ident,
        "onesf": onesf, "onesrowf": onesrowf,
        "routerT": np.ascontiguousarray(np.asarray(inputs["L0_ffn_router_w"], f32).T),
        "sgT": np.ascontiguousarray(np.asarray(inputs["L0_ffn_shared_gate"], f32).T),
        "suT": np.ascontiguousarray(np.asarray(inputs["L0_ffn_shared_up"], f32).T),
        "sdT": np.ascontiguousarray(np.asarray(inputs["L0_ffn_shared_down"], f32).T),
    }
    for e in range(NEXP):
        m[f"wgT{e}"] = np.ascontiguousarray(np.asarray(inputs["L0_ffn_w_gate"][e], f32).T)
        m[f"wuT{e}"] = np.ascontiguousarray(np.asarray(inputs["L0_ffn_w_up"][e], f32).T)
        m[f"wdT{e}"] = np.ascontiguousarray(np.asarray(inputs["L0_ffn_w_down"][e], f32).T)
    return m




def _sel8_host():
    m = np.zeros((8, 8 * 128), dtype=bfdt)
    for e in range(8):
        m[e, e * 128:(e + 1) * 128] = 1.0
    return m


def stage_fused(inputs, b):
    m = stage_l0(inputs, b)
    f32 = np.float32
    m["sel8c"] = _sel8_host()
    m["m1wqT"] = np.ascontiguousarray(np.asarray(inputs["L1_attn_w_q"], f32).T)
    m["m1wkT"] = np.ascontiguousarray(np.asarray(inputs["L1_attn_w_k"], f32).T)
    m["m1wvT"] = np.ascontiguousarray(np.asarray(inputs["L1_attn_w_v"], f32).T)
    m["m1woT"] = np.ascontiguousarray(np.asarray(inputs["L1_attn_w_o"], f32).T)
    m["m1w1T"] = np.ascontiguousarray(np.asarray(inputs["L1_ffn_w1"], f32).T)
    m["m1w2T"] = np.ascontiguousarray(np.asarray(inputs["L1_ffn_w2"], f32).T)
    m["m1w3T"] = np.ascontiguousarray(np.asarray(inputs["L1_ffn_w3"], f32).T)
    return m


def run_fused(inputs, cores=8, dev=False):
    nc = build_fused(dev=dev)
    in_maps = [stage_fused(inputs, b) for b in range(cores)]
    res = run_bass_kernel_spmd(nc, in_maps, core_ids=list(range(cores)))
    return res.results


def kernel(**inputs):
    n = 8
    res = run_fused(inputs, cores=n, dev=False)
    out = np.stack([np.asarray(res[b]["outT"], dtype=np.float32).T for b in range(n)], axis=0)
    return out
